# version 02
# baseline (speedup 1.0000x reference)
import numpy as np
from contextlib import ExitStack

import concourse.bass as bass
import concourse.mybir as mybir
from concourse.bass_utils import run_bass_kernel_spmd

F32 = mybir.dt.float32
BF16 = mybir.dt.bfloat16
I32 = mybir.dt.int32
ALU = mybir.AluOpType
AF = mybir.ActivationFunctionType
AX = mybir.AxisListType

D = 2048
T = 1024
NQH = 2
NH = 16
H0 = 5632
H1 = 8192
NC = D // 128
RS128 = 0.08838834764831845

ENGINES = ("sync", "gpsimd", "vector", "scalar", "tensor")

DEBUG_TAPS = ()
ONLY_L0 = False


class Sched:
    def __init__(self, nc, stk):
        self.nc = nc
        self.stk = stk
        self.ops = {e: [] for e in ENGINES}
        self.sems = {}
        self.cnt = {}
        self.guards = {}

    def sem(self, name):
        if name not in self.sems:
            self.sems[name] = self.stk.enter_context(self.nc.semaphore(name))
            self.cnt[name] = 0
        return self.sems[name]

    def on(self, eng, fn):
        self.ops[eng].append(fn)

    def wait(self, eng, sem, cnt):
        if cnt is None or cnt <= 0:
            return
        assert cnt <= self.cnt.get(sem, 0), f"wait beyond emitted: {sem} {cnt}"
        h = self.sem(sem)
        self.on(eng, lambda e, h=h, c=cnt: e.wait_ge(h, c))

    def bump(self, sem, n):
        self.sem(sem)
        self.cnt[sem] += n
        return self.cnt[sem]

    def dma(self, eng, out, in_, sem):
        h = self.sem(sem)
        c = self.bump(sem, 16)
        self.on(eng, lambda e, o=out, i=in_, h=h:
                e.dma_start(out=o, in_=i).then_inc(h, 16))
        return c

    def op(self, eng, sem, fn):
        h = self.sem(sem)
        c = self.bump(sem, 1)
        self.on(eng, lambda e, fn=fn, h=h: fn(e).then_inc(h, 1))
        return c

    def opq(self, eng, fn):
        self.on(eng, fn)

    def fence(self, eng, sems=("mv", "mt", "ma")):
        for s in sems:
            self.wait(eng, s, self.cnt.get(s, 0))

    def guard(self, tag):
        g = self.guards.get(tag)
        if g:
            self.wait("tensor", g[0], g[1])
            self.guards[tag] = None

    def setg(self, tag, sem, cnt):
        self.guards[tag] = (sem, cnt)

    def mm(self, tag, items, sem="mt"):
        self.guard(tag)
        h = self.sem(sem)
        c = self.bump(sem, 1)
        items = list(items)

        def fn(e, items=items, h=h):
            n = len(items)
            ins = None
            for i, (ps, lh, rh) in enumerate(items):
                ins = e.matmul(ps, lh, rh, start=(i == 0), stop=(i == n - 1),
                               skip_group_check=True)
            return ins.then_inc(h, 1)
        self.on("tensor", fn)
        return c


def build_program(taps=(), only_l0=False):
    nc = bass.Bass("TRN2", target_bir_lowering=False,
                   detect_race_conditions=False)

    def din(name, shape, dtype=F32):
        return nc.dram_tensor(name, shape, dtype, kind="ExternalInput")

    xfm = din("xfm", [D, T])
    wq0 = din("wq0", [D, D]); wk0 = din("wk0", [D, D])
    wv0 = din("wv0", [D, D]); wo0 = din("wo0", [D, D])
    bq0 = din("bq0", [128, NC]); bk0 = din("bk0", [128, NC])
    bv0 = din("bv0", [128, NC]); bo0 = din("bo0", [128, NC])
    w1t0 = din("w1t0", [D, H0]); w2t0 = din("w2t0", [D, H0])
    w3t0 = din("w3t0", [H0, D])
    wq1 = din("wq1", [D, D]); wk1 = din("wk1", [D, D])
    wv1 = din("wv1", [D, D]); wo1 = din("wo1", [D, D])
    w1t1 = din("w1t1", [D, H1]); w2t1 = din("w2t1", [H1, D])
    b11 = din("b11", [128, 64]); lng1 = din("lng1", [128, 64])
    lnb1 = din("lnb1", [128, 64]); b21 = din("b21", [128, NC])
    qiota = din("qiota", [1, T])
    kvec = din("kvec", [128, NH])

    y = nc.dram_tensor("y", [D, T], F32, kind="ExternalOutput")

    x1d = nc.dram_tensor("x1d", [D, T], BF16)
    qd = nc.dram_tensor("qd", [D, T], BF16)
    k_own = nc.dram_tensor("k_own", [D, T], BF16)
    v_own = nc.dram_tensor("v_own", [D, T], BF16)
    k_full = nc.dram_tensor("k_full", [2 * D, T], BF16)
    v_full = nc.dram_tensor("v_full", [2 * D, T], BF16)
    hid0 = nc.dram_tensor("hid0", [H0, T], BF16)
    hid1 = nc.dram_tensor("hid1", [H1, T], BF16)
    s1loc = nc.dram_tensor("s1loc", [D, D], F32)
    s1sum = nc.dram_tensor("s1sum", [D, D], F32)
    vd1 = nc.dram_tensor("vd1", [D, T], BF16)
    wdbg = nc.dram_tensor("wdbg", [D, D], BF16)
    rdbg = nc.dram_tensor("rdbg", [1, 4224], F32)
    hdbg = nc.dram_tensor("hdbg", [H1, T], BF16)
    g2acc = nc.dram_tensor("g2acc", [D, T], F32)
    xdbg = nc.dram_tensor("xdbg", [128, 8192], BF16)
    g2s01 = nc.dram_tensor("g2s01", [4096, 1024], F32)
    pdbg = nc.dram_tensor("pdbg", [128, 2048], BF16)
    qdbg = nc.dram_tensor("qdbg", [128, 16384], BF16)

    tshape = {"x1d": ([D, T], BF16), "qd": ([D, T], BF16),
              "k_full": ([2 * D, T], BF16), "v_full": ([2 * D, T], BF16),
              "s1loc": ([D, D], F32), "s1sum": ([D, D], F32),
              "hid0": ([H0, T], BF16), "hid1": ([H1, T], BF16),
              "vd1": ([D, T], BF16), "xn_t": ([D, T], BF16),
              "g_t": ([D, T], BF16), "att_t": ([D, T], BF16),
              "cols_t": ([128, 32], F32), "rows_t": ([1, 4224], F32),
              "wdbg": ([D, D], BF16), "rdbg": ([1, 4224], F32),
              "hdbg": ([H1, T], BF16), "xdbg": ([128, 8192], BF16),
              "g2d": ([D, T], F32), "g2s01": ([4096, 1024], F32),
              "pdbg": ([128, 2048], BF16),
              "qdbg": ([128, 16384], BF16)}
    tap_t = {tp: nc.dram_tensor("tap_" + tp, *tshape[tp], kind="ExternalOutput")
             for tp in taps}

    groups = [[0, 1], [2, 3], [4, 5], [6, 7]]

    with ExitStack() as stk:
        e_ = stk.enter_context
        S = Sched(nc, stk)

        xn = e_(nc.sbuf_tensor("xn", [128, NC * T], BF16))
        g = e_(nc.sbuf_tensor("g", [128, NC * T], BF16))
        att = e_(nc.sbuf_tensor("att", [128, NC * T], BF16))
        pool = e_(nc.sbuf_tensor("pool", [128, 24576], BF16))
        fbuf = [e_(nc.sbuf_tensor(f"fbuf{i}", [128, 2048], F32)) for i in range(2)]
        Bq = e_(nc.sbuf_tensor("Bq", [128, T], F32))
        stg = e_(nc.sbuf_tensor("stg", [128, 4096], BF16))
        stgi = e_(nc.sbuf_tensor("stgi", [128, 512], I32))
        rows = e_(nc.sbuf_tensor("rows", [1, 4224], F32))
        biases = e_(nc.sbuf_tensor("biases", [128, 5 * NC], F32))
        bias1 = e_(nc.sbuf_tensor("bias1", [128, 192], F32))
        qstage = e_(nc.sbuf_tensor("qstage", [128, T], BF16))
        x0t = e_(nc.sbuf_tensor("x0t", [128, 512], F32))
        x1t = e_(nc.sbuf_tensor("x1t", [128, 512], BF16))
        yst = e_(nc.sbuf_tensor("yst", [128, 512], F32))
        ident = e_(nc.sbuf_tensor("ident", [128, 128], BF16))
        ones_b = e_(nc.sbuf_tensor("ones_b", [128, 1], BF16))
        ones_f = e_(nc.sbuf_tensor("ones_f", [128, 1], F32))
        onesr_f = e_(nc.sbuf_tensor("onesr_f", [1, 128], F32))
        kv = e_(nc.sbuf_tensor("kv", [128, NH], F32))
        col = e_(nc.sbuf_tensor("col", [128, 32], F32))
        mx2 = e_(nc.sbuf_tensor("mx2", [128, 8], BF16))
        denb = e_(nc.sbuf_tensor("denb", [128, 1024], F32))
        colacc = e_(nc.sbuf_tensor("colacc", [128, 64], F32))
        gstg0 = e_(nc.sbuf_tensor("gstg0", [128, 512], F32))
        mxacc = e_(nc.sbuf_tensor("mxacc", [128, 64], F32))

        R0 = rows[0:1, 0:1024]
        R1 = rows[0:1, 1024:2048]
        R2 = rows[0:1, 2048:3072]
        R3 = rows[0:1, 3072:4096]
        RS = rows[0:1, 4096:4224]

        PA = [e_(nc.psum_tensor(f"PA{i}", [128, 512], F32)) for i in range(2)]
        PV = [e_(nc.psum_tensor(f"PV{i}", [128, 512], F32)) for i in range(2)]
        PS = [e_(nc.psum_tensor(f"PS{i}", [1, 512], F32)) for i in range(2)]
        PT = [e_(nc.psum_tensor(f"PT{i}", [128, 1024], BF16)) for i in range(2)]

        def bcast(dst_col, src_row, gtag="bc0", psum=None):
            n = src_row.shape[1]
            ps = PV[0] if psum is None else psum
            S.wait("tensor", "mv", S.cnt["mv"])
            S.guard(gtag)
            c = S.mm(gtag, [(ps[:, 0:n], onesr_f[0:1, :], src_row)])
            S.wait("vector", "mt", c)
            cv = S.op("vector", "mv", lambda e, d=dst_col, n=n, ps=ps:
                      e.tensor_copy(d, ps[:, 0:n]))
            S.setg(gtag, "mv", cv)
            return cv

        for dst, srcd in ((Bq[0:1, :], qiota[:, :]), (kv[:, :], kvec[:, :]),
                          (biases[:, 0:NC], bq0[:, :]),
                          (biases[:, NC:2 * NC], bk0[:, :]),
                          (biases[:, 2 * NC:3 * NC], bv0[:, :]),
                          (biases[:, 3 * NC:4 * NC], bo0[:, :]),
                          (biases[:, 4 * NC:5 * NC], b21[:, :]),
                          (bias1[:, 0:64], b11[:, :]),
                          (bias1[:, 64:128], lng1[:, :]),
                          (bias1[:, 128:192], lnb1[:, :])):
            c_ini = S.dma("sync", dst, srcd, "sini")
            S.wait("sync", "sini", c_ini)
        eps5 = col[:, 7:8]
        S.op("gpsimd", "mg", lambda e: e.memset(rows[:, :], 0.0))
        S.op("gpsimd", "mg", lambda e: e.memset(col[:, :], 0.0))
        S.op("gpsimd", "mg", lambda e: e.memset(ones_b[:, :], 1.0))
        S.op("gpsimd", "mg", lambda e: e.memset(ones_f[:, :], 1.0))
        S.op("gpsimd", "mg", lambda e: e.memset(onesr_f[:, :], 1.0))
        S.wait("gpsimd", "mg", S.cnt["mg"])
        S.op("gpsimd", "mg", lambda e: e.memset(eps5, 1e-5))
        c_mident = S.op("gpsimd", "mg", lambda e: e.memset(ident[:, :], 0.0))
        S.wait("gpsimd", "mg", c_mident)
        cg_ident = S.op("gpsimd", "mg", lambda e: e.affine_select(
            out=ident[:, :], in_=ident[:, :], compare_op=ALU.not_equal,
            fill=1.0, base=0, pattern=[[-1, 128]], channel_multiplier=1))
        S.wait("tensor", "mg", cg_ident)
        S.wait("vector", "mg", cg_ident)
        S.wait("scalar", "mg", cg_ident)
        S.wait("tensor", "sini", S.cnt["sini"])
        for qh in range(NQH):
            c = S.mm(f"PA{qh}", [(PA[qh][:, :], onesr_f[0:1, :],
                                  Bq[0:1, qh * 512:(qh + 1) * 512])])
            S.wait("vector", "mt", c)
            cv = S.op("vector", "mv", lambda e, qh=qh: e.tensor_copy(
                Bq[:, qh * 512:(qh + 1) * 512], PA[qh][:, :]))
            S.setg(f"PA{qh}", "mv", cv)

        def emit_layernorm(src_of_chunk, src_dtype):
            f32src = src_dtype == F32
            onesv = ones_f if f32src else ones_b
            if f32src:
                xt = [fbuf[0][:, 0:1024], fbuf[0][:, 1024:2048]]
                sqt = fbuf[1][:, 0:1024]
            else:
                xt = [pool[:, 0:1024], pool[:, 1024:2048]]
                sqt = pool[:, 2048:3072]
            for tg in ("PS0", "PS1", "PV0", "PV1"):
                S.guard(tg)
            S.fence("sync")
            last_mm = 0
            prev_mm = [0, 0]
            fence_v = S.cnt["mv"]
            for c in range(NC):
                b = c % 2
                sem = "sxa" if b == 0 else "sxb"
                S.wait("sync", "mt", prev_mm[b])
                if c == 0:
                    S.wait("sync", "mv", fence_v)
                cl = S.dma("sync", xt[b], src_of_chunk(c), sem)
                S.wait("vector", sem, cl)
                S.wait("vector", "mt", last_mm)
                cv = S.op("vector", "mv", lambda e, t=xt[b], s=sqt:
                          e.tensor_tensor(s, t, t, ALU.mult))
                S.wait("tensor", sem, cl)
                S.wait("tensor", "mv", cv)

                def mmfn(e, t=xt[b], s=sqt, first=(c == 0), stop=(c == NC - 1)):
                    ins = None
                    for qh in range(NQH):
                        sl = slice(qh * 512, (qh + 1) * 512)
                        e.matmul(PS[qh][:, :], onesv[:, 0:1], t[:, sl],
                                 start=first, stop=stop, skip_group_check=True)
                        ins = e.matmul(PV[qh][0:1, :], onesv[:, 0:1], s[:, sl],
                                       start=first, stop=stop,
                                       skip_group_check=True)
                    return ins
                last_mm = S.op("tensor", "mt", mmfn)
                prev_mm[b] = last_mm
            S.wait("vector", "mt", last_mm)

            def fin1(e):
                e.tensor_scalar(R0[:, 0:512], PS[0][:, :], 1.0 / D, None, ALU.mult)
                e.tensor_scalar(R0[:, 512:1024], PS[1][:, :], 1.0 / D, None, ALU.mult)
                e.tensor_scalar(R1[:, 0:512], PV[0][0:1, :], 1.0 / D, None, ALU.mult)
                return e.tensor_scalar(R1[:, 512:1024], PV[1][0:1, :], 1.0 / D,
                                       None, ALU.mult)
            cv = S.op("vector", "mv", fin1)
            for tg in ("PS0", "PS1", "PV0", "PV1"):
                S.setg(tg, "mv", cv)
            S.wait("vector", "mv", cv)
            cv = S.op("vector", "mv", lambda e: e.tensor_tensor(
                R2[:, :], R0[:, :], R0[:, :], ALU.mult))
            S.wait("vector", "mv", cv)
            cv = S.op("vector", "mv", lambda e: e.tensor_tensor(
                R1[:, :], R1[:, :], R2[:, :], ALU.subtract))
            S.wait("scalar", "mv", cv)
            S.wait("scalar", "mg", cg_ident)
            ca = S.op("scalar", "ma", lambda e: e.activation(
                R1[:, :], R1[:, :], AF.Sqrt, bias=eps5[0:1, :], scale=1.0))
            S.wait("vector", "ma", ca)
            cv = S.op("vector", "mv", lambda e: e.reciprocal(R1[:, :], R1[:, :]))
            prev = [0, 0]
            for qh in range(NQH):
                sl = slice(qh * 512, (qh + 1) * 512)
                S.wait("tensor", "mv", cv)
                c1 = S.mm("PV0", [(PV[0][:, :], onesr_f[0:1, :], R0[:, sl])])
                c2 = S.mm("PV1", [(PV[1][:, :], onesr_f[0:1, :], R1[:, sl])])
                for c in range(NC):
                    b = c % 2
                    sem = "sxa" if b == 0 else "sxb"
                    S.wait("sync", "mv", prev[b])
                    if c < 2 and qh == 0:
                        S.wait("sync", "mt", last_mm)
                    cl = S.dma("sync", xt[b], src_of_chunk(c), sem)
                    S.wait("vector", sem, cl)
                    S.wait("vector", "mt", c2)

                    def ap(e, t=xt[b], c=c, sl=sl):
                        e.tensor_tensor(t[:, sl], t[:, sl], PV[0][:, :],
                                        ALU.subtract)
                        return e.tensor_tensor(
                            xn[:, c * T + sl.start:c * T + sl.stop],
                            t[:, sl], PV[1][:, :], ALU.mult)
                    prev[b] = S.op("vector", "mv", ap)
                pc = max(prev)
                S.setg("PV0", "mv", pc)
                S.setg("PV1", "mv", pc)
            return S.cnt["mv"]

        wqa = pool[:, 0:8192]
        wqb = pool[:, 8192:16384]
        wfree = {0: None, 1: None}

        def emit_proj(wdram, out_writer, xn_ready, rhs_buf=None):
            loaded = {}
            rb = xn if rhs_buf is None else rhs_buf
            S.fence("gpsimd", ("mv", "mt", "ma"))

            def load_quarter(q):
                b = q % 2
                if wfree[b] is not None:
                    S.wait("gpsimd", wfree[b][0], wfree[b][1])
                    wfree[b] = None
                sem = "swa" if b == 0 else "swb"
                last = 0
                for c in range(NC):
                    last = S.dma("gpsimd", (wqa if b == 0 else wqb)[:, c * 512:(c + 1) * 512],
                                 wdram[c * 128:(c + 1) * 128, q * 512:(q + 1) * 512],
                                 sem)
                loaded[q] = (sem, last)

            def compute_quarter(q):
                b = q % 2
                wst = wqa if b == 0 else wqb
                S.wait("tensor", loaded[q][0], loaded[q][1])
                S.wait("tensor", "mv", xn_ready)
                for otl in range(4):
                    ot = q * 4 + otl
                    for qh in range(NQH):
                        ptag = f"PA{(ot * NQH + qh) % 2}"
                        ps = PA[(ot * NQH + qh) % 2]
                        items = []
                        for c in range(NC):
                            lh = wst[:, c * 512 + otl * 128:c * 512 + otl * 128 + 128]
                            rh = rb[:, c * T + qh * 512:c * T + qh * 512 + 512]
                            items.append((ps[:, :], lh, rh))
                        cm = S.mm(ptag, items)
                        sem, cnt = out_writer(ot, qh, ps, cm)
                        S.setg(ptag, sem, cnt)
                wfree[b] = ("mt", S.cnt["mt"])

            load_quarter(0)
            for q in range(1, 4):
                load_quarter(q)
                compute_quarter(q - 1)
            compute_quarter(3)

        qkv_state = {"dma": 0}

        def mk_qkv_writer(dst, bidx, maxcol):
            state = qkv_state

            def wr(ot, qh, ps, cm):
                S.wait("vector", "mt", cm)
                S.wait("vector", "shd", state["dma"])

                def ev(e, ot=ot, qh=qh, ps=ps):
                    return e.tensor_scalar_add(
                        qstage[:, qh * 512:(qh + 1) * 512], ps[:, :],
                        biases[:, bidx * NC + ot:bidx * NC + ot + 1])
                cv = S.op("vector", "mv", ev)
                if maxcol is not None:
                    idx = maxcol * 32 + ot * 2 + qh
                    cv = S.op("vector", "mv", lambda e, qh=qh, idx=idx:
                              e.reduce_max(mxacc[:, idx:idx + 1],
                                           qstage[:, qh * 512:(qh + 1) * 512],
                                           axis=AX.X, apply_absolute_value=True))
                if qh == NQH - 1:
                    S.wait("scalar", "mv", cv)
                    state["dma"] = S.dma(
                        "scalar", dst[ot * 128:(ot + 1) * 128, :], qstage[:, :],
                        "shd")
                return ("mv", cv)
            return wr

        def emit_wstats(wdram, nrows, rowlen, acc):
            S.fence("sync")
            prev = [0, 0]
            k = 0
            segs = []
            o = 0
            while o < rowlen:
                w = min(2048, rowlen - o)
                segs.append((o, w))
                o += w
            nacc = len(segs) * (nrows // 128)
            assert nacc <= 64
            for c in range(nrows // 128):
                for (o, w) in segs:
                    b = k % 2
                    S.wait("sync", "mv", prev[b])
                    sem = "sqa" if b == 0 else "sqb"
                    cl = S.dma("sync", fbuf[b][:, 0:w],
                               wdram[c * 128:(c + 1) * 128, o:o + w], sem)
                    S.wait("vector", sem, cl)
                    prev[b] = S.op("vector", "mv",
                                   lambda e, b=b, w=w, k=k: e.reduce_sum(
                                       colacc[:, k:k + 1], fbuf[b][:, 0:w],
                                       axis=AX.X, apply_absolute_value=True))
                    k += 1
            cv = S.cnt["mv"]
            S.wait("vector", "mv", cv)
            cv = S.op("vector", "mv", lambda e, n=nacc: e.reduce_sum(
                acc, colacc[:, 0:n], axis=AX.X))
            S.wait("vector", "mv", cv)

        def finalize_gamma(acc, nelem, tcol, ncol, grow, mode):
            S.wait("tensor", "mv", S.cnt["mv"])
            cm = S.mm("PS0", [(PS[0][0:1, 0:1], ones_f[:, 0:1], acc)])
            S.wait("vector", "mt", cm)
            op1 = ALU.add if mode == "l0" else ALU.max
            c2 = 1e-8 if mode == "l0" else 1e-5
            cv = S.op("vector", "mv", lambda e: e.tensor_scalar(
                grow[:, 1:2], PS[0][0:1, 0:1], 1.0 / nelem, c2, ALU.mult, op1))
            S.setg("PS0", "mv", cv)
            S.wait("vector", "mv", cv)

            def fin2(e):
                e.tensor_scalar_mul(grow[:, 2:3], grow[:, 1:2], 0.5)
                return e.tensor_scalar_mul(grow[:, 3:4], grow[:, 1:2], -0.5)
            cv = S.op("vector", "mv", fin2)
            S.wait("vector", "mv", cv)
            bcast(tcol, grow[:, 2:3], "bc0")
            bcast(ncol, grow[:, 3:4], "bc0")
            return S.cnt["mv"]

        def ternary(e, dst, src, w, tcol, ncol):
            t1 = stg[:, 0:2048]
            t2 = stg[:, 2048:4096]
            e.tensor_single_scalar(t1[:, 0:w], src, tcol, ALU.is_gt)
            e.tensor_single_scalar(t2[:, 0:w], src, ncol, ALU.is_lt)
            return e.tensor_tensor(dst, t1[:, 0:w], t2[:, 0:w], ALU.subtract)

        def src_x0(c):
            return xfm[c * 128:(c + 1) * 128, :]

        xn_mv = emit_layernorm(src_x0, F32)

        emit_proj(wq0, mk_qkv_writer(qd, 0, 0), xn_mv)
        emit_proj(wk0, mk_qkv_writer(k_own, 1, 1), xn_mv)
        emit_proj(wv0, mk_qkv_writer(v_own, 2, None), xn_mv)
        qkv_shd = S.cnt["shd"]

        S.wait("gpsimd", "shd", qkv_shd)
        S.op("gpsimd", "scc", lambda e: e.collective_compute(
            "AllGather", ALU.bypass, replica_groups=groups,
            ins=[k_own.ap().opt()], outs=[k_full.ap().opt()]))
        cc_ag = S.op("gpsimd", "scc", lambda e: e.collective_compute(
            "AllGather", ALU.bypass, replica_groups=groups,
            ins=[v_own.ap().opt()], outs=[v_full.ap().opt()]))

        S.wait("vector", "mv", S.cnt["mv"])
        cv = S.op("vector", "mv", lambda e: e.reduce_max(
            mx2[:, 0:1], mxacc[:, 0:32], axis=AX.X))
        cv = S.op("vector", "mv", lambda e: e.reduce_max(
            mx2[:, 1:2], mxacc[:, 32:64], axis=AX.X))
        S.wait("tensor", "mv", cv)
        S.wait("tensor", "mg", cg_ident)
        for i in range(2):
            S.guard(f"PT{i}")
            ctr = S.op("tensor", "mt", lambda e, i=i: e.transpose(
                PT[i][0:1, 0:128], mx2[:, i:i + 1], ident[:, :]))
            S.wait("vector", "mt", ctr)
            cvi = S.op("vector", "mv", lambda e, i=i: e.tensor_copy(
                stg[0:1, i * 128:(i + 1) * 128], PT[i][0:1, 0:128]))
            S.setg(f"PT{i}", "mv", cvi)
        S.wait("vector", "mv", S.cnt["mv"])
        cv = S.op("vector", "mv", lambda e: e.reduce_max(
            RS[:, 0:1], stg[0:1, 0:128], axis=AX.X))
        cv = S.op("vector", "mv", lambda e: e.reduce_max(
            RS[:, 1:2], stg[0:1, 128:256], axis=AX.X))
        S.wait("vector", "mv", cv)
        cv = S.op("vector", "mv", lambda e: e.tensor_tensor(
            RS[:, 2:3], RS[:, 0:1], RS[:, 1:2], ALU.mult))
        S.wait("vector", "mv", cv)
        cv = S.op("vector", "mv", lambda e: e.tensor_scalar_mul(
            RS[:, 2:3], RS[:, 2:3], -128.0 * RS128))
        S.wait("vector", "mv", cv)
        ebias = col[:, 2:3]
        bcast(ebias, RS[:, 2:3], "bc0")
        ebias_mv = S.cnt["mv"]

        thr = {}
        for i, (wd, nr, rl) in enumerate(((w1t0, D, H0), (w2t0, D, H0),
                                          (w3t0, H0, D))):
            acc = col[:, 3 + i:4 + i]
            emit_wstats(wd, nr, rl, acc)
            tcol = col[:, 8 + 2 * i:9 + 2 * i]
            ncol = col[:, 9 + 2 * i:10 + 2 * i]
            finalize_gamma(acc, nr * rl, tcol, ncol,
                           RS[:, 8 + 4 * i:12 + 4 * i], "l0")
            thr[i] = (tcol, ncol, S.cnt["mv"])

        wcolq = pool[:, 16384:24576]
        h1grp = pool[:, 0:4096]
        NG0 = H0 // 512
        hid_dma = 0
        wcol_free = 0
        for gi in range(NG0):
            for wi, (wd, (tcol, ncol, tmv)) in enumerate(
                    ((w1t0, thr[0]), (w2t0, thr[1]))):
                S.fence("sync", ("mv",))
                prev = [0, 0]
                for c in range(NC):
                    b = c % 2
                    S.wait("sync", "mv", prev[b])
                    sem = "sqa" if b == 0 else "sqb"
                    cl = S.dma("sync", fbuf[b][:, 0:512],
                               wd[c * 128:(c + 1) * 128,
                                  gi * 512:(gi + 1) * 512], sem)
                    S.wait("vector", sem, cl)
                    S.wait("vector", "mv", tmv)
                    S.wait("vector", "mt", wcol_free)

                    def qf(e, b=b, c=c, tcol=tcol, ncol=ncol):
                        return ternary(e, wcolq[:, c * 512:(c + 1) * 512],
                                       fbuf[b][:, 0:512], 512, tcol, ncol)
                    prev[b] = S.op("vector", "mv", qf)
                q_mv = S.cnt["mv"]
                S.wait("tensor", "mv", q_mv)
                for htl in range(4):
                    for qh in range(NQH):
                        ptag = f"PA{(htl * NQH + qh) % 2}"
                        ps = PA[(htl * NQH + qh) % 2]
                        items = []
                        for c in range(NC):
                            lh = wcolq[:, c * 512 + htl * 128:c * 512 + htl * 128 + 128]
                            rh = xn[:, c * T + qh * 512:c * T + qh * 512 + 512]
                            items.append((ps[:, :], lh, rh))
                        cm = S.mm(ptag, items)
                        sl = slice(htl * T + qh * 512, htl * T + qh * 512 + 512)
                        if wi == 0:
                            S.wait("scalar", "mt", cm)
                            S.wait("scalar", "shd", hid_dma)
                            ca = S.op("scalar", "ma", lambda e, sl=sl, ps=ps:
                                      e.activation(h1grp[:, sl], ps[:, :],
                                                   AF.Sigmoid))
                            S.setg(ptag, "ma", ca)
                        else:
                            S.wait("vector", "mt", cm)
                            S.wait("vector", "ma", S.cnt["ma"])
                            cv = S.op("vector", "mv", lambda e, sl=sl, ps=ps:
                                      e.tensor_tensor(h1grp[:, sl], h1grp[:, sl],
                                                      ps[:, :], ALU.mult))
                            S.setg(ptag, "mv", cv)
                            ht_g = gi * 4 + htl
                            S.wait("scalar", "mv", cv)
                            hid_dma = S.dma(
                                "scalar",
                                hid0[ht_g * 128:(ht_g + 1) * 128,
                                     qh * 512:(qh + 1) * 512],
                                h1grp[:, sl], "shd")
                wcol_free = S.cnt["mt"]

        qw3s = att[:, 0:16384]
        hstr = pool[:, 0:8192]
        sup_sizes = [8, 8, 8, 8, 8, 4]
        hc0 = 0
        hstr_free = 0
        qw3_free = 0
        for si, ssz in enumerate(sup_sizes):
            S.fence("sync", ("mv",))
            hload = 0
            prevq = [0, 0]
            for j in range(ssz):
                hc = hc0 + j
                S.wait("scalar", "shd", hid_dma if si == 0 and j == 0 else 0)
                S.wait("scalar", "mt", hstr_free if j == 0 else 0)
                hload = S.dma("scalar", hstr[:, j * 1024:(j + 1) * 1024],
                              hid0[hc * 128:(hc + 1) * 128, :], "shd")
                b = j % 2
                S.wait("sync", "mv", prevq[b])
                sem = "sqa" if b == 0 else "sqb"
                cl = S.dma("sync", fbuf[b][:, 0:2048],
                           w3t0[hc * 128:(hc + 1) * 128, :], sem)
                S.wait("vector", sem, cl)
                S.wait("vector", "mv", thr[2][2] if si == 0 and j == 0 else 0)
                S.wait("vector", "mt", qw3_free if j == 0 else 0)

                def qf3(e, b=b, j=j):
                    return ternary(e, qw3s[:, j * 2048:(j + 1) * 2048],
                                   fbuf[b][:, 0:2048], 2048, thr[2][0], thr[2][1])
                prevq[b] = S.op("vector", "mv", qf3)
            q_mv = S.cnt["mv"]
            S.wait("tensor", "mv", q_mv)
            S.wait("tensor", "shd", hload)
            for dt_ in range(NC):
                for qh in range(NQH):
                    ptag = f"PV{(dt_ * NQH + qh) % 2}"
                    ps = PV[(dt_ * NQH + qh) % 2]
                    items = []
                    for j in range(ssz):
                        lh = qw3s[:, j * 2048 + dt_ * 128:j * 2048 + dt_ * 128 + 128]
                        rh = hstr[:, j * 1024 + qh * 512:j * 1024 + qh * 512 + 512]
                        items.append((ps[:, :], lh, rh))
                    cm = S.mm(ptag, items)
                    S.wait("vector", "mt", cm)
                    sl = slice(dt_ * T + qh * 512, dt_ * T + qh * 512 + 512)
                    if si == 0:
                        cv = S.op("vector", "mv", lambda e, sl=sl, ps=ps:
                                  e.tensor_copy(g[:, sl], ps[:, :]))
                    else:
                        cv = S.op("vector", "mv", lambda e, sl=sl, ps=ps:
                                  e.tensor_tensor(g[:, sl], g[:, sl], ps[:, :],
                                                  ALU.add))
                    S.setg(ptag, "mv", cv)
            hstr_free = S.cnt["mt"]
            qw3_free = S.cnt["mt"]
            hc0 += ssz
        g_mv = S.cnt["mv"]
        S.wait("scalar", "mv", g_mv)
        for c in range(NC):
            ca = S.op("scalar", "ma", lambda e, c=c: e.activation(
                g[:, c * T:(c + 1) * T], g[:, c * T:(c + 1) * T], AF.Silu))
        g_ma = S.cnt["ma"]

        qh_t = pool[:, 0:1024]
        kh_t = pool[:, 1024:3072]
        vh_f = pool[:, 3072:5120]
        vh_t = pool[:, 5120:7168]
        et_t = pool[:, 7168:8192]
        den = R2
        S.wait("scalar", "scc", cc_ag)
        S.wait("scalar", "mv", g_mv)
        S.wait("scalar", "mt", S.cnt["mt"])
        head_mt = 0
        acc_prev = 0
        for h in range(NH):
            S.wait("scalar", "mt", head_mt)
            c_q = S.dma("scalar", qh_t, qd[h * 128:(h + 1) * 128, :], "shd")
            c_k = S.dma("scalar", kh_t[:, 0:1024],
                        k_full[h * 128:(h + 1) * 128, :], "shd")
            c_k = S.dma("scalar", kh_t[:, 1024:2048],
                        k_full[D + h * 128:D + h * 128 + 128, :], "shd")
            c_v = S.dma("scalar", vh_f[:, 0:1024],
                        v_full[h * 128:(h + 1) * 128, :], "shd")
            c_v = S.dma("scalar", vh_f[:, 1024:2048],
                        v_full[D + h * 128:D + h * 128 + 128, :], "shd")
            S.wait("tensor", "shd", c_v)
            for t in range(16):
                pt = PT[t % 2]
                S.guard(f"PT{t % 2}")
                cm = S.op("tensor", "mt", lambda e, t=t, pt=pt: e.transpose(
                    pt[:, 0:128], vh_f[:, t * 128:(t + 1) * 128], ident[:, :]))
                S.wait("vector", "mt", cm)
                cv = S.op("vector", "mv", lambda e, t=t, pt=pt: e.tensor_copy(
                    vh_t[:, t * 128:(t + 1) * 128], pt[:, 0:128]))
                S.setg(f"PT{t % 2}", "mv", cv)
            vt_mv = S.cnt["mv"]
            if h == 0:
                S.wait("tensor", "mv", ebias_mv)
                S.guard("PS0"); S.guard("PS1")
                S.guard("PV0"); S.guard("PV1")
            for t in range(16):
                S.guard("PA0"); S.guard("PA1")

                def smm(e, t=t):
                    e.matmul(PA[0][:, :], kh_t[:, t * 128:(t + 1) * 128],
                             qh_t[:, 0:512], start=True, stop=True,
                             skip_group_check=True)
                    return e.matmul(PA[1][:, :], kh_t[:, t * 128:(t + 1) * 128],
                                    qh_t[:, 512:1024], start=True, stop=True,
                                    skip_group_check=True)
                cm = S.op("tensor", "mt", smm)
                S.wait("scalar", "mt", cm)
                S.wait("scalar", "mt", acc_prev)
                ca = 0
                for qh in range(NQH):
                    ca = S.op("scalar", "ma", lambda e, qh=qh: e.activation(
                        et_t[:, qh * 512:(qh + 1) * 512], PA[qh][:, :], AF.Exp,
                        bias=ebias, scale=RS128))
                S.wait("vector", "ma", ca)

                def mask(e, t=t):
                    return e.scalar_tensor_tensor(
                        et_t[:, 0:1024], Bq[:, :], kv[:, t:t + 1],
                        et_t[:, 0:1024], op0=ALU.is_ge, op1=ALU.mult)
                cv = S.op("vector", "mv", mask)
                S.setg("PA0", "ma", ca)
                S.setg("PA1", "ma", ca)
                S.wait("tensor", "mv", cv)
                if t == 0:
                    S.wait("tensor", "mv", vt_mv)

                def acc(e, t=t):
                    e.matmul(PS[0][:, :], ones_b[:, 0:1], et_t[:, 0:512],
                             start=(t == 0), stop=(t == 15),
                             skip_group_check=True)
                    e.matmul(PS[1][:, :], ones_b[:, 0:1], et_t[:, 512:1024],
                             start=(t == 0), stop=(t == 15),
                             skip_group_check=True)
                    e.matmul(PV[0][:, :], vh_t[:, t * 128:(t + 1) * 128],
                             et_t[:, 0:512], start=(t == 0), stop=(t == 15),
                             skip_group_check=True)
                    return e.matmul(PV[1][:, :], vh_t[:, t * 128:(t + 1) * 128],
                                    et_t[:, 512:1024], start=(t == 0),
                                    stop=(t == 15), skip_group_check=True)
                acc_prev = S.op("tensor", "mt", acc)
            S.wait("vector", "mt", acc_prev)

            def dfin(e):
                e.tensor_copy(den[:, 0:512], PS[0][:, :])
                return e.tensor_copy(den[:, 512:1024], PS[1][:, :])
            cv = S.op("vector", "mv", dfin)
            S.setg("PS0", "mv", cv); S.setg("PS1", "mv", cv)
            S.wait("vector", "mv", cv)
            cv = S.op("vector", "mv", lambda e: e.reciprocal(den[:, :], den[:, :]))
            S.wait("tensor", "mv", cv)
            for qh in range(NQH):
                cmb = S.mm(f"PA{qh}", [(PA[qh][:, :], onesr_f[0:1, :],
                                        den[:, qh * 512:(qh + 1) * 512])])
                S.wait("vector", "mt", cmb)

                def avd(e, h=h, qh=qh):
                    e.tensor_copy(denb[:, qh * 512:(qh + 1) * 512], PA[qh][:, :])
                    return e.tensor_tensor(
                        att[:, h * T + qh * 512:h * T + qh * 512 + 512],
                        PV[qh][:, :], denb[:, qh * 512:(qh + 1) * 512], ALU.mult)
                cv = S.op("vector", "mv", avd)
                S.setg(f"PA{qh}", "mv", cv)
                S.setg(f"PV{qh}", "mv", cv)
            head_mt = S.cnt["mt"]
            if h == 0:
                pass
            S.guard("PS0"); S.guard("PS1")
        att_mv = S.cnt["mv"]

        ostate = {"dma": 0, "pv": 0}

        def wr_o0(ot, qh, ps, cm):
            S.wait("sync", "mv", ostate["pv"])
            cl = S.dma("sync", x0t[:, :],
                       xfm[ot * 128:(ot + 1) * 128, qh * 512:(qh + 1) * 512],
                       "sxo")
            S.wait("vector", "mt", cm)
            S.wait("vector", "sxo", cl)
            S.wait("vector", "shd", ostate["dma"])
            if ot == 0 and qh == 0:
                S.wait("vector", "ma", g_ma)

            def ev(e, ot=ot, qh=qh, ps=ps):
                e.scalar_tensor_tensor(
                    stg[:, 0:512], ps[:, :],
                    biases[:, 3 * NC + ot:3 * NC + ot + 1],
                    g[:, ot * T + qh * 512:ot * T + qh * 512 + 512],
                    op0=ALU.add, op1=ALU.mult)
                return e.tensor_tensor(x1t[:, :], stg[:, 0:512], x0t[:, :],
                                       ALU.add)
            cv = S.op("vector", "mv", ev)
            ostate["pv"] = cv
            S.wait("scalar", "mv", cv)
            ostate["dma"] = S.dma(
                "scalar", x1d[ot * 128:(ot + 1) * 128, qh * 512:(qh + 1) * 512],
                x1t[:, :], "shd")
            return ("mv", cv)

        S.wait("gpsimd", "mv", att_mv)
        emit_proj(wo0, wr_o0, att_mv, rhs_buf=att)
        x1_shd = S.cnt["shd"]

        def emit_layer1():
            S.wait("sync", "shd", x1_shd)

            def src_x1(c):
                return x1d[c * 128:(c + 1) * 128, :]
            xn1_mv = emit_layernorm(src_x1, BF16)

            vstate = {"dma": 0}

            def wr_v1(ot, qh, ps, cm):
                S.wait("vector", "mt", cm)
                S.wait("vector", "shd", vstate["dma"])
                cv = S.op("vector", "mv", lambda e, qh=qh, ps=ps: e.tensor_copy(
                    qstage[:, qh * 512:(qh + 1) * 512], ps[:, :]))
                if qh == NQH - 1:
                    S.wait("scalar", "mv", cv)
                    vstate["dma"] = S.dma(
                        "scalar", vd1[ot * 128:(ot + 1) * 128, :], qstage[:, :],
                        "shd")
                return ("mv", cv)
            emit_proj(wv1, wr_v1, xn1_mv)

            wstage = pool[:, 16384:24576]
            for wd, dstbuf in ((wq1, att), (wk1, pool)):
                for eq in range(4):
                    S.fence("gpsimd", ("mt",))
                    last = 0
                    for c in range(NC):
                        last = S.dma("gpsimd",
                                     wstage[:, c * 512:(c + 1) * 512],
                                     wd[c * 128:(c + 1) * 128,
                                        eq * 512:(eq + 1) * 512], "swa")
                    S.wait("tensor", "swa", last)
                    S.wait("tensor", "mv", xn1_mv)
                    for st_ in range(8):
                        ptag = f"PA{st_ % 2}"
                        ps = PA[st_ % 2]
                        items = []
                        for c in range(NC):
                            lh = xn[:, c * T + st_ * 128:c * T + st_ * 128 + 128]
                            rh = wstage[:, c * 512:(c + 1) * 512]
                            items.append((ps[:, :], lh, rh))
                        cm = S.mm(ptag, items)
                        S.wait("vector", "mt", cm)
                        cv = S.op("vector", "mv",
                                  lambda e, st_=st_, eq=eq, ps=ps, dstbuf=dstbuf:
                                  e.tensor_copy(
                                      dstbuf[:, st_ * 2048 + eq * 512:
                                             st_ * 2048 + eq * 512 + 512],
                                      ps[:, :]))
                        S.setg(ptag, "mv", cv)
            qk_mv = S.cnt["mv"]

            S.wait("tensor", "mv", qk_mv)
            s1state = {"dma": 0}
            for et in range(NC):
                for fq in range(4):
                    ptag = f"PA{(et * 4 + fq) % 2}"
                    ps = PA[(et * 4 + fq) % 2]
                    items = []
                    for st_ in range(8):
                        lh = att[:, st_ * 2048 + et * 128:st_ * 2048 + et * 128 + 128]
                        rh = pool[:, st_ * 2048 + fq * 512:st_ * 2048 + fq * 512 + 512]
                        items.append((ps[:, :], lh, rh))
                    cm = S.mm(ptag, items)
                    S.wait("vector", "mt", cm)
                    S.wait("vector", "shd", s1state["dma"])
                    cv = S.op("vector", "mv", lambda e, fq=fq, ps=ps:
                              e.tensor_copy(denb[0:128, fq * 256:fq * 256 + 512]
                                            if False else
                                            fbuf[fq % 2][:, 0:512], ps[:, :]))
                    S.setg(ptag, "mv", cv)
                    S.wait("scalar", "mv", cv)
                    s1state["dma"] = S.dma(
                        "scalar",
                        s1loc[et * 128:(et + 1) * 128, fq * 512:(fq + 1) * 512],
                        fbuf[fq % 2][:, 0:512], "shd")
            s1_shd = S.cnt["shd"]

            S.wait("gpsimd", "shd", s1_shd)
            cc_ar = S.op("gpsimd", "scc", lambda e: e.collective_compute(
                "AllReduce", ALU.add, replica_groups=groups,
                ins=[s1loc.ap().opt()], outs=[s1sum.ap().opt()]))

            acc4 = col[:, 3:4]
            acc5 = col[:, 4:5]
            t4, n4 = col[:, 14:15], col[:, 15:16]
            t5, n5 = col[:, 16:17], col[:, 17:18]
            emit_wstats(w1t1, D, H1, acc4)
            finalize_gamma(acc4, D * H1, t4, n4, RS[:, 24:28], "l1")
            g1row = RS[:, 25:26]
            thr4_mv = S.cnt["mv"]
            emit_wstats(w2t1, H1, D, acc5)
            finalize_gamma(acc5, H1 * D, t5, n5, RS[:, 28:32], "l1")
            g2row = RS[:, 29:30]
            thr5_mv = S.cnt["mv"]

            for tg in ("PS0", "PS1"):
                S.guard(tg)
            S.wait("vector", "mv", S.cnt["mv"])
            S.opq("vector", lambda e: e.memset(denb[:, :], 0.0))
            last_mm = 0
            cv = 0
            for c in range(NC):
                def absmax1(e, c=c):
                    e.tensor_scalar_mul(stg[:, 1024:2048],
                                        xn[:, c * T:(c + 1) * T], -1.0)
                    e.tensor_tensor(denb[:, :], denb[:, :],
                                    stg[:, 1024:2048], ALU.max)
                    return e.tensor_tensor(denb[:, :], denb[:, :],
                                           xn[:, c * T:(c + 1) * T], ALU.max)
                cv = S.op("vector", "mv", absmax1)
                cs = S.op("vector", "mv", lambda e, c=c: e.tensor_tensor(
                    stg[:, 0:1024], xn[:, c * T:(c + 1) * T],
                    xn[:, c * T:(c + 1) * T], ALU.mult))
                S.wait("tensor", "mv", cs)
                S.wait("tensor", "mt", last_mm) if False else None

                def mmq(e, first=(c == 0), stop=(c == NC - 1)):
                    e.matmul(PS[0][:, :], ones_b[:, 0:1], stg[:, 0:512],
                             start=first, stop=stop, skip_group_check=True)
                    return e.matmul(PS[1][:, :], ones_b[:, 0:1], stg[:, 512:1024],
                                    start=first, stop=stop, skip_group_check=True)
                last_mm = S.op("tensor", "mt", mmq)
                S.wait("vector", "mt", last_mm)
            S.wait("vector", "mv", cv)
            cv = S.op("vector", "mv", lambda e: e.tensor_copy(
                stg[:, 0:1024], denb[:, :]))
            S.wait("tensor", "mv", cv)
            for j in range(8):
                S.guard(f"PT{j % 2}")
                cm = S.op("tensor", "mt", lambda e, j=j: e.transpose(
                    PT[j % 2][:, 0:128], stg[:, j * 128:(j + 1) * 128],
                    ident[:, :]))
                S.wait("vector", "mt", cm)
                cv = S.op("vector", "mv", lambda e, j=j: e.reduce_max(
                    mxacc[:, j:j + 1], PT[j % 2][:, 0:128], axis=AX.X))
                S.setg(f"PT{j % 2}", "mv", cv)
            S.wait("sync", "mv", cv)
            gat = 0
            for j in range(8):
                gat = S.dma("sync", R3[:, j * 128:(j + 1) * 128].rearrange(
                    "a b -> b a"), mxacc[:, j:j + 1], "sg8") if False else gat
            for j in range(8):
                gat = S.dma("sync", R3[:, j * 128:(j + 1) * 128],
                            mxacc[:, j:j + 1], "sg8")
            S.wait("vector", "sg8", gat)
            S.wait("vector", "mt", last_mm)
            cv = S.op("vector", "mv", lambda e: e.tensor_scalar(
                R0[:, 0:512], PS[0][:, :], 1.0 / D, 1e-5, ALU.mult, ALU.add))
            cv = S.op("vector", "mv", lambda e: e.tensor_scalar(
                R0[:, 512:1024], PS[1][:, :], 1.0 / D, 1e-5, ALU.mult, ALU.add))
            S.setg("PS0", "mv", cv); S.setg("PS1", "mv", cv)
            S.wait("scalar", "mv", cv)
            ca = S.op("scalar", "ma", lambda e: e.activation(
                R0[:, :], R0[:, :], AF.Sqrt))
            S.wait("vector", "ma", ca)
            cv = S.op("vector", "mv", lambda e: e.reciprocal(R0[:, :], R0[:, :]))
            S.wait("vector", "mv", cv)
            cv = S.op("vector", "mv", lambda e: e.tensor_tensor(
                R3[:, :], R3[:, :], R0[:, :], ALU.mult))
            S.wait("vector", "mv", cv)
            cv = S.op("vector", "mv", lambda e: e.tensor_scalar_max(
                R3[:, :], R3[:, :], 1e-5))
            S.wait("vector", "mv", cv)
            cv = S.op("vector", "mv", lambda e: e.reciprocal(R1[:, :], R3[:, :]))
            S.wait("vector", "mv", cv)

            def mkalpha(e):
                e.tensor_tensor(R2[:, :], R0[:, :], R1[:, :], ALU.mult)
                return e.tensor_scalar_mul(R2[:, :], R2[:, :], 127.0)
            cv = S.op("vector", "mv", mkalpha)
            S.wait("vector", "mv", thr4_mv)
            S.wait("vector", "mv", cv)
            cv = S.op("vector", "mv", lambda e: e.tensor_single_scalar(
                R1[:, :], R3[:, :], g1row, ALU.mult))
            S.wait("vector", "mv", cv)
            cv = S.op("vector", "mv", lambda e: e.tensor_scalar_mul(
                R1[:, :], R1[:, :], 1.0 / 127.0))
            S.wait("vector", "mv", cv)
            bcast(denb[:, 0:512], R2[:, 0:512], "bc0")
            bcast(denb[:, 512:1024], R2[:, 512:1024], "bc0")
            ba_mv = S.cnt["mv"]
            S.wait("vector", "mv", ba_mv)
            for c in range(NC):
                def qa(e, c=c):
                    e.tensor_tensor(fbuf[0][:, 0:1024], xn[:, c * T:(c + 1) * T],
                                    denb[:, :], ALU.mult)
                    e.tensor_copy(stgi[:, 0:512], fbuf[0][:, 0:512])
                    e.tensor_copy(xn[:, c * T:c * T + 512], stgi[:, 0:512])
                    e.tensor_copy(stgi[:, 0:512], fbuf[0][:, 512:1024])
                    return e.tensor_copy(xn[:, c * T + 512:c * T + 1024],
                                         stgi[:, 0:512])
                cv = S.op("vector", "mv", qa)
            xq1_mv = S.cnt["mv"]

            for tg in ("PS0", "PS1", "PV0", "PV1"):
                S.guard(tg)
            S.wait("vector", "mv", xq1_mv)
            S.opq("vector", lambda e: e.memset(stg[:, 1024:2048], 0.0))
            bcast(denb[:, 0:512], R1[:, 0:512], "bc0")
            bcast(denb[:, 512:1024], R1[:, 512:1024], "bc0")
            bb_mv = S.cnt["mv"]
            NG1 = H1 // 512
            hdma = 0
            wcol_free = 0
            stat_mm = 0
            for gi in range(NG1):
                S.fence("sync", ("mv", "ma", "mt"))
                prev = [0, 0]
                ma_gate = S.cnt.get("ma", 0)
                for c in range(NC):
                    b = c % 2
                    sem = "sqa" if b == 0 else "sqb"
                    S.wait("sync", "mv", prev[b])
                    cl = S.dma("sync", fbuf[b][:, 0:512],
                               w1t1[c * 128:(c + 1) * 128,
                                    gi * 512:(gi + 1) * 512], sem)
                    S.wait("vector", sem, cl)
                    S.wait("vector", "mv", thr4_mv)
                    S.wait("vector", "ma", ma_gate if c == 0 else 0)
                    S.wait("vector", "mt", wcol_free)

                    def qf(e, b=b, c=c):
                        return ternary(e, wcolq[:, c * 512:(c + 1) * 512],
                                       fbuf[b][:, 0:512], 512, t4, n4)
                    prev[b] = S.op("vector", "mv", qf)
                q_mv = S.cnt["mv"]
                S.wait("tensor", "mv", q_mv)
                for htl in range(4):
                    ht = gi * 4 + htl
                    for qh in range(NQH):
                        ptag = f"PA{(htl * NQH + qh) % 2}"
                        ps = PA[(htl * NQH + qh) % 2]
                        items = []
                        for c in range(NC):
                            lh = wcolq[:, c * 512 + htl * 128:
                                       c * 512 + htl * 128 + 128]
                            rh = xn[:, c * T + qh * 512:c * T + qh * 512 + 512]
                            items.append((ps[:, :], lh, rh))
                        cm = S.mm(ptag, items)
                        S.wait("vector", "mt", cm)
                        S.wait("vector", "ma", S.cnt.get("ma", 0))
                        S.wait("vector", "mv", bb_mv)
                        sl5 = slice(qh * 512, qh * 512 + 512)
                        cv = S.op("vector", "mv", lambda e, ps=ps, sl5=sl5:
                                  e.tensor_tensor(fbuf[0][:, 0:512], ps[:, :],
                                                  denb[:, sl5], ALU.mult))
                        S.setg(ptag, "mv", cv)
                        S.wait("scalar", "mv", cv)
                        S.wait("scalar", "shd", hdma)
                        ca = S.op("scalar", "ma", lambda e, ht=ht: e.activation(
                            stg[:, 0:512], fbuf[0][:, 0:512], AF.Silu,
                            bias=bias1[:, ht // 2:ht // 2 + 1]
                            if False else bias1[:, ht:ht + 1] if ht < 64 else None))
                        ca = S.op("scalar", "ma", lambda e: e.activation(
                            stg[:, 512:1024], stg[:, 0:512], AF.Gelu))
                        S.wait("vector", "ma", ca)
                        S.wait("vector", "mt", stat_mm)

                        def stt_(e, sl5=sl5):
                            return e.tensor_tensor(fbuf[0][:, 0:512],
                                                   stg[:, 512:1024],
                                                   stg[:, 512:1024], ALU.mult)
                        cv = S.op("vector", "mv", stt_)
                        S.wait("tensor", "mv", cv)

                        def smm_(e, first=(ht == 0),
                                 stop=(ht == H1 // 128 - 1),
                                 qh=qh):
                            e.matmul(PS[qh][:, :], ones_b[:, 0:1],
                                     stg[:, 512:1024], start=first, stop=stop,
                                     skip_group_check=True)
                            return e.matmul(PV[qh][0:1, :], ones_f[:, 0:1],
                                            fbuf[0][:, 0:512], start=first,
                                            stop=stop, skip_group_check=True)
                        stat_mm = S.op("tensor", "mt", smm_)
                        S.wait("scalar", "mt", stat_mm)
                        hdma = S.dma("scalar",
                                     hid1[ht * 128:(ht + 1) * 128,
                                          qh * 512:(qh + 1) * 512],
                                     stg[:, 512:1024], "shd")
                wcol_free = S.cnt["mt"]
            S.wait("vector", "mt", stat_mm)
            if taps:
                S.wait("gpsimd", "shd", hdma)
                hdbg_c = S.dma("gpsimd", hdbg.ap(), hid1.ap(), "sstg")
                S.wait("scalar", "sstg", hdbg_c)

                def rawdump(e):
                    e.tensor_copy(R2[:, 0:512], PS[0][:, :])
                    e.tensor_copy(R2[:, 512:1024], PS[1][:, :])
                    e.tensor_copy(R3[:, 0:512], PV[0][0:1, :])
                    return e.tensor_copy(R3[:, 512:1024], PV[1][0:1, :])
                cvd = S.op("vector", "mv", rawdump)
                S.wait("scalar", "mv", cvd)
                cdd = S.dma("scalar", rdbg.ap(), rows[:, :], "shd")
                S.wait("vector", "shd", cdd)

            cv = S.op("vector", "mv", lambda e: e.tensor_scalar(
                R0[:, 0:512], PS[0][:, :], 1.0 / H1, None, ALU.mult))
            cv = S.op("vector", "mv", lambda e: e.tensor_scalar(
                R0[:, 512:1024], PS[1][:, :], 1.0 / H1, None, ALU.mult))
            cv = S.op("vector", "mv", lambda e: e.tensor_scalar(
                R1[:, 0:512], PV[0][0:1, :], 1.0 / H1, None, ALU.mult))
            cv = S.op("vector", "mv", lambda e: e.tensor_scalar(
                R1[:, 512:1024], PV[1][0:1, :], 1.0 / H1, None, ALU.mult))
            for tg in ("PS0", "PS1", "PV0", "PV1"):
                S.setg(tg, "mv", cv)
            S.wait("vector", "mv", cv)
            cv = S.op("vector", "mv", lambda e: e.tensor_tensor(
                R2[:, :], R0[:, :], R0[:, :], ALU.mult))
            S.wait("vector", "mv", cv)
            cv = S.op("vector", "mv", lambda e: e.tensor_tensor(
                R1[:, :], R1[:, :], R2[:, :], ALU.subtract))
            S.wait("scalar", "mv", cv)
            ca = S.op("scalar", "ma", lambda e: e.activation(
                R1[:, :], R1[:, :], AF.Sqrt, bias=eps5[0:1, :]))
            S.wait("vector", "ma", ca)
            cv = S.op("vector", "mv", lambda e: e.reciprocal(R1[:, :], R1[:, :]))
            mu_mv = S.cnt["mv"]

            S.wait("vector", "mv", mu_mv)
            bcast(denb[:, 0:512], R0[:, 0:512], "bc0")
            bcast(denb[:, 512:1024], R0[:, 512:1024], "bc0")
            bcast(fbuf[1][:, 0:512], R1[:, 0:512], "bc1", psum=PV[1])
            bcast(fbuf[1][:, 512:1024], R1[:, 512:1024], "bc1", psum=PV[1])
            bmu_mv = S.cnt["mv"]
            S.opq("vector", lambda e: e.memset(stg[:, 1024:2048], 0.0))
            for tg in ("PS0", "PS1"):
                S.guard(tg)
            h2_mm = 0
            h2dma = 0
            for hc in range(H1 // 128):
                cl = S.dma("scalar", stg[:, 0:1024],
                           hid1[hc * 128:(hc + 1) * 128, :], "shd")
                S.wait("vector", "shd", cl)
                S.wait("vector", "mv", bmu_mv)
                S.wait("vector", "mt", h2_mm)

                def lnap(e, hc=hc):
                    e.tensor_tensor(fbuf[0][:, 0:1024], stg[:, 0:1024],
                                    denb[:, :], ALU.subtract)
                    e.tensor_tensor(fbuf[0][:, 0:1024], fbuf[0][:, 0:1024],
                                    fbuf[1][:, 0:1024], ALU.mult)
                    return e.tensor_scalar(
                        stg[:, 0:1024], fbuf[0][:, 0:1024],
                        bias1[:, 64 + hc:64 + hc + 1],
                        bias1[:, 128 + hc:128 + hc + 1], ALU.mult, ALU.add)
                cv = S.op("vector", "mv", lnap)
                S.wait("vector", "mv", cv)

                def st2(e):
                    e.tensor_scalar_mul(stg[:, 2048:3072], stg[:, 0:1024],
                                        -1.0)
                    e.tensor_tensor(stg[:, 1024:2048], stg[:, 1024:2048],
                                    stg[:, 2048:3072], ALU.max)
                    e.tensor_tensor(stg[:, 1024:2048], stg[:, 1024:2048],
                                    stg[:, 0:1024], ALU.max)
                    return e.tensor_tensor(fbuf[0][:, 0:1024], stg[:, 0:1024],
                                           stg[:, 0:1024], ALU.mult)
                cv = S.op("vector", "mv", st2)
                S.wait("tensor", "mv", cv)

                def smm2(e, first=(hc == 0), stop=(hc == H1 // 128 - 1)):
                    e.matmul(PS[0][:, :], ones_f[:, 0:1], fbuf[0][:, 0:512],
                             start=first, stop=stop, skip_group_check=True)
                    return e.matmul(PS[1][:, :], ones_f[:, 0:1],
                                    fbuf[0][:, 512:1024], start=first,
                                    stop=stop, skip_group_check=True)
                h2_mm = S.op("tensor", "mt", smm2)
                S.wait("scalar", "mv", cv)
                S.wait("scalar", "mt", h2_mm)
                h2dma = S.dma("scalar", hid1[hc * 128:(hc + 1) * 128, :],
                              stg[:, 0:1024], "shd")
            S.wait("vector", "mt", h2_mm)
            S.wait("vector", "shd", h2dma)
            cv = S.op("vector", "mv", lambda e: e.tensor_copy(
                stg[:, 0:1024], stg[:, 1024:2048]))
            S.wait("tensor", "mv", cv)
            for j in range(8):
                S.guard(f"PT{j % 2}")
                cm = S.op("tensor", "mt", lambda e, j=j: e.transpose(
                    PT[j % 2][:, 0:128], stg[:, j * 128:(j + 1) * 128],
                    ident[:, :]))
                S.wait("vector", "mt", cm)
                cv = S.op("vector", "mv", lambda e, j=j: e.reduce_max(
                    mxacc[:, j:j + 1], PT[j % 2][:, 0:128], axis=AX.X))
                S.setg(f"PT{j % 2}", "mv", cv)
            S.wait("sync", "mv", cv)
            for j in range(8):
                gat = S.dma("sync", R3[:, j * 128:(j + 1) * 128],
                            mxacc[:, j:j + 1], "sg8")
            S.wait("vector", "sg8", gat)
            cv = S.op("vector", "mv", lambda e: e.tensor_scalar(
                R0[:, 0:512], PS[0][:, :], 1.0 / H1, 1e-5, ALU.mult, ALU.add))
            cv = S.op("vector", "mv", lambda e: e.tensor_scalar(
                R0[:, 512:1024], PS[1][:, :], 1.0 / H1, 1e-5, ALU.mult, ALU.add))
            S.setg("PS0", "mv", cv); S.setg("PS1", "mv", cv)
            S.wait("scalar", "mv", cv)
            ca = S.op("scalar", "ma", lambda e: e.activation(
                R0[:, :], R0[:, :], AF.Sqrt))
            S.wait("vector", "ma", ca)
            cv = S.op("vector", "mv", lambda e: e.reciprocal(R0[:, :], R0[:, :]))
            S.wait("vector", "mv", cv)
            cv = S.op("vector", "mv", lambda e: e.tensor_tensor(
                R3[:, :], R3[:, :], R0[:, :], ALU.mult))
            S.wait("vector", "mv", cv)
            cv = S.op("vector", "mv", lambda e: e.tensor_scalar_max(
                R3[:, :], R3[:, :], 1e-5))
            S.wait("vector", "mv", cv)
            cv = S.op("vector", "mv", lambda e: e.reciprocal(R1[:, :], R3[:, :]))
            S.wait("vector", "mv", cv)

            def mkalpha2(e):
                e.tensor_tensor(R2[:, :], R0[:, :], R1[:, :], ALU.mult)
                return e.tensor_scalar_mul(R2[:, :], R2[:, :], 127.0)
            cv = S.op("vector", "mv", mkalpha2)
            S.wait("vector", "mv", thr5_mv)
            S.wait("vector", "mv", cv)
            cv = S.op("vector", "mv", lambda e: e.tensor_single_scalar(
                R1[:, :], R3[:, :], g2row, ALU.mult))
            S.wait("vector", "mv", cv)
            cv = S.op("vector", "mv", lambda e: e.tensor_scalar_mul(
                R1[:, :], R1[:, :], 1.0 / 127.0))
            S.wait("vector", "mv", cv)
            bcast(denb[:, 0:512], R2[:, 0:512], "bc0")
            bcast(denb[:, 512:1024], R2[:, 512:1024], "bc0")
            ba2_mv = S.cnt["mv"]

            xq2b = pool[:, 0:8192]
            qw2s = att[:, 0:16384]
            gt1 = pool[:, 16384:17408]
            gt2 = pool[:, 17408:18432]
            sda_last = {0: 0, 1: 0}
            h3dma = 0
            xq2_used = 0
            qw2_used = 0
            prevq = [0, 0]
            prevqg = [0, 0]
            prev_qg = 0
            prev_qa = 0
            S.wait("gpsimd", "mv", thr5_mv)
            for sup in range(8):
                S.fence("sync", ("mv", "ma", "mt"))
                for j in range(8):
                    hc = sup * 8 + j
                    S.wait("scalar", "mv", prev_qa)
                    cl = S.dma("scalar", stg[:, 0:1024],
                               hid1[hc * 128:(hc + 1) * 128, :], "shd")
                    S.wait("vector", "shd", cl)
                    S.wait("vector", "mv", ba2_mv)
                    S.wait("vector", "mt", xq2_used if j == 0 else 0)

                    def qa2(e, j=j):
                        e.tensor_tensor(x0t[:, :], stg[:, 0:512],
                                        denb[:, 0:512], ALU.mult)
                        e.tensor_copy(stgi[:, 0:512], x0t[:, :])
                        e.tensor_copy(xq2b[:, j * 1024:j * 1024 + 512],
                                      stgi[:, 0:512])
                        e.tensor_tensor(x0t[:, :], stg[:, 512:1024],
                                        denb[:, 512:1024], ALU.mult)
                        e.tensor_copy(stgi[:, 0:512], x0t[:, :])
                        return e.tensor_copy(xq2b[:, j * 1024 + 512:(j + 1) * 1024],
                                             stgi[:, 0:512])
                    prev_qa = cv = S.op("vector", "mv", qa2)
                    b = j % 2
                    sem = "sqa" if b == 0 else "sqb"
                    S.wait("sync", "mv", prevq[b])
                    S.wait("sync", "mg", prevqg[b])
                    cl = S.dma("sync", fbuf[b][:, 0:2048],
                               w2t1[hc * 128:(hc + 1) * 128, :], sem)
                    S.wait("vector", sem, cl)
                    S.wait("vector", "mv", thr5_mv)
                    S.wait("vector", "mt", qw2_used if j == 0 else 0)

                    def qf2a(e, b=b, j=j):
                        t1 = stg[:, 1024:2048]
                        t2 = stg[:, 2048:3072]
                        e.tensor_single_scalar(t1, fbuf[b][:, 0:1024], t5,
                                               ALU.is_gt)
                        e.tensor_single_scalar(t2, fbuf[b][:, 0:1024], n5,
                                               ALU.is_lt)
                        return e.tensor_tensor(
                            qw2s[:, j * 2048:j * 2048 + 1024], t1,
                            t2, ALU.subtract)
                    prevq[b] = S.op("vector", "mv", qf2a)
                    S.wait("gpsimd", sem, cl)
                    S.wait("gpsimd", "mt", qw2_used if j == 0 else 0)
                    S.wait("gpsimd", "mg", prev_qg)
                    S.op("gpsimd", "mg", lambda e, b=b: e.tensor_single_scalar(
                        gt1, fbuf[b][:, 1024:2048], t5, ALU.is_gt))
                    cg2 = S.op("gpsimd", "mg", lambda e, b=b:
                               e.tensor_single_scalar(
                                   gt2, fbuf[b][:, 1024:2048], n5, ALU.is_lt))
                    S.wait("gpsimd", "mg", cg2)
                    prev_qg = S.op("gpsimd", "mg", lambda e, j=j:
                                   e.tensor_tensor(
                                       qw2s[:, j * 2048 + 1024:(j + 1) * 2048],
                                       gt1, gt2, ALU.subtract))
                    prevqg[b] = prev_qg
                q_mv = S.cnt["mv"]
                q_mg = S.cnt["mg"]
                S.wait("tensor", "mv", q_mv)
                S.wait("tensor", "mg", q_mg)
                if sup == 0 and taps:
                    S.wait("sync", "mv", q_mv)
                    cpd = S.dma("sync", pdbg[:, 0:1024],
                                xq2b[:, 7 * 1024:8 * 1024], "spd")
                    S.wait("tensor", "spd", cpd)
                sda_before = S.cnt.get("sda", 0)
                if sup == 0:
                    S.sem("sda")
                    sda_last.clear(); sda_last.update({0: 0, 1: 0})
                for dt_ in range(NC):
                    for qh in range(NQH):
                        ptag = f"PA{(dt_ * NQH + qh) % 2}"
                        ps = PA[(dt_ * NQH + qh) % 2]
                        items = []
                        for j in range(8):
                            lh = qw2s[:, j * 2048 + dt_ * 128:
                                      j * 2048 + dt_ * 128 + 128]
                            rh = xq2b[:, j * 1024 + qh * 512:
                                      j * 1024 + qh * 512 + 512]
                            items.append((ps[:, :], lh, rh))
                        cm = S.mm(ptag, items)
                        gb = (dt_ * NQH + qh) % 2
                        gt = gstg0 if gb == 0 else yst
                        S.wait("vector", "mt", cm)
                        S.wait("vector", "sda", sda_last[gb])
                        cv = S.op("vector", "mv", lambda e, ps=ps, gt=gt:
                                  e.tensor_copy(gt[:, :], ps[:, :]))
                        S.setg(ptag, "mv", cv)
                        S.wait("gpsimd", "mv", cv)
                        if sup > 0:
                            S.wait("gpsimd", "sda", sda_before)

                        def acd(e, dt_=dt_, qh=qh, gt=gt, sup=sup):
                            return e.dma_start(
                                out=g2acc[dt_ * 128:(dt_ + 1) * 128,
                                          qh * 512:(qh + 1) * 512],
                                in_=gt[:, :],
                                accum_op=(ALU.bypass if sup == 0 else ALU.add))
                        h = S.sem("sda")
                        cda = S.bump("sda", 16)
                        S.on("gpsimd", lambda e, f=acd, h=h: f(e).then_inc(h, 16))
                        sda_last[gb] = cda
                xq2_used = S.cnt["mt"]
                qw2_used = S.cnt["mt"]
                if sup == 0 and taps:
                    S.wait("sync", "mt", S.cnt["mt"])
                    cpd2 = S.dma("sync", pdbg[:, 1024:2048],
                                 xq2b[:, 7 * 1024:8 * 1024], "spd")
                    S.wait("vector", "spd", cpd2)
                if sup in (0, 1) and taps:
                    S.wait("gpsimd", "sda", S.cnt.get("sda", 0))
                    S.dma("gpsimd", g2s01[sup * 2048:(sup + 1) * 2048, :],
                          g2acc.ap(), "sstg")
                if sup == 7 and taps:
                    S.wait("scalar", "mt", xq2_used)
                    S.dma("scalar", xdbg.ap(), xq2b, "shd")
                    S.dma("scalar", qdbg.ap(), qw2s, "shd")
            S.wait("vector", "mv", S.cnt["mv"])
            bcast(denb[:, 0:512], R1[:, 0:512], "bc0")
            bcast(denb[:, 512:1024], R1[:, 512:1024], "bc0")
            bb2_mv = S.cnt["mv"]
            S.wait("sync", "sda", S.cnt.get("sda", 0))
            prev_ld = 0
            for c in range(NC):
                for qh in range(NQH):
                    sl = slice(c * T + qh * 512, c * T + qh * 512 + 512)
                    S.wait("sync", "mv", prev_ld)
                    cl = S.dma("sync", yst[:, :],
                               g2acc[c * 128:(c + 1) * 128,
                                     qh * 512:(qh + 1) * 512], "sy")
                    S.wait("vector", "sy", cl)
                    S.wait("vector", "mv", bb2_mv)
                    cv = S.op("vector", "mv", lambda e, sl=sl, qh=qh:
                              e.tensor_tensor(fbuf[0][:, 0:512], yst[:, :],
                                              denb[:, qh * 512:qh * 512 + 512],
                                              ALU.mult))
                    prev_ld = cv
                    S.wait("scalar", "mv", cv)
                    ca = S.op("scalar", "ma", lambda e, sl=sl, c=c: e.activation(
                        g[:, sl], fbuf[0][:, 0:512], AF.Silu,
                        bias=biases[:, 4 * NC + c:4 * NC + c + 1]))
                    S.wait("vector", "ma", ca)
            g1_ma = S.cnt["ma"]

            S.wait("scalar", "scc", cc_ar)
            S.wait("scalar", "ma", g1_ma)
            S.fence("scalar", ("mv", "mt"))
            vload = 0
            for c in range(NC):
                vload = S.dma("scalar", pool[:, c * T:(c + 1) * T],
                              vd1[c * 128:(c + 1) * 128, :], "shd")
            att_done = 0
            eprev_mm = 0
            for ec in range(NC):
                b = ec % 2
                sem = "sqa" if b == 0 else "sqb"
                if ec == 0:
                    S.wait("sync", "scc", cc_ar)
                S.wait("sync", "mv", S.cnt["mv"] if ec < 2 else 0)
                S.wait("sync", "mt", eprev_mm)
                cl = S.dma("sync", fbuf[b][:, 0:2048],
                           s1sum[ec * 128:(ec + 1) * 128, :], sem)
                S.wait("vector", sem, cl)
                S.wait("vector", "ma", S.cnt.get("ma", 0))
                cv = S.op("vector", "mv", lambda e, b=b: e.reduce_max(
                    col[:, 20:21], fbuf[b][:, 0:2048], axis=AX.X))
                S.wait("vector", "mv", cv)
                cv = S.op("vector", "mv", lambda e: e.tensor_scalar_mul(
                    col[:, 21:22], col[:, 20:21], -0.125))
                S.wait("scalar", "mv", cv)
                S.wait("scalar", "mt", eprev_mm)
                ca = S.op("scalar", "ma", lambda e, b=b: e.activation(
                    stg[:, 0:2048], fbuf[b][:, 0:2048], AF.Exp,
                    bias=col[:, 21:22], scale=0.125,
                    accum_out=col[:, 22:23]))
                S.wait("vector", "ma", ca)
                cv = S.op("vector", "mv", lambda e: e.reciprocal(
                    col[:, 23:24], col[:, 22:23]))
                S.wait("vector", "mv", cv)
                S.wait("vector", "mt", eprev_mm)
                cv = S.op("vector", "mv", lambda e: e.tensor_scalar_mul(
                    stg[:, 2048:4096], stg[:, 0:2048], col[:, 23:24]))
                if taps:
                    S.wait("scalar", "mv", cv)
                    S.dma("scalar", wdbg[ec * 128:(ec + 1) * 128, :],
                          stg[:, 2048:4096], "shd")
                S.wait("tensor", "mv", cv)
                avh = {}
                for fc in range(NC):
                    S.guard(f"PT{fc % 2}")
                    cm = S.op("tensor", "mt", lambda e, fc=fc: e.transpose(
                        PT[fc % 2][:, 0:128],
                        stg[:, 2048 + fc * 128:2048 + fc * 128 + 128],
                        ident[:, :]))
                    S.wait("vector", "mt", cm)
                    cv = S.op("vector", "mv", lambda e, fc=fc: e.tensor_copy(
                        stg[:, fc * 128:(fc + 1) * 128], PT[fc % 2][:, 0:128]))
                    S.setg(f"PT{fc % 2}", "mv", cv)
                S.wait("tensor", "mv", cv)
                if ec == 0:
                    S.wait("tensor", "shd", vload)

                def avmm(e, ec=ec):
                    ins = None
                    for qh in range(NQH):
                        for fc in range(NC):
                            ins = e.matmul(
                                PV[qh][:, :], stg[:, fc * 128:(fc + 1) * 128],
                                pool[:, fc * T + qh * 512:fc * T + qh * 512 + 512],
                                start=(fc == 0), stop=(fc == NC - 1),
                                skip_group_check=True)
                    return ins
                S.guard("PV0"); S.guard("PV1")
                cm = S.op("tensor", "mt", avmm)
                eprev_mm = cm
                S.wait("vector", "mt", cm)
                for qh in range(NQH):
                    cv = S.op("vector", "mv", lambda e, ec=ec, qh=qh:
                              e.tensor_copy(
                                  att[:, ec * T + qh * 512:ec * T + qh * 512 + 512],
                                  PV[qh][:, :]))
                S.setg("PV0", "mv", cv); S.setg("PV1", "mv", cv)
            attn1_mv = S.cnt["mv"]

            o1state = {"dma": 0, "pv": 0, "st": 0}

            def wr_o1(ot, qh, ps, cm):
                S.wait("sync", "mv", o1state["pv"])
                cl = S.dma("sync", x1t[:, :],
                           x1d[ot * 128:(ot + 1) * 128,
                               qh * 512:(qh + 1) * 512], "sxo")
                S.wait("vector", "mt", cm)
                S.wait("vector", "sxo", cl)
                S.wait("vector", "sst", o1state["st"])

                def ev(e, ot=ot, qh=qh, ps=ps):
                    e.tensor_tensor(stg[:, 0:512],
                                    g[:, ot * T + qh * 512:ot * T + qh * 512 + 512],
                                    ps[:, :], ALU.mult)
                    return e.tensor_tensor(yst[:, :], stg[:, 0:512], x1t[:, :],
                                           ALU.add)
                cv = S.op("vector", "mv", ev)
                o1state["pv"] = cv
                S.wait("sync", "mv", cv)
                o1state["st"] = S.dma(
                    "sync", y[ot * 128:(ot + 1) * 128, qh * 512:(qh + 1) * 512],
                    yst[:, :], "sst")
                return ("mv", cv)

            S.wait("gpsimd", "mv", attn1_mv)
            emit_proj(wo1, wr_o1, attn1_mv, rhs_buf=att)

        def emit_taps():
            for tp in taps:
                src = {"x1d": x1d, "qd": qd, "k_full": k_full,
                       "v_full": v_full, "s1sum": s1sum, "s1loc": s1loc,
                       "hid0": hid0, "hid1": hid1, "vd1": vd1,
                       "wdbg": wdbg, "rdbg": rdbg, "hdbg": hdbg,
                       "xdbg": xdbg, "qdbg": qdbg, "g2d": g2acc,
                       "g2s01": g2s01, "pdbg": pdbg}.get(tp)
                for sname in ("shd", "sxa", "sxb", "mv", "ma", "mt"):
                    S.wait("gpsimd", sname, S.cnt.get(sname, 0))
                if src is not None:
                    S.dma("gpsimd", tap_t[tp].ap(), src.ap(), "sstg")
                elif tp == "cols_t":
                    S.dma("gpsimd", tap_t[tp].ap(), col[:, :], "sstg")
                elif tp == "rows_t":
                    S.dma("gpsimd", tap_t[tp].ap(), rows[:, :], "sstg")
                else:
                    buf = {"xn_t": xn, "g_t": g, "att_t": att}[tp]
                    for c in range(NC):
                        S.dma("gpsimd", tap_t[tp][c * 128:(c + 1) * 128, :],
                              buf[:, c * T:(c + 1) * T], "sstg")

        def emit_y_from(dram):
            prev_cv, prev_st = 0, 0
            for ot in range(NC):
                for qh in range(NQH):
                    S.wait("sync", "mv", prev_cv)
                    cl = S.dma("sync", x1t[:, :],
                               dram[ot * 128:(ot + 1) * 128,
                                    qh * 512:(qh + 1) * 512], "sy")
                    S.wait("vector", "sy", cl)
                    S.wait("vector", "sst", prev_st)
                    prev_cv = S.op("vector", "mv", lambda e:
                                   e.tensor_copy(yst[:, :], x1t[:, :]))
                    S.wait("sync", "mv", prev_cv)
                    prev_st = S.dma("sync",
                                    y[ot * 128:(ot + 1) * 128,
                                      qh * 512:(qh + 1) * 512], yst[:, :],
                                    "sst")

        if only_l0:
            S.wait("sync", "shd", x1_shd)
            emit_y_from(x1d)
        else:
            emit_layer1()
        emit_taps()
        S.wait("gpsimd", "sst", S.cnt.get("sst", 0))
        S.wait("gpsimd", "sstg", S.cnt.get("sstg", 0))
        S.wait("gpsimd", "shd", S.cnt.get("shd", 0))

        with nc.Block() as block:
            @block.sync
            def _(e):
                for f in S.ops["sync"]:
                    f(e)

            @block.gpsimd
            def _(e):
                for f in S.ops["gpsimd"]:
                    f(e)

            @block.vector
            def _(e):
                for f in S.ops["vector"]:
                    f(e)

            @block.scalar
            def _(e):
                for f in S.ops["scalar"]:
                    f(e)

            @block.tensor
            def _(e):
                for f in S.ops["tensor"]:
                    f(e)

    return nc


def _prep_inputs(inp):
    f = np.ascontiguousarray
    a = np.asarray

    def f32(v):
        return np.asarray(v, dtype=np.float32)

    shared = {
        "wq0": f(f32(inp["L0_attn_w_q"])), "wk0": f(f32(inp["L0_attn_w_k"])),
        "wv0": f(f32(inp["L0_attn_w_v"])), "wo0": f(f32(inp["L0_attn_w_o"])),
        "bq0": f(f32(inp["L0_attn_b_q"]).reshape(16, 128).T),
        "bk0": f(f32(inp["L0_attn_b_k"]).reshape(16, 128).T),
        "bv0": f(f32(inp["L0_attn_b_v"]).reshape(16, 128).T),
        "bo0": f(f32(inp["L0_attn_b_o"]).reshape(16, 128).T),
        "w1t0": f(f32(inp["L0_ffn_w1"]).T),
        "w2t0": f(f32(inp["L0_ffn_w2"]).T),
        "w3t0": f(f32(inp["L0_ffn_w3"]).T),
        "wq1": f(f32(inp["L1_attn_Wq"])), "wk1": f(f32(inp["L1_attn_Wk"])),
        "wv1": f(f32(inp["L1_attn_Wv"])), "wo1": f(f32(inp["L1_attn_Wo"])),
        "w1t1": f(f32(inp["L1_ffn_w1"]).T),
        "w2t1": f(f32(inp["L1_ffn_w2"]).T),
        "b11": f(f32(inp["L1_ffn_b1"]).reshape(64, 128).T),
        "lng1": f(f32(inp["L1_ffn_ln_g"]).reshape(64, 128).T),
        "lnb1": f(f32(inp["L1_ffn_ln_b"]).reshape(64, 128).T),
        "b21": f(f32(inp["L1_ffn_b2"]).reshape(16, 128).T),
        "kvec": f((np.arange(128)[:, None] + 128.0 * np.arange(16)[None, :])
                  .astype(np.float32)),
    }
    x = f32(inp["L0_attn_query"])
    in_maps = []
    for core in range(8):
        b, half = core // 2, core % 2
        m = dict(shared)
        m["xfm"] = f(x[b, half * 1024:(half + 1) * 1024, :].T)
        m["qiota"] = (half * 1024.0
                      + np.arange(1024, dtype=np.float32))[None, :]
        in_maps.append(m)
    return in_maps


def kernel(**inputs):
    in_maps = _prep_inputs(inputs)
    nc = build_program(taps=DEBUG_TAPS, only_l0=ONLY_L0)
    res = run_bass_kernel_spmd(nc, in_maps, core_ids=list(range(8)))
    out = np.empty((4, 2048, 2048), dtype=np.float32)
    for core in range(8):
        b, half = core // 2, core % 2
        out[b, half * 1024:(half + 1) * 1024, :] = res.results[core]["y"].T
    kernel.last_results = res
    return out


# version 03
# speedup vs baseline: 1.0990x; 1.0990x over previous
import numpy as np
from contextlib import ExitStack

import concourse.bass as bass
import concourse.mybir as mybir
from concourse.bass_utils import run_bass_kernel_spmd

F32 = mybir.dt.float32
BF16 = mybir.dt.bfloat16
I32 = mybir.dt.int32
ALU = mybir.AluOpType
AF = mybir.ActivationFunctionType
AX = mybir.AxisListType

D = 2048
T = 1024
NQH = 2
NH = 16
H0 = 5632
H1 = 8192
NC = D // 128
RS128 = 0.08838834764831845

ENGINES = ("sync", "gpsimd", "vector", "scalar", "tensor")

DEBUG_TAPS = ()
ONLY_L0 = False


class Sched:
    def __init__(self, nc, stk):
        self.nc = nc
        self.stk = stk
        self.ops = {e: [] for e in ENGINES}
        self.sems = {}
        self.cnt = {}
        self.guards = {}

    def sem(self, name):
        if name not in self.sems:
            self.sems[name] = self.stk.enter_context(self.nc.semaphore(name))
            self.cnt[name] = 0
        return self.sems[name]

    def on(self, eng, fn):
        self.ops[eng].append(fn)

    def wait(self, eng, sem, cnt):
        if cnt is None or cnt <= 0:
            return
        assert cnt <= self.cnt.get(sem, 0), f"wait beyond emitted: {sem} {cnt}"
        h = self.sem(sem)
        self.on(eng, lambda e, h=h, c=cnt: e.wait_ge(h, c))

    def bump(self, sem, n):
        self.sem(sem)
        self.cnt[sem] += n
        return self.cnt[sem]

    def dma(self, eng, out, in_, sem):
        h = self.sem(sem)
        c = self.bump(sem, 16)
        self.on(eng, lambda e, o=out, i=in_, h=h:
                e.dma_start(out=o, in_=i).then_inc(h, 16))
        return c

    def op(self, eng, sem, fn):
        h = self.sem(sem)
        c = self.bump(sem, 1)
        self.on(eng, lambda e, fn=fn, h=h: fn(e).then_inc(h, 1))
        return c

    def opq(self, eng, fn):
        self.on(eng, fn)

    def fence(self, eng, sems=("mv", "mt", "ma")):
        for s in sems:
            self.wait(eng, s, self.cnt.get(s, 0))

    def guard(self, tag):
        g = self.guards.get(tag)
        if g:
            self.wait("tensor", g[0], g[1])
            self.guards[tag] = None

    def setg(self, tag, sem, cnt):
        self.guards[tag] = (sem, cnt)

    def mm(self, tag, items, sem="mt"):
        self.guard(tag)
        h = self.sem(sem)
        c = self.bump(sem, 1)
        items = list(items)

        def fn(e, items=items, h=h):
            n = len(items)
            ins = None
            for i, (ps, lh, rh) in enumerate(items):
                ins = e.matmul(ps, lh, rh, start=(i == 0), stop=(i == n - 1),
                               skip_group_check=True)
            return ins.then_inc(h, 1)
        self.on("tensor", fn)
        return c


def build_program(taps=(), only_l0=False):
    nc = bass.Bass("TRN2", target_bir_lowering=False,
                   detect_race_conditions=False)

    def din(name, shape, dtype=F32):
        return nc.dram_tensor(name, shape, dtype, kind="ExternalInput")

    xfm = din("xfm", [D, T])
    wq0 = din("wq0", [D, D]); wk0 = din("wk0", [D, D])
    wv0 = din("wv0", [D, D]); wo0 = din("wo0", [D, D])
    bq0 = din("bq0", [128, NC]); bk0 = din("bk0", [128, NC])
    bv0 = din("bv0", [128, NC]); bo0 = din("bo0", [128, NC])
    w1t0 = din("w1t0", [D, H0]); w2t0 = din("w2t0", [D, H0])
    w3t0 = din("w3t0", [H0, D])
    wq1 = din("wq1", [D, D]); wk1 = din("wk1", [D, D])
    wv1 = din("wv1", [D, D]); wo1 = din("wo1", [D, D])
    w1t1 = din("w1t1", [D, H1]); w2t1 = din("w2t1", [H1, D])
    b11 = din("b11", [128, 64]); lng1 = din("lng1", [128, 64])
    lnb1 = din("lnb1", [128, 64]); b21 = din("b21", [128, NC])
    qiota = din("qiota", [1, T])
    kvec = din("kvec", [128, NH])

    y = nc.dram_tensor("y", [D, T], F32, kind="ExternalOutput")

    x1d = nc.dram_tensor("x1d", [D, T], BF16)
    qd = nc.dram_tensor("qd", [D, T], BF16)
    k_own = nc.dram_tensor("k_own", [D, T], BF16)
    v_own = nc.dram_tensor("v_own", [D, T], BF16)
    k_full = nc.dram_tensor("k_full", [2 * D, T], BF16)
    v_full = nc.dram_tensor("v_full", [2 * D, T], BF16)
    hid0 = nc.dram_tensor("hid0", [H0, T], BF16)
    hid1 = nc.dram_tensor("hid1", [H1, T], BF16)
    s1loc = nc.dram_tensor("s1loc", [D, D], F32)
    s1sum = nc.dram_tensor("s1sum", [D, D], F32)
    vd1 = nc.dram_tensor("vd1", [D, T], BF16)
    wdbg = nc.dram_tensor("wdbg", [D, D], BF16)
    rdbg = nc.dram_tensor("rdbg", [1, 4224], F32)
    hdbg = nc.dram_tensor("hdbg", [H1, T], BF16)
    g2acc = nc.dram_tensor("g2acc", [D, T], F32)
    xdbg = nc.dram_tensor("xdbg", [128, 8192], BF16)
    g2s01 = nc.dram_tensor("g2s01", [4096, 1024], F32)
    pdbg = nc.dram_tensor("pdbg", [128, 2048], BF16)
    qdbg = nc.dram_tensor("qdbg", [128, 16384], BF16)

    tshape = {"x1d": ([D, T], BF16), "qd": ([D, T], BF16),
              "k_full": ([2 * D, T], BF16), "v_full": ([2 * D, T], BF16),
              "s1loc": ([D, D], F32), "s1sum": ([D, D], F32),
              "hid0": ([H0, T], BF16), "hid1": ([H1, T], BF16),
              "vd1": ([D, T], BF16), "xn_t": ([D, T], BF16),
              "g_t": ([D, T], BF16), "att_t": ([D, T], BF16),
              "cols_t": ([128, 32], F32), "rows_t": ([1, 4224], F32),
              "wdbg": ([D, D], BF16), "rdbg": ([1, 4224], F32),
              "hdbg": ([H1, T], BF16), "xdbg": ([128, 8192], BF16),
              "g2d": ([D, T], F32), "g2s01": ([4096, 1024], F32),
              "pdbg": ([128, 2048], BF16),
              "qdbg": ([128, 16384], BF16)}
    tap_t = {tp: nc.dram_tensor("tap_" + tp, *tshape[tp], kind="ExternalOutput")
             for tp in taps}

    groups = [[0, 1], [2, 3], [4, 5], [6, 7]]

    with ExitStack() as stk:
        e_ = stk.enter_context
        S = Sched(nc, stk)

        xn = e_(nc.sbuf_tensor("xn", [128, NC * T], BF16))
        g = e_(nc.sbuf_tensor("g", [128, NC * T], BF16))
        att = e_(nc.sbuf_tensor("att", [128, NC * T], BF16))
        pool = e_(nc.sbuf_tensor("pool", [128, 24576], BF16))
        fbuf = [e_(nc.sbuf_tensor(f"fbuf{i}", [128, 2048], F32)) for i in range(2)]
        Bq = e_(nc.sbuf_tensor("Bq", [128, T], F32))
        stg = e_(nc.sbuf_tensor("stg", [128, 4096], BF16))
        stgi = e_(nc.sbuf_tensor("stgi", [128, 512], I32))
        rows = e_(nc.sbuf_tensor("rows", [1, 4224], F32))
        biases = e_(nc.sbuf_tensor("biases", [128, 5 * NC], F32))
        bias1 = e_(nc.sbuf_tensor("bias1", [128, 192], F32))
        qstage = e_(nc.sbuf_tensor("qstage", [128, T], BF16))
        x0t = e_(nc.sbuf_tensor("x0t", [128, 512], F32))
        x1t = e_(nc.sbuf_tensor("x1t", [128, 512], BF16))
        yst = e_(nc.sbuf_tensor("yst", [128, 512], F32))
        ident = e_(nc.sbuf_tensor("ident", [128, 128], BF16))
        ones_b = e_(nc.sbuf_tensor("ones_b", [128, 1], BF16))
        ones_f = e_(nc.sbuf_tensor("ones_f", [128, 1], F32))
        onesr_f = e_(nc.sbuf_tensor("onesr_f", [1, 128], F32))
        kv = e_(nc.sbuf_tensor("kv", [128, NH], F32))
        col = e_(nc.sbuf_tensor("col", [128, 32], F32))
        mx2 = e_(nc.sbuf_tensor("mx2", [128, 8], BF16))
        denb = e_(nc.sbuf_tensor("denb", [128, 1024], F32))
        colacc = e_(nc.sbuf_tensor("colacc", [128, 64], F32))
        gstg0 = e_(nc.sbuf_tensor("gstg0", [128, 512], F32))
        mxacc = e_(nc.sbuf_tensor("mxacc", [128, 64], F32))

        R0 = rows[0:1, 0:1024]
        R1 = rows[0:1, 1024:2048]
        R2 = rows[0:1, 2048:3072]
        R3 = rows[0:1, 3072:4096]
        RS = rows[0:1, 4096:4224]

        PA = [e_(nc.psum_tensor(f"PA{i}", [128, 512], F32)) for i in range(2)]
        PV = [e_(nc.psum_tensor(f"PV{i}", [128, 512], F32)) for i in range(2)]
        PS = [e_(nc.psum_tensor(f"PS{i}", [1, 512], F32)) for i in range(2)]
        PT = [e_(nc.psum_tensor(f"PT{i}", [128, 1024], BF16)) for i in range(2)]

        def bcast(dst_col, src_row, gtag="bc0", psum=None):
            n = src_row.shape[1]
            ps = PV[0] if psum is None else psum
            S.wait("tensor", "mv", S.cnt["mv"])
            S.guard(gtag)
            c = S.mm(gtag, [(ps[:, 0:n], onesr_f[0:1, :], src_row)])
            S.wait("vector", "mt", c)
            cv = S.op("vector", "mv", lambda e, d=dst_col, n=n, ps=ps:
                      e.tensor_copy(d, ps[:, 0:n]))
            S.setg(gtag, "mv", cv)
            return cv

        for dst, srcd in ((Bq[0:1, :], qiota[:, :]), (kv[:, :], kvec[:, :]),
                          (biases[:, 0:NC], bq0[:, :]),
                          (biases[:, NC:2 * NC], bk0[:, :]),
                          (biases[:, 2 * NC:3 * NC], bv0[:, :]),
                          (biases[:, 3 * NC:4 * NC], bo0[:, :]),
                          (biases[:, 4 * NC:5 * NC], b21[:, :]),
                          (bias1[:, 0:64], b11[:, :]),
                          (bias1[:, 64:128], lng1[:, :]),
                          (bias1[:, 128:192], lnb1[:, :])):
            c_ini = S.dma("sync", dst, srcd, "sini")
            S.wait("sync", "sini", c_ini)
        eps5 = col[:, 7:8]
        S.op("gpsimd", "mg", lambda e: e.memset(rows[:, :], 0.0))
        S.op("gpsimd", "mg", lambda e: e.memset(col[:, :], 0.0))
        S.op("gpsimd", "mg", lambda e: e.memset(ones_b[:, :], 1.0))
        S.op("gpsimd", "mg", lambda e: e.memset(ones_f[:, :], 1.0))
        S.op("gpsimd", "mg", lambda e: e.memset(onesr_f[:, :], 1.0))
        S.wait("gpsimd", "mg", S.cnt["mg"])
        S.op("gpsimd", "mg", lambda e: e.memset(eps5, 1e-5))
        c_mident = S.op("gpsimd", "mg", lambda e: e.memset(ident[:, :], 0.0))
        S.wait("gpsimd", "mg", c_mident)
        cg_ident = S.op("gpsimd", "mg", lambda e: e.affine_select(
            out=ident[:, :], in_=ident[:, :], compare_op=ALU.not_equal,
            fill=1.0, base=0, pattern=[[-1, 128]], channel_multiplier=1))
        S.wait("tensor", "mg", cg_ident)
        S.wait("vector", "mg", cg_ident)
        S.wait("scalar", "mg", cg_ident)
        S.wait("tensor", "sini", S.cnt["sini"])
        for qh in range(NQH):
            c = S.mm(f"PA{qh}", [(PA[qh][:, :], onesr_f[0:1, :],
                                  Bq[0:1, qh * 512:(qh + 1) * 512])])
            S.wait("vector", "mt", c)
            cv = S.op("vector", "mv", lambda e, qh=qh: e.tensor_copy(
                Bq[:, qh * 512:(qh + 1) * 512], PA[qh][:, :]))
            S.setg(f"PA{qh}", "mv", cv)

        def emit_layernorm(src_of_chunk, src_dtype):
            f32src = src_dtype == F32
            onesv = ones_f if f32src else ones_b
            if f32src:
                xt = [fbuf[0][:, 0:1024], fbuf[0][:, 1024:2048]]
                sqt = fbuf[1][:, 0:1024]
            else:
                xt = [pool[:, 0:1024], pool[:, 1024:2048]]
                sqt = pool[:, 2048:3072]
            for tg in ("PS0", "PS1", "PV0", "PV1"):
                S.guard(tg)
            S.fence("sync")
            last_mm = 0
            prev_mm = [0, 0]
            fence_v = S.cnt["mv"]
            for c in range(NC):
                b = c % 2
                sem = "sxa" if b == 0 else "sxb"
                S.wait("sync", "mt", prev_mm[b])
                if c == 0:
                    S.wait("sync", "mv", fence_v)
                cl = S.dma("sync", xt[b], src_of_chunk(c), sem)
                S.wait("vector", sem, cl)
                S.wait("vector", "mt", last_mm)
                cv = S.op("vector", "mv", lambda e, t=xt[b], s=sqt:
                          e.tensor_tensor(s, t, t, ALU.mult))
                S.wait("tensor", sem, cl)
                S.wait("tensor", "mv", cv)

                def mmfn(e, t=xt[b], s=sqt, first=(c == 0), stop=(c == NC - 1)):
                    ins = None
                    for qh in range(NQH):
                        sl = slice(qh * 512, (qh + 1) * 512)
                        e.matmul(PS[qh][:, :], onesv[:, 0:1], t[:, sl],
                                 start=first, stop=stop, skip_group_check=True)
                        ins = e.matmul(PV[qh][0:1, :], onesv[:, 0:1], s[:, sl],
                                       start=first, stop=stop,
                                       skip_group_check=True)
                    return ins
                last_mm = S.op("tensor", "mt", mmfn)
                prev_mm[b] = last_mm
            S.wait("vector", "mt", last_mm)

            def fin1(e):
                e.tensor_scalar(R0[:, 0:512], PS[0][:, :], 1.0 / D, None, ALU.mult)
                e.tensor_scalar(R0[:, 512:1024], PS[1][:, :], 1.0 / D, None, ALU.mult)
                e.tensor_scalar(R1[:, 0:512], PV[0][0:1, :], 1.0 / D, None, ALU.mult)
                return e.tensor_scalar(R1[:, 512:1024], PV[1][0:1, :], 1.0 / D,
                                       None, ALU.mult)
            cv = S.op("vector", "mv", fin1)
            for tg in ("PS0", "PS1", "PV0", "PV1"):
                S.setg(tg, "mv", cv)
            S.wait("vector", "mv", cv)
            cv = S.op("vector", "mv", lambda e: e.tensor_tensor(
                R2[:, :], R0[:, :], R0[:, :], ALU.mult))
            S.wait("vector", "mv", cv)
            cv = S.op("vector", "mv", lambda e: e.tensor_tensor(
                R1[:, :], R1[:, :], R2[:, :], ALU.subtract))
            S.wait("scalar", "mv", cv)
            S.wait("scalar", "mg", cg_ident)
            ca = S.op("scalar", "ma", lambda e: e.activation(
                R1[:, :], R1[:, :], AF.Sqrt, bias=eps5[0:1, :], scale=1.0))
            S.wait("vector", "ma", ca)
            cv = S.op("vector", "mv", lambda e: e.reciprocal(R1[:, :], R1[:, :]))
            prev = [0, 0]
            for qh in range(NQH):
                sl = slice(qh * 512, (qh + 1) * 512)
                S.wait("tensor", "mv", cv)
                c1 = S.mm("PV0", [(PV[0][:, :], onesr_f[0:1, :], R0[:, sl])])
                c2 = S.mm("PV1", [(PV[1][:, :], onesr_f[0:1, :], R1[:, sl])])
                for c in range(NC):
                    b = c % 2
                    sem = "sxa" if b == 0 else "sxb"
                    S.wait("sync", "mv", prev[b])
                    if c < 2 and qh == 0:
                        S.wait("sync", "mt", last_mm)
                    cl = S.dma("sync", xt[b], src_of_chunk(c), sem)
                    S.wait("vector", sem, cl)
                    S.wait("vector", "mt", c2)

                    def ap(e, t=xt[b], c=c, sl=sl):
                        e.tensor_tensor(t[:, sl], t[:, sl], PV[0][:, :],
                                        ALU.subtract)
                        return e.tensor_tensor(
                            xn[:, c * T + sl.start:c * T + sl.stop],
                            t[:, sl], PV[1][:, :], ALU.mult)
                    prev[b] = S.op("vector", "mv", ap)
                pc = max(prev)
                S.setg("PV0", "mv", pc)
                S.setg("PV1", "mv", pc)
            return S.cnt["mv"]

        wqa = pool[:, 0:8192]
        wqb = pool[:, 8192:16384]
        wfree = {0: None, 1: None}

        def emit_proj(wdram, out_writer, xn_ready, rhs_buf=None):
            loaded = {}
            rb = xn if rhs_buf is None else rhs_buf
            S.fence("gpsimd", ("mv", "mt", "ma"))

            def load_quarter(q):
                b = q % 2
                if wfree[b] is not None:
                    S.wait("gpsimd", wfree[b][0], wfree[b][1])
                    wfree[b] = None
                sem = "swa" if b == 0 else "swb"
                last = 0
                for c in range(NC):
                    last = S.dma("gpsimd", (wqa if b == 0 else wqb)[:, c * 512:(c + 1) * 512],
                                 wdram[c * 128:(c + 1) * 128, q * 512:(q + 1) * 512],
                                 sem)
                loaded[q] = (sem, last)

            def compute_quarter(q):
                b = q % 2
                wst = wqa if b == 0 else wqb
                S.wait("tensor", loaded[q][0], loaded[q][1])
                S.wait("tensor", "mv", xn_ready)
                for otl in range(4):
                    ot = q * 4 + otl
                    for qh in range(NQH):
                        ptag = f"PA{(ot * NQH + qh) % 2}"
                        ps = PA[(ot * NQH + qh) % 2]
                        items = []
                        for c in range(NC):
                            lh = wst[:, c * 512 + otl * 128:c * 512 + otl * 128 + 128]
                            rh = rb[:, c * T + qh * 512:c * T + qh * 512 + 512]
                            items.append((ps[:, :], lh, rh))
                        cm = S.mm(ptag, items)
                        sem, cnt = out_writer(ot, qh, ps, cm)
                        S.setg(ptag, sem, cnt)
                wfree[b] = ("mt", S.cnt["mt"])

            load_quarter(0)
            for q in range(1, 4):
                load_quarter(q)
                compute_quarter(q - 1)
            compute_quarter(3)

        qkv_state = {"dma": 0}

        def mk_qkv_writer(dst, bidx, maxcol):
            state = qkv_state

            def wr(ot, qh, ps, cm):
                S.wait("vector", "mt", cm)
                S.wait("vector", "shd", state["dma"])

                def ev(e, ot=ot, qh=qh, ps=ps):
                    return e.tensor_scalar_add(
                        qstage[:, qh * 512:(qh + 1) * 512], ps[:, :],
                        biases[:, bidx * NC + ot:bidx * NC + ot + 1])
                cv = S.op("vector", "mv", ev)
                if maxcol is not None:
                    idx = maxcol * 32 + ot * 2 + qh
                    cv = S.op("vector", "mv", lambda e, qh=qh, idx=idx:
                              e.reduce_max(mxacc[:, idx:idx + 1],
                                           qstage[:, qh * 512:(qh + 1) * 512],
                                           axis=AX.X, apply_absolute_value=True))
                if qh == NQH - 1:
                    S.wait("scalar", "mv", cv)
                    state["dma"] = S.dma(
                        "scalar", dst[ot * 128:(ot + 1) * 128, :], qstage[:, :],
                        "shd")
                return ("mv", cv)
            return wr

        def emit_wstats(wdram, nrows, rowlen, acc):
            S.fence("sync")
            prev = [0, 0]
            k = 0
            segs = []
            o = 0
            while o < rowlen:
                w = min(2048, rowlen - o)
                segs.append((o, w))
                o += w
            nacc = len(segs) * (nrows // 128)
            assert nacc <= 64
            for c in range(nrows // 128):
                for (o, w) in segs:
                    b = k % 2
                    S.wait("sync", "mv", prev[b])
                    sem = "sqa" if b == 0 else "sqb"
                    cl = S.dma("sync", fbuf[b][:, 0:w],
                               wdram[c * 128:(c + 1) * 128, o:o + w], sem)
                    S.wait("vector", sem, cl)
                    prev[b] = S.op("vector", "mv",
                                   lambda e, b=b, w=w, k=k: e.reduce_sum(
                                       colacc[:, k:k + 1], fbuf[b][:, 0:w],
                                       axis=AX.X, apply_absolute_value=True))
                    k += 1
            cv = S.cnt["mv"]
            S.wait("vector", "mv", cv)
            cv = S.op("vector", "mv", lambda e, n=nacc: e.reduce_sum(
                acc, colacc[:, 0:n], axis=AX.X))
            S.wait("vector", "mv", cv)

        def finalize_gamma(acc, nelem, tcol, ncol, grow, mode):
            S.wait("tensor", "mv", S.cnt["mv"])
            cm = S.mm("PS0", [(PS[0][0:1, 0:1], ones_f[:, 0:1], acc)])
            S.wait("vector", "mt", cm)
            op1 = ALU.add if mode == "l0" else ALU.max
            c2 = 1e-8 if mode == "l0" else 1e-5
            cv = S.op("vector", "mv", lambda e: e.tensor_scalar(
                grow[:, 1:2], PS[0][0:1, 0:1], 1.0 / nelem, c2, ALU.mult, op1))
            S.setg("PS0", "mv", cv)
            S.wait("vector", "mv", cv)

            def fin2(e):
                e.tensor_scalar_mul(grow[:, 2:3], grow[:, 1:2], 0.5)
                return e.tensor_scalar_mul(grow[:, 3:4], grow[:, 1:2], -0.5)
            cv = S.op("vector", "mv", fin2)
            S.wait("vector", "mv", cv)
            bcast(tcol, grow[:, 2:3], "bc0")
            bcast(ncol, grow[:, 3:4], "bc0")
            return S.cnt["mv"]

        def ternary(e, dst, src, w, tcol, ncol):
            t1 = stg[:, 0:2048]
            t2 = stg[:, 2048:4096]
            e.tensor_single_scalar(t1[:, 0:w], src, tcol, ALU.is_gt)
            e.tensor_single_scalar(t2[:, 0:w], src, ncol, ALU.is_lt)
            return e.tensor_tensor(dst, t1[:, 0:w], t2[:, 0:w], ALU.subtract)

        def src_x0(c):
            return xfm[c * 128:(c + 1) * 128, :]

        xn_mv = emit_layernorm(src_x0, F32)

        emit_proj(wq0, mk_qkv_writer(qd, 0, 0), xn_mv)
        emit_proj(wk0, mk_qkv_writer(k_own, 1, 1), xn_mv)
        emit_proj(wv0, mk_qkv_writer(v_own, 2, None), xn_mv)
        qkv_shd = S.cnt["shd"]

        S.wait("gpsimd", "shd", qkv_shd)
        S.op("gpsimd", "scc", lambda e: e.collective_compute(
            "AllGather", ALU.bypass, replica_groups=groups,
            ins=[k_own.ap().opt()], outs=[k_full.ap().opt()]))
        cc_ag = S.op("gpsimd", "scc", lambda e: e.collective_compute(
            "AllGather", ALU.bypass, replica_groups=groups,
            ins=[v_own.ap().opt()], outs=[v_full.ap().opt()]))

        S.wait("vector", "mv", S.cnt["mv"])
        cv = S.op("vector", "mv", lambda e: e.reduce_max(
            mx2[:, 0:1], mxacc[:, 0:32], axis=AX.X))
        cv = S.op("vector", "mv", lambda e: e.reduce_max(
            mx2[:, 1:2], mxacc[:, 32:64], axis=AX.X))
        S.wait("tensor", "mv", cv)
        S.wait("tensor", "mg", cg_ident)
        for i in range(2):
            S.guard(f"PT{i}")
            ctr = S.op("tensor", "mt", lambda e, i=i: e.transpose(
                PT[i][0:1, 0:128], mx2[:, i:i + 1], ident[:, :]))
            S.wait("vector", "mt", ctr)
            cvi = S.op("vector", "mv", lambda e, i=i: e.tensor_copy(
                stg[0:1, i * 128:(i + 1) * 128], PT[i][0:1, 0:128]))
            S.setg(f"PT{i}", "mv", cvi)
        S.wait("vector", "mv", S.cnt["mv"])
        cv = S.op("vector", "mv", lambda e: e.reduce_max(
            RS[:, 0:1], stg[0:1, 0:128], axis=AX.X))
        cv = S.op("vector", "mv", lambda e: e.reduce_max(
            RS[:, 1:2], stg[0:1, 128:256], axis=AX.X))
        S.wait("vector", "mv", cv)
        cv = S.op("vector", "mv", lambda e: e.tensor_tensor(
            RS[:, 2:3], RS[:, 0:1], RS[:, 1:2], ALU.mult))
        S.wait("vector", "mv", cv)
        cv = S.op("vector", "mv", lambda e: e.tensor_scalar_mul(
            RS[:, 2:3], RS[:, 2:3], -128.0 * RS128))
        S.wait("vector", "mv", cv)
        ebias = col[:, 2:3]
        bcast(ebias, RS[:, 2:3], "bc0")
        ebias_mv = S.cnt["mv"]

        thr = {}
        for i, (wd, nr, rl) in enumerate(((w1t0, D, H0), (w2t0, D, H0),
                                          (w3t0, H0, D))):
            acc = col[:, 3 + i:4 + i]
            emit_wstats(wd, nr, rl, acc)
            tcol = col[:, 8 + 2 * i:9 + 2 * i]
            ncol = col[:, 9 + 2 * i:10 + 2 * i]
            finalize_gamma(acc, nr * rl, tcol, ncol,
                           RS[:, 8 + 4 * i:12 + 4 * i], "l0")
            thr[i] = (tcol, ncol, S.cnt["mv"])

        wcolq = pool[:, 16384:24576]
        h1grp = pool[:, 0:4096]
        NG0 = H0 // 512
        hid_dma = 0
        wcol_free = 0
        for gi in range(NG0):
            for wi, (wd, (tcol, ncol, tmv)) in enumerate(
                    ((w1t0, thr[0]), (w2t0, thr[1]))):
                S.fence("sync", ("mv",))
                prev = [0, 0]
                for c in range(NC):
                    b = c % 2
                    S.wait("sync", "mv", prev[b])
                    sem = "sqa" if b == 0 else "sqb"
                    cl = S.dma("sync", fbuf[b][:, 0:512],
                               wd[c * 128:(c + 1) * 128,
                                  gi * 512:(gi + 1) * 512], sem)
                    S.wait("vector", sem, cl)
                    S.wait("vector", "mv", tmv)
                    S.wait("vector", "mt", wcol_free)

                    def qf(e, b=b, c=c, tcol=tcol, ncol=ncol):
                        return ternary(e, wcolq[:, c * 512:(c + 1) * 512],
                                       fbuf[b][:, 0:512], 512, tcol, ncol)
                    prev[b] = S.op("vector", "mv", qf)
                q_mv = S.cnt["mv"]
                S.wait("tensor", "mv", q_mv)
                for htl in range(4):
                    for qh in range(NQH):
                        ptag = f"PA{(htl * NQH + qh) % 2}"
                        ps = PA[(htl * NQH + qh) % 2]
                        items = []
                        for c in range(NC):
                            lh = wcolq[:, c * 512 + htl * 128:c * 512 + htl * 128 + 128]
                            rh = xn[:, c * T + qh * 512:c * T + qh * 512 + 512]
                            items.append((ps[:, :], lh, rh))
                        cm = S.mm(ptag, items)
                        sl = slice(htl * T + qh * 512, htl * T + qh * 512 + 512)
                        if wi == 0:
                            S.wait("scalar", "mt", cm)
                            S.wait("scalar", "shd", hid_dma)
                            ca = S.op("scalar", "ma", lambda e, sl=sl, ps=ps:
                                      e.activation(h1grp[:, sl], ps[:, :],
                                                   AF.Sigmoid))
                            S.setg(ptag, "ma", ca)
                        else:
                            S.wait("vector", "mt", cm)
                            S.wait("vector", "ma", S.cnt["ma"])
                            cv = S.op("vector", "mv", lambda e, sl=sl, ps=ps:
                                      e.tensor_tensor(h1grp[:, sl], h1grp[:, sl],
                                                      ps[:, :], ALU.mult))
                            S.setg(ptag, "mv", cv)
                            ht_g = gi * 4 + htl
                            S.wait("scalar", "mv", cv)
                            hid_dma = S.dma(
                                "scalar",
                                hid0[ht_g * 128:(ht_g + 1) * 128,
                                     qh * 512:(qh + 1) * 512],
                                h1grp[:, sl], "shd")
                wcol_free = S.cnt["mt"]

        qw3s = att[:, 0:16384]
        hstr = pool[:, 0:8192]
        sup_sizes = [8, 8, 8, 8, 8, 4]
        hc0 = 0
        hstr_free = 0
        qw3_free = 0
        for si, ssz in enumerate(sup_sizes):
            S.fence("sync", ("mv",))
            hload = 0
            prevq = [0, 0]
            for j in range(ssz):
                hc = hc0 + j
                S.wait("scalar", "shd", hid_dma if si == 0 and j == 0 else 0)
                S.wait("scalar", "mt", hstr_free if j == 0 else 0)
                hload = S.dma("scalar", hstr[:, j * 1024:(j + 1) * 1024],
                              hid0[hc * 128:(hc + 1) * 128, :], "shd")
                b = j % 2
                S.wait("sync", "mv", prevq[b])
                sem = "sqa" if b == 0 else "sqb"
                cl = S.dma("sync", fbuf[b][:, 0:2048],
                           w3t0[hc * 128:(hc + 1) * 128, :], sem)
                S.wait("vector", sem, cl)
                S.wait("vector", "mv", thr[2][2] if si == 0 and j == 0 else 0)
                S.wait("vector", "mt", qw3_free if j == 0 else 0)

                def qf3(e, b=b, j=j):
                    return ternary(e, qw3s[:, j * 2048:(j + 1) * 2048],
                                   fbuf[b][:, 0:2048], 2048, thr[2][0], thr[2][1])
                prevq[b] = S.op("vector", "mv", qf3)
            q_mv = S.cnt["mv"]
            S.wait("tensor", "mv", q_mv)
            S.wait("tensor", "shd", hload)
            for dt_ in range(NC):
                for qh in range(NQH):
                    ptag = f"PV{(dt_ * NQH + qh) % 2}"
                    ps = PV[(dt_ * NQH + qh) % 2]
                    items = []
                    for j in range(ssz):
                        lh = qw3s[:, j * 2048 + dt_ * 128:j * 2048 + dt_ * 128 + 128]
                        rh = hstr[:, j * 1024 + qh * 512:j * 1024 + qh * 512 + 512]
                        items.append((ps[:, :], lh, rh))
                    cm = S.mm(ptag, items)
                    S.wait("vector", "mt", cm)
                    sl = slice(dt_ * T + qh * 512, dt_ * T + qh * 512 + 512)
                    if si == 0:
                        cv = S.op("vector", "mv", lambda e, sl=sl, ps=ps:
                                  e.tensor_copy(g[:, sl], ps[:, :]))
                    else:
                        cv = S.op("vector", "mv", lambda e, sl=sl, ps=ps:
                                  e.tensor_tensor(g[:, sl], g[:, sl], ps[:, :],
                                                  ALU.add))
                    S.setg(ptag, "mv", cv)
            hstr_free = S.cnt["mt"]
            qw3_free = S.cnt["mt"]
            hc0 += ssz
        g_mv = S.cnt["mv"]
        S.wait("scalar", "mv", g_mv)
        for c in range(NC):
            ca = S.op("scalar", "ma", lambda e, c=c: e.activation(
                g[:, c * T:(c + 1) * T], g[:, c * T:(c + 1) * T], AF.Silu))
        g_ma = S.cnt["ma"]

        qh_t = pool[:, 0:1024]
        kh_t = pool[:, 1024:3072]
        vh_f = pool[:, 3072:5120]
        vh_t = pool[:, 5120:7168]
        et_t = pool[:, 7168:8192]
        den = R2
        S.wait("scalar", "scc", cc_ag)
        S.wait("scalar", "mv", g_mv)
        S.wait("scalar", "mt", S.cnt["mt"])
        head_mt = 0
        acc_prev = 0
        for h in range(NH):
            S.wait("scalar", "mt", head_mt)
            c_q = S.dma("scalar", qh_t, qd[h * 128:(h + 1) * 128, :], "shd")
            c_k = S.dma("scalar", kh_t[:, 0:1024],
                        k_full[h * 128:(h + 1) * 128, :], "shd")
            c_k = S.dma("scalar", kh_t[:, 1024:2048],
                        k_full[D + h * 128:D + h * 128 + 128, :], "shd")
            c_v = S.dma("scalar", vh_f[:, 0:1024],
                        v_full[h * 128:(h + 1) * 128, :], "shd")
            c_v = S.dma("scalar", vh_f[:, 1024:2048],
                        v_full[D + h * 128:D + h * 128 + 128, :], "shd")
            S.wait("tensor", "shd", c_v)
            for t in range(16):
                pt = PT[t % 2]
                S.guard(f"PT{t % 2}")
                cm = S.op("tensor", "mt", lambda e, t=t, pt=pt: e.transpose(
                    pt[:, 0:128], vh_f[:, t * 128:(t + 1) * 128], ident[:, :]))
                S.wait("vector", "mt", cm)
                cv = S.op("vector", "mv", lambda e, t=t, pt=pt: e.tensor_copy(
                    vh_t[:, t * 128:(t + 1) * 128], pt[:, 0:128]))
                S.setg(f"PT{t % 2}", "mv", cv)
            vt_mv = S.cnt["mv"]
            if h == 0:
                S.wait("tensor", "mv", ebias_mv)
                S.guard("PS0"); S.guard("PS1")
                S.guard("PV0"); S.guard("PV1")
            for t in range(16):
                S.guard("PA0"); S.guard("PA1")

                def smm(e, t=t):
                    e.matmul(PA[0][:, :], kh_t[:, t * 128:(t + 1) * 128],
                             qh_t[:, 0:512], start=True, stop=True,
                             skip_group_check=True)
                    return e.matmul(PA[1][:, :], kh_t[:, t * 128:(t + 1) * 128],
                                    qh_t[:, 512:1024], start=True, stop=True,
                                    skip_group_check=True)
                cm = S.op("tensor", "mt", smm)
                S.wait("scalar", "mt", cm)
                S.wait("scalar", "mt", acc_prev)
                ca = 0
                for qh in range(NQH):
                    ca = S.op("scalar", "ma", lambda e, qh=qh: e.activation(
                        et_t[:, qh * 512:(qh + 1) * 512], PA[qh][:, :], AF.Exp,
                        bias=ebias, scale=RS128))
                S.wait("vector", "ma", ca)

                def mask(e, t=t):
                    return e.scalar_tensor_tensor(
                        et_t[:, 0:1024], Bq[:, :], kv[:, t:t + 1],
                        et_t[:, 0:1024], op0=ALU.is_ge, op1=ALU.mult)
                cv = S.op("vector", "mv", mask)
                S.setg("PA0", "ma", ca)
                S.setg("PA1", "ma", ca)
                S.wait("tensor", "mv", cv)
                if t == 0:
                    S.wait("tensor", "mv", vt_mv)

                def acc(e, t=t):
                    e.matmul(PS[0][:, :], ones_b[:, 0:1], et_t[:, 0:512],
                             start=(t == 0), stop=(t == 15),
                             skip_group_check=True)
                    e.matmul(PS[1][:, :], ones_b[:, 0:1], et_t[:, 512:1024],
                             start=(t == 0), stop=(t == 15),
                             skip_group_check=True)
                    e.matmul(PV[0][:, :], vh_t[:, t * 128:(t + 1) * 128],
                             et_t[:, 0:512], start=(t == 0), stop=(t == 15),
                             skip_group_check=True)
                    return e.matmul(PV[1][:, :], vh_t[:, t * 128:(t + 1) * 128],
                                    et_t[:, 512:1024], start=(t == 0),
                                    stop=(t == 15), skip_group_check=True)
                acc_prev = S.op("tensor", "mt", acc)
            S.wait("vector", "mt", acc_prev)

            def dfin(e):
                e.tensor_copy(den[:, 0:512], PS[0][:, :])
                return e.tensor_copy(den[:, 512:1024], PS[1][:, :])
            cv = S.op("vector", "mv", dfin)
            S.setg("PS0", "mv", cv); S.setg("PS1", "mv", cv)
            S.wait("vector", "mv", cv)
            cv = S.op("vector", "mv", lambda e: e.reciprocal(den[:, :], den[:, :]))
            S.wait("tensor", "mv", cv)
            for qh in range(NQH):
                cmb = S.mm(f"PA{qh}", [(PA[qh][:, :], onesr_f[0:1, :],
                                        den[:, qh * 512:(qh + 1) * 512])])
                S.wait("vector", "mt", cmb)

                def avd(e, h=h, qh=qh):
                    e.tensor_copy(denb[:, qh * 512:(qh + 1) * 512], PA[qh][:, :])
                    return e.tensor_tensor(
                        att[:, h * T + qh * 512:h * T + qh * 512 + 512],
                        PV[qh][:, :], denb[:, qh * 512:(qh + 1) * 512], ALU.mult)
                cv = S.op("vector", "mv", avd)
                S.setg(f"PA{qh}", "mv", cv)
                S.setg(f"PV{qh}", "mv", cv)
            head_mt = S.cnt["mt"]
            if h == 0:
                pass
            S.guard("PS0"); S.guard("PS1")
        att_mv = S.cnt["mv"]

        ostate = {"dma": 0, "pv": 0}

        def wr_o0(ot, qh, ps, cm):
            S.wait("sync", "mv", ostate["pv"])
            cl = S.dma("sync", x0t[:, :],
                       xfm[ot * 128:(ot + 1) * 128, qh * 512:(qh + 1) * 512],
                       "sxo")
            S.wait("vector", "mt", cm)
            S.wait("vector", "sxo", cl)
            S.wait("vector", "shd", ostate["dma"])
            if ot == 0 and qh == 0:
                S.wait("vector", "ma", g_ma)

            def ev(e, ot=ot, qh=qh, ps=ps):
                e.scalar_tensor_tensor(
                    stg[:, 0:512], ps[:, :],
                    biases[:, 3 * NC + ot:3 * NC + ot + 1],
                    g[:, ot * T + qh * 512:ot * T + qh * 512 + 512],
                    op0=ALU.add, op1=ALU.mult)
                return e.tensor_tensor(x1t[:, :], stg[:, 0:512], x0t[:, :],
                                       ALU.add)
            cv = S.op("vector", "mv", ev)
            ostate["pv"] = cv
            S.wait("scalar", "mv", cv)
            ostate["dma"] = S.dma(
                "scalar", x1d[ot * 128:(ot + 1) * 128, qh * 512:(qh + 1) * 512],
                x1t[:, :], "shd")
            return ("mv", cv)

        S.wait("gpsimd", "mv", att_mv)
        emit_proj(wo0, wr_o0, att_mv, rhs_buf=att)
        x1_shd = S.cnt["shd"]

        def emit_layer1():
            S.wait("sync", "shd", x1_shd)

            def src_x1(c):
                return x1d[c * 128:(c + 1) * 128, :]
            xn1_mv = emit_layernorm(src_x1, BF16)

            vstate = {"dma": 0}

            def wr_v1(ot, qh, ps, cm):
                S.wait("vector", "mt", cm)
                S.wait("vector", "shd", vstate["dma"])
                cv = S.op("vector", "mv", lambda e, qh=qh, ps=ps: e.tensor_copy(
                    qstage[:, qh * 512:(qh + 1) * 512], ps[:, :]))
                if qh == NQH - 1:
                    S.wait("scalar", "mv", cv)
                    vstate["dma"] = S.dma(
                        "scalar", vd1[ot * 128:(ot + 1) * 128, :], qstage[:, :],
                        "shd")
                return ("mv", cv)
            emit_proj(wv1, wr_v1, xn1_mv)

            wstage = pool[:, 16384:24576]
            for wd, dstbuf in ((wq1, att), (wk1, pool)):
                for eq in range(4):
                    S.fence("gpsimd", ("mt",))
                    last = 0
                    for c in range(NC):
                        last = S.dma("gpsimd",
                                     wstage[:, c * 512:(c + 1) * 512],
                                     wd[c * 128:(c + 1) * 128,
                                        eq * 512:(eq + 1) * 512], "swa")
                    S.wait("tensor", "swa", last)
                    S.wait("tensor", "mv", xn1_mv)
                    for st_ in range(8):
                        ptag = f"PA{st_ % 2}"
                        ps = PA[st_ % 2]
                        items = []
                        for c in range(NC):
                            lh = xn[:, c * T + st_ * 128:c * T + st_ * 128 + 128]
                            rh = wstage[:, c * 512:(c + 1) * 512]
                            items.append((ps[:, :], lh, rh))
                        cm = S.mm(ptag, items)
                        S.wait("vector", "mt", cm)
                        cv = S.op("vector", "mv",
                                  lambda e, st_=st_, eq=eq, ps=ps, dstbuf=dstbuf:
                                  e.tensor_copy(
                                      dstbuf[:, st_ * 2048 + eq * 512:
                                             st_ * 2048 + eq * 512 + 512],
                                      ps[:, :]))
                        S.setg(ptag, "mv", cv)
            qk_mv = S.cnt["mv"]

            S.wait("tensor", "mv", qk_mv)
            s1state = {"dma": 0}
            for et in range(NC):
                for fq in range(4):
                    ptag = f"PA{(et * 4 + fq) % 2}"
                    ps = PA[(et * 4 + fq) % 2]
                    items = []
                    for st_ in range(8):
                        lh = att[:, st_ * 2048 + et * 128:st_ * 2048 + et * 128 + 128]
                        rh = pool[:, st_ * 2048 + fq * 512:st_ * 2048 + fq * 512 + 512]
                        items.append((ps[:, :], lh, rh))
                    cm = S.mm(ptag, items)
                    S.wait("vector", "mt", cm)
                    S.wait("vector", "shd", s1state["dma"])
                    cv = S.op("vector", "mv", lambda e, fq=fq, ps=ps:
                              e.tensor_copy(denb[0:128, fq * 256:fq * 256 + 512]
                                            if False else
                                            fbuf[fq % 2][:, 0:512], ps[:, :]))
                    S.setg(ptag, "mv", cv)
                    S.wait("scalar", "mv", cv)
                    s1state["dma"] = S.dma(
                        "scalar",
                        s1loc[et * 128:(et + 1) * 128, fq * 512:(fq + 1) * 512],
                        fbuf[fq % 2][:, 0:512], "shd")
            s1_shd = S.cnt["shd"]

            S.wait("gpsimd", "shd", s1_shd)
            cc_ar = S.op("gpsimd", "scc", lambda e: e.collective_compute(
                "AllReduce", ALU.add, replica_groups=groups,
                ins=[s1loc.ap().opt()], outs=[s1sum.ap().opt()]))

            acc4 = col[:, 3:4]
            acc5 = col[:, 4:5]
            t4, n4 = col[:, 14:15], col[:, 15:16]
            t5, n5 = col[:, 16:17], col[:, 17:18]
            emit_wstats(w1t1, D, H1, acc4)
            finalize_gamma(acc4, D * H1, t4, n4, RS[:, 24:28], "l1")
            g1row = RS[:, 25:26]
            thr4_mv = S.cnt["mv"]
            emit_wstats(w2t1, H1, D, acc5)
            finalize_gamma(acc5, H1 * D, t5, n5, RS[:, 28:32], "l1")
            g2row = RS[:, 29:30]
            thr5_mv = S.cnt["mv"]

            for tg in ("PS0", "PS1"):
                S.guard(tg)
            S.wait("vector", "mv", S.cnt["mv"])
            S.opq("vector", lambda e: e.memset(denb[:, :], 0.0))
            last_mm = 0
            cv = 0
            for c in range(NC):
                def absmax1(e, c=c):
                    e.tensor_scalar_mul(stg[:, 1024:2048],
                                        xn[:, c * T:(c + 1) * T], -1.0)
                    e.tensor_tensor(denb[:, :], denb[:, :],
                                    stg[:, 1024:2048], ALU.max)
                    return e.tensor_tensor(denb[:, :], denb[:, :],
                                           xn[:, c * T:(c + 1) * T], ALU.max)
                cv = S.op("vector", "mv", absmax1)
                cs = S.op("vector", "mv", lambda e, c=c: e.tensor_tensor(
                    stg[:, 0:1024], xn[:, c * T:(c + 1) * T],
                    xn[:, c * T:(c + 1) * T], ALU.mult))
                S.wait("tensor", "mv", cs)
                S.wait("tensor", "mt", last_mm) if False else None

                def mmq(e, first=(c == 0), stop=(c == NC - 1)):
                    e.matmul(PS[0][:, :], ones_b[:, 0:1], stg[:, 0:512],
                             start=first, stop=stop, skip_group_check=True)
                    return e.matmul(PS[1][:, :], ones_b[:, 0:1], stg[:, 512:1024],
                                    start=first, stop=stop, skip_group_check=True)
                last_mm = S.op("tensor", "mt", mmq)
                S.wait("vector", "mt", last_mm)
            S.wait("vector", "mv", cv)
            cv = S.op("vector", "mv", lambda e: e.tensor_copy(
                stg[:, 0:1024], denb[:, :]))
            S.wait("tensor", "mv", cv)
            for j in range(8):
                S.guard(f"PT{j % 2}")
                cm = S.op("tensor", "mt", lambda e, j=j: e.transpose(
                    PT[j % 2][:, 0:128], stg[:, j * 128:(j + 1) * 128],
                    ident[:, :]))
                S.wait("vector", "mt", cm)
                cv = S.op("vector", "mv", lambda e, j=j: e.reduce_max(
                    mxacc[:, j:j + 1], PT[j % 2][:, 0:128], axis=AX.X))
                S.setg(f"PT{j % 2}", "mv", cv)
            S.wait("sync", "mv", cv)
            gat = 0
            for j in range(8):
                gat = S.dma("sync", R3[:, j * 128:(j + 1) * 128].rearrange(
                    "a b -> b a"), mxacc[:, j:j + 1], "sg8") if False else gat
            for j in range(8):
                gat = S.dma("sync", R3[:, j * 128:(j + 1) * 128],
                            mxacc[:, j:j + 1], "sg8")
            S.wait("vector", "sg8", gat)
            S.wait("vector", "mt", last_mm)
            cv = S.op("vector", "mv", lambda e: e.tensor_scalar(
                R0[:, 0:512], PS[0][:, :], 1.0 / D, 1e-5, ALU.mult, ALU.add))
            cv = S.op("vector", "mv", lambda e: e.tensor_scalar(
                R0[:, 512:1024], PS[1][:, :], 1.0 / D, 1e-5, ALU.mult, ALU.add))
            S.setg("PS0", "mv", cv); S.setg("PS1", "mv", cv)
            S.wait("scalar", "mv", cv)
            ca = S.op("scalar", "ma", lambda e: e.activation(
                R0[:, :], R0[:, :], AF.Sqrt))
            S.wait("vector", "ma", ca)
            cv = S.op("vector", "mv", lambda e: e.reciprocal(R0[:, :], R0[:, :]))
            S.wait("vector", "mv", cv)
            cv = S.op("vector", "mv", lambda e: e.tensor_tensor(
                R3[:, :], R3[:, :], R0[:, :], ALU.mult))
            S.wait("vector", "mv", cv)
            cv = S.op("vector", "mv", lambda e: e.tensor_scalar_max(
                R3[:, :], R3[:, :], 1e-5))
            S.wait("vector", "mv", cv)
            cv = S.op("vector", "mv", lambda e: e.reciprocal(R1[:, :], R3[:, :]))
            S.wait("vector", "mv", cv)

            def mkalpha(e):
                e.tensor_tensor(R2[:, :], R0[:, :], R1[:, :], ALU.mult)
                return e.tensor_scalar_mul(R2[:, :], R2[:, :], 127.0)
            cv = S.op("vector", "mv", mkalpha)
            S.wait("vector", "mv", thr4_mv)
            S.wait("vector", "mv", cv)
            cv = S.op("vector", "mv", lambda e: e.tensor_single_scalar(
                R1[:, :], R3[:, :], g1row, ALU.mult))
            S.wait("vector", "mv", cv)
            cv = S.op("vector", "mv", lambda e: e.tensor_scalar_mul(
                R1[:, :], R1[:, :], 1.0 / 127.0))
            S.wait("vector", "mv", cv)
            bcast(denb[:, 0:512], R2[:, 0:512], "bc0")
            bcast(denb[:, 512:1024], R2[:, 512:1024], "bc0")
            ba_mv = S.cnt["mv"]
            S.wait("vector", "mv", ba_mv)
            for c in range(NC):
                def qa(e, c=c):
                    e.tensor_tensor(fbuf[0][:, 0:1024], xn[:, c * T:(c + 1) * T],
                                    denb[:, :], ALU.mult)
                    e.tensor_copy(stgi[:, 0:512], fbuf[0][:, 0:512])
                    e.tensor_copy(xn[:, c * T:c * T + 512], stgi[:, 0:512])
                    e.tensor_copy(stgi[:, 0:512], fbuf[0][:, 512:1024])
                    return e.tensor_copy(xn[:, c * T + 512:c * T + 1024],
                                         stgi[:, 0:512])
                cv = S.op("vector", "mv", qa)
            xq1_mv = S.cnt["mv"]

            for tg in ("PS0", "PS1", "PV0", "PV1"):
                S.guard(tg)
            S.wait("vector", "mv", xq1_mv)
            S.opq("vector", lambda e: e.memset(stg[:, 1024:2048], 0.0))
            bcast(denb[:, 0:512], R1[:, 0:512], "bc0")
            bcast(denb[:, 512:1024], R1[:, 512:1024], "bc0")
            bb_mv = S.cnt["mv"]
            NG1 = H1 // 512
            hdma = 0
            wcol_free = 0
            stat_mm = 0
            for gi in range(NG1):
                S.fence("sync", ("mv", "ma", "mt"))
                prev = [0, 0]
                ma_gate = S.cnt.get("ma", 0)
                for c in range(NC):
                    b = c % 2
                    sem = "sqa" if b == 0 else "sqb"
                    S.wait("sync", "mv", prev[b])
                    cl = S.dma("sync", fbuf[b][:, 0:512],
                               w1t1[c * 128:(c + 1) * 128,
                                    gi * 512:(gi + 1) * 512], sem)
                    S.wait("vector", sem, cl)
                    S.wait("vector", "mv", thr4_mv)
                    S.wait("vector", "ma", ma_gate if c == 0 else 0)
                    S.wait("vector", "mt", wcol_free)

                    def qf(e, b=b, c=c):
                        return ternary(e, wcolq[:, c * 512:(c + 1) * 512],
                                       fbuf[b][:, 0:512], 512, t4, n4)
                    prev[b] = S.op("vector", "mv", qf)
                q_mv = S.cnt["mv"]
                S.wait("tensor", "mv", q_mv)
                for htl in range(4):
                    ht = gi * 4 + htl
                    for qh in range(NQH):
                        ptag = f"PA{(htl * NQH + qh) % 2}"
                        ps = PA[(htl * NQH + qh) % 2]
                        items = []
                        for c in range(NC):
                            lh = wcolq[:, c * 512 + htl * 128:
                                       c * 512 + htl * 128 + 128]
                            rh = xn[:, c * T + qh * 512:c * T + qh * 512 + 512]
                            items.append((ps[:, :], lh, rh))
                        cm = S.mm(ptag, items)
                        S.wait("vector", "mt", cm)
                        S.wait("vector", "ma", S.cnt.get("ma", 0))
                        S.wait("vector", "mv", bb_mv)
                        sl5 = slice(qh * 512, qh * 512 + 512)
                        cv = S.op("vector", "mv", lambda e, ps=ps, sl5=sl5:
                                  e.tensor_tensor(fbuf[0][:, 0:512], ps[:, :],
                                                  denb[:, sl5], ALU.mult))
                        S.setg(ptag, "mv", cv)
                        S.wait("scalar", "mv", cv)
                        S.wait("scalar", "shd", hdma)
                        ca = S.op("scalar", "ma", lambda e, ht=ht: e.activation(
                            stg[:, 0:512], fbuf[0][:, 0:512], AF.Silu,
                            bias=bias1[:, ht // 2:ht // 2 + 1]
                            if False else bias1[:, ht:ht + 1] if ht < 64 else None))
                        ca = S.op("scalar", "ma", lambda e: e.activation(
                            stg[:, 512:1024], stg[:, 0:512], AF.Gelu))
                        S.wait("vector", "ma", ca)
                        S.wait("vector", "mt", stat_mm)

                        def stt_(e, sl5=sl5):
                            return e.tensor_tensor(fbuf[0][:, 0:512],
                                                   stg[:, 512:1024],
                                                   stg[:, 512:1024], ALU.mult)
                        cv = S.op("vector", "mv", stt_)
                        S.wait("tensor", "mv", cv)

                        def smm_(e, first=(ht == 0),
                                 stop=(ht == H1 // 128 - 1),
                                 qh=qh):
                            e.matmul(PS[qh][:, :], ones_b[:, 0:1],
                                     stg[:, 512:1024], start=first, stop=stop,
                                     skip_group_check=True)
                            return e.matmul(PV[qh][0:1, :], ones_f[:, 0:1],
                                            fbuf[0][:, 0:512], start=first,
                                            stop=stop, skip_group_check=True)
                        stat_mm = S.op("tensor", "mt", smm_)
                        S.wait("scalar", "mt", stat_mm)
                        hdma = S.dma("scalar",
                                     hid1[ht * 128:(ht + 1) * 128,
                                          qh * 512:(qh + 1) * 512],
                                     stg[:, 512:1024], "shd")
                wcol_free = S.cnt["mt"]
            S.wait("vector", "mt", stat_mm)
            if taps:
                S.wait("gpsimd", "shd", hdma)
                hdbg_c = S.dma("gpsimd", hdbg.ap(), hid1.ap(), "sstg")
                S.wait("scalar", "sstg", hdbg_c)

                def rawdump(e):
                    e.tensor_copy(R2[:, 0:512], PS[0][:, :])
                    e.tensor_copy(R2[:, 512:1024], PS[1][:, :])
                    e.tensor_copy(R3[:, 0:512], PV[0][0:1, :])
                    return e.tensor_copy(R3[:, 512:1024], PV[1][0:1, :])
                cvd = S.op("vector", "mv", rawdump)
                S.wait("scalar", "mv", cvd)
                cdd = S.dma("scalar", rdbg.ap(), rows[:, :], "shd")
                S.wait("vector", "shd", cdd)

            cv = S.op("vector", "mv", lambda e: e.tensor_scalar(
                R0[:, 0:512], PS[0][:, :], 1.0 / H1, None, ALU.mult))
            cv = S.op("vector", "mv", lambda e: e.tensor_scalar(
                R0[:, 512:1024], PS[1][:, :], 1.0 / H1, None, ALU.mult))
            cv = S.op("vector", "mv", lambda e: e.tensor_scalar(
                R1[:, 0:512], PV[0][0:1, :], 1.0 / H1, None, ALU.mult))
            cv = S.op("vector", "mv", lambda e: e.tensor_scalar(
                R1[:, 512:1024], PV[1][0:1, :], 1.0 / H1, None, ALU.mult))
            for tg in ("PS0", "PS1", "PV0", "PV1"):
                S.setg(tg, "mv", cv)
            S.wait("vector", "mv", cv)
            cv = S.op("vector", "mv", lambda e: e.tensor_tensor(
                R2[:, :], R0[:, :], R0[:, :], ALU.mult))
            S.wait("vector", "mv", cv)
            cv = S.op("vector", "mv", lambda e: e.tensor_tensor(
                R1[:, :], R1[:, :], R2[:, :], ALU.subtract))
            S.wait("scalar", "mv", cv)
            ca = S.op("scalar", "ma", lambda e: e.activation(
                R1[:, :], R1[:, :], AF.Sqrt, bias=eps5[0:1, :]))
            S.wait("vector", "ma", ca)
            cv = S.op("vector", "mv", lambda e: e.reciprocal(R1[:, :], R1[:, :]))
            mu_mv = S.cnt["mv"]

            S.wait("vector", "mv", mu_mv)
            bcast(denb[:, 0:512], R0[:, 0:512], "bc0")
            bcast(denb[:, 512:1024], R0[:, 512:1024], "bc0")
            bcast(fbuf[1][:, 0:512], R1[:, 0:512], "bc1", psum=PV[1])
            bcast(fbuf[1][:, 512:1024], R1[:, 512:1024], "bc1", psum=PV[1])
            bmu_mv = S.cnt["mv"]
            S.opq("vector", lambda e: e.memset(stg[:, 1024:2048], 0.0))
            for tg in ("PS0", "PS1"):
                S.guard(tg)
            h2_mm = 0
            h2dma = 0
            for hc in range(H1 // 128):
                cl = S.dma("scalar", stg[:, 0:1024],
                           hid1[hc * 128:(hc + 1) * 128, :], "shd")
                S.wait("vector", "shd", cl)
                S.wait("vector", "mv", bmu_mv)
                S.wait("vector", "mt", h2_mm)

                def lnap(e, hc=hc):
                    e.tensor_tensor(fbuf[0][:, 0:1024], stg[:, 0:1024],
                                    denb[:, :], ALU.subtract)
                    e.tensor_tensor(fbuf[0][:, 0:1024], fbuf[0][:, 0:1024],
                                    fbuf[1][:, 0:1024], ALU.mult)
                    return e.tensor_scalar(
                        stg[:, 0:1024], fbuf[0][:, 0:1024],
                        bias1[:, 64 + hc:64 + hc + 1],
                        bias1[:, 128 + hc:128 + hc + 1], ALU.mult, ALU.add)
                cv = S.op("vector", "mv", lnap)
                S.wait("vector", "mv", cv)

                def st2(e):
                    e.tensor_scalar_mul(stg[:, 2048:3072], stg[:, 0:1024],
                                        -1.0)
                    e.tensor_tensor(stg[:, 1024:2048], stg[:, 1024:2048],
                                    stg[:, 2048:3072], ALU.max)
                    e.tensor_tensor(stg[:, 1024:2048], stg[:, 1024:2048],
                                    stg[:, 0:1024], ALU.max)
                    return e.tensor_tensor(fbuf[0][:, 0:1024], stg[:, 0:1024],
                                           stg[:, 0:1024], ALU.mult)
                cv = S.op("vector", "mv", st2)
                S.wait("tensor", "mv", cv)

                def smm2(e, first=(hc == 0), stop=(hc == H1 // 128 - 1)):
                    e.matmul(PS[0][:, :], ones_f[:, 0:1], fbuf[0][:, 0:512],
                             start=first, stop=stop, skip_group_check=True)
                    return e.matmul(PS[1][:, :], ones_f[:, 0:1],
                                    fbuf[0][:, 512:1024], start=first,
                                    stop=stop, skip_group_check=True)
                h2_mm = S.op("tensor", "mt", smm2)
                S.wait("scalar", "mv", cv)
                S.wait("scalar", "mt", h2_mm)
                h2dma = S.dma("scalar", hid1[hc * 128:(hc + 1) * 128, :],
                              stg[:, 0:1024], "shd")
            S.wait("vector", "mt", h2_mm)
            S.wait("vector", "shd", h2dma)
            cv = S.op("vector", "mv", lambda e: e.tensor_copy(
                stg[:, 0:1024], stg[:, 1024:2048]))
            S.wait("tensor", "mv", cv)
            for j in range(8):
                S.guard(f"PT{j % 2}")
                cm = S.op("tensor", "mt", lambda e, j=j: e.transpose(
                    PT[j % 2][:, 0:128], stg[:, j * 128:(j + 1) * 128],
                    ident[:, :]))
                S.wait("vector", "mt", cm)
                cv = S.op("vector", "mv", lambda e, j=j: e.reduce_max(
                    mxacc[:, j:j + 1], PT[j % 2][:, 0:128], axis=AX.X))
                S.setg(f"PT{j % 2}", "mv", cv)
            S.wait("sync", "mv", cv)
            for j in range(8):
                gat = S.dma("sync", R3[:, j * 128:(j + 1) * 128],
                            mxacc[:, j:j + 1], "sg8")
            S.wait("vector", "sg8", gat)
            cv = S.op("vector", "mv", lambda e: e.tensor_scalar(
                R0[:, 0:512], PS[0][:, :], 1.0 / H1, 1e-5, ALU.mult, ALU.add))
            cv = S.op("vector", "mv", lambda e: e.tensor_scalar(
                R0[:, 512:1024], PS[1][:, :], 1.0 / H1, 1e-5, ALU.mult, ALU.add))
            S.setg("PS0", "mv", cv); S.setg("PS1", "mv", cv)
            S.wait("scalar", "mv", cv)
            ca = S.op("scalar", "ma", lambda e: e.activation(
                R0[:, :], R0[:, :], AF.Sqrt))
            S.wait("vector", "ma", ca)
            cv = S.op("vector", "mv", lambda e: e.reciprocal(R0[:, :], R0[:, :]))
            S.wait("vector", "mv", cv)
            cv = S.op("vector", "mv", lambda e: e.tensor_tensor(
                R3[:, :], R3[:, :], R0[:, :], ALU.mult))
            S.wait("vector", "mv", cv)
            cv = S.op("vector", "mv", lambda e: e.tensor_scalar_max(
                R3[:, :], R3[:, :], 1e-5))
            S.wait("vector", "mv", cv)
            cv = S.op("vector", "mv", lambda e: e.reciprocal(R1[:, :], R3[:, :]))
            S.wait("vector", "mv", cv)

            def mkalpha2(e):
                e.tensor_tensor(R2[:, :], R0[:, :], R1[:, :], ALU.mult)
                return e.tensor_scalar_mul(R2[:, :], R2[:, :], 127.0)
            cv = S.op("vector", "mv", mkalpha2)
            S.wait("vector", "mv", thr5_mv)
            S.wait("vector", "mv", cv)
            cv = S.op("vector", "mv", lambda e: e.tensor_single_scalar(
                R1[:, :], R3[:, :], g2row, ALU.mult))
            S.wait("vector", "mv", cv)
            cv = S.op("vector", "mv", lambda e: e.tensor_scalar_mul(
                R1[:, :], R1[:, :], 1.0 / 127.0))
            S.wait("vector", "mv", cv)
            bcast(denb[:, 0:512], R2[:, 0:512], "bc0")
            bcast(denb[:, 512:1024], R2[:, 512:1024], "bc0")
            ba2_mv = S.cnt["mv"]

            xq2b = pool[:, 0:8192]
            qw2s = att[:, 0:16384]
            sda_last = {0: 0, 1: 0}
            used_mt = {0: 0, 1: 0}
            pst = {"prev_qa": 0, "prevq": [0, 0]}
            q_mv_of = {}
            S.fence("sync", ("mv", "ma", "mt"))
            S.sem("sda")

            def emit_quant_half(hs):
                grp = hs % 2
                for jj in range(4):
                    j = grp * 4 + jj
                    hc = hs * 4 + jj
                    S.wait("scalar", "mv", pst["prev_qa"])
                    cl = S.dma("scalar", stg[:, 0:1024],
                               hid1[hc * 128:(hc + 1) * 128, :], "shd")
                    S.wait("vector", "shd", cl)
                    S.wait("vector", "mv", ba2_mv)
                    S.wait("vector", "mt", used_mt[grp] if jj == 0 else 0)

                    def qa2(e, j=j):
                        e.tensor_tensor(x0t[:, :], stg[:, 0:512],
                                        denb[:, 0:512], ALU.mult)
                        e.tensor_copy(stgi[:, 0:512], x0t[:, :])
                        e.tensor_copy(xq2b[:, j * 1024:j * 1024 + 512],
                                      stgi[:, 0:512])
                        e.tensor_tensor(x0t[:, :], stg[:, 512:1024],
                                        denb[:, 512:1024], ALU.mult)
                        e.tensor_copy(stgi[:, 0:512], x0t[:, :])
                        return e.tensor_copy(
                            xq2b[:, j * 1024 + 512:(j + 1) * 1024],
                            stgi[:, 0:512])
                    pst["prev_qa"] = S.op("vector", "mv", qa2)
                    b = j % 2
                    sem = "sqa" if b == 0 else "sqb"
                    S.wait("sync", "mv", pst["prevq"][b])
                    cl = S.dma("sync", fbuf[b][:, 0:2048],
                               w2t1[hc * 128:(hc + 1) * 128, :], sem)
                    S.wait("vector", sem, cl)
                    S.wait("vector", "mv", thr5_mv)

                    def qf2a(e, b=b, j=j):
                        t1 = stg[:, 1024:2048]
                        t2 = stg[:, 2048:3072]
                        e.tensor_single_scalar(t1, fbuf[b][:, 0:1024], t5,
                                               ALU.is_gt)
                        e.tensor_single_scalar(t2, fbuf[b][:, 0:1024], n5,
                                               ALU.is_lt)
                        e.tensor_tensor(qw2s[:, j * 2048:j * 2048 + 1024], t1,
                                        t2, ALU.subtract)
                        e.tensor_single_scalar(t1, fbuf[b][:, 1024:2048], t5,
                                               ALU.is_gt)
                        e.tensor_single_scalar(t2, fbuf[b][:, 1024:2048], n5,
                                               ALU.is_lt)
                        return e.tensor_tensor(
                            qw2s[:, j * 2048 + 1024:(j + 1) * 2048], t1, t2,
                            ALU.subtract)
                    pst["prevq"][b] = S.op("vector", "mv", qf2a)
                q_mv_of[hs] = S.cnt["mv"]

            emit_quant_half(0)
            for hs in range(16):
                if hs + 1 < 16:
                    emit_quant_half(hs + 1)
                grp = hs % 2
                S.wait("tensor", "mv", q_mv_of[hs])
                sda_before = S.cnt.get("sda", 0)
                for dt_ in range(NC):
                    for qh in range(NQH):
                        ptag = f"PA{(dt_ * NQH + qh) % 2}"
                        ps = PA[(dt_ * NQH + qh) % 2]
                        items = []
                        for j in range(grp * 4, grp * 4 + 4):
                            lh = qw2s[:, j * 2048 + dt_ * 128:
                                      j * 2048 + dt_ * 128 + 128]
                            rh = xq2b[:, j * 1024 + qh * 512:
                                      j * 1024 + qh * 512 + 512]
                            items.append((ps[:, :], lh, rh))
                        cm = S.mm(ptag, items)
                        gb = (dt_ * NQH + qh) % 2
                        gt = gstg0 if gb == 0 else yst
                        S.wait("scalar", "mt", cm)
                        S.wait("scalar", "sda", sda_last[gb])
                        ca = S.op("scalar", "ma", lambda e, ps=ps, gt=gt:
                                  e.activation(gt[:, :], ps[:, :], AF.Copy))
                        S.setg(ptag, "ma", ca)
                        S.wait("gpsimd", "ma", ca)
                        if hs > 0:
                            S.wait("gpsimd", "sda", sda_before)

                        def acd(e, dt_=dt_, qh=qh, gt=gt, hs=hs):
                            return e.dma_start(
                                out=g2acc[dt_ * 128:(dt_ + 1) * 128,
                                          qh * 512:(qh + 1) * 512],
                                in_=gt[:, :],
                                accum_op=(ALU.bypass if hs == 0 else ALU.add))
                        h = S.sem("sda")
                        cda = S.bump("sda", 16)
                        S.on("gpsimd", lambda e, f=acd, h=h: f(e).then_inc(h, 16))
                        sda_last[gb] = cda
                used_mt[grp] = S.cnt["mt"]
            S.wait("vector", "mv", S.cnt["mv"])
            bcast(denb[:, 0:512], R1[:, 0:512], "bc0")
            bcast(denb[:, 512:1024], R1[:, 512:1024], "bc0")
            bb2_mv = S.cnt["mv"]
            S.wait("sync", "sda", S.cnt.get("sda", 0))
            prev_ld = 0
            for c in range(NC):
                for qh in range(NQH):
                    sl = slice(c * T + qh * 512, c * T + qh * 512 + 512)
                    S.wait("sync", "mv", prev_ld)
                    cl = S.dma("sync", yst[:, :],
                               g2acc[c * 128:(c + 1) * 128,
                                     qh * 512:(qh + 1) * 512], "sy")
                    S.wait("vector", "sy", cl)
                    S.wait("vector", "mv", bb2_mv)
                    cv = S.op("vector", "mv", lambda e, sl=sl, qh=qh:
                              e.tensor_tensor(fbuf[0][:, 0:512], yst[:, :],
                                              denb[:, qh * 512:qh * 512 + 512],
                                              ALU.mult))
                    prev_ld = cv
                    S.wait("scalar", "mv", cv)
                    ca = S.op("scalar", "ma", lambda e, sl=sl, c=c: e.activation(
                        g[:, sl], fbuf[0][:, 0:512], AF.Silu,
                        bias=biases[:, 4 * NC + c:4 * NC + c + 1]))
                    S.wait("vector", "ma", ca)
            g1_ma = S.cnt["ma"]

            S.wait("scalar", "scc", cc_ar)
            S.wait("scalar", "ma", g1_ma)
            S.fence("scalar", ("mv", "mt"))
            vload = 0
            for c in range(NC):
                vload = S.dma("scalar", pool[:, c * T:(c + 1) * T],
                              vd1[c * 128:(c + 1) * 128, :], "shd")
            att_done = 0
            eprev_mm = 0
            for ec in range(NC):
                b = ec % 2
                sem = "sqa" if b == 0 else "sqb"
                if ec == 0:
                    S.wait("sync", "scc", cc_ar)
                S.wait("sync", "mv", S.cnt["mv"] if ec < 2 else 0)
                S.wait("sync", "mt", eprev_mm)
                cl = S.dma("sync", fbuf[b][:, 0:2048],
                           s1sum[ec * 128:(ec + 1) * 128, :], sem)
                S.wait("vector", sem, cl)
                S.wait("vector", "ma", S.cnt.get("ma", 0))
                cv = S.op("vector", "mv", lambda e, b=b: e.reduce_max(
                    col[:, 20:21], fbuf[b][:, 0:2048], axis=AX.X))
                S.wait("vector", "mv", cv)
                cv = S.op("vector", "mv", lambda e: e.tensor_scalar_mul(
                    col[:, 21:22], col[:, 20:21], -0.125))
                S.wait("scalar", "mv", cv)
                S.wait("scalar", "mt", eprev_mm)
                ca = S.op("scalar", "ma", lambda e, b=b: e.activation(
                    stg[:, 0:2048], fbuf[b][:, 0:2048], AF.Exp,
                    bias=col[:, 21:22], scale=0.125,
                    accum_out=col[:, 22:23]))
                S.wait("vector", "ma", ca)
                cv = S.op("vector", "mv", lambda e: e.reciprocal(
                    col[:, 23:24], col[:, 22:23]))
                S.wait("vector", "mv", cv)
                S.wait("vector", "mt", eprev_mm)
                cv = S.op("vector", "mv", lambda e: e.tensor_scalar_mul(
                    stg[:, 2048:4096], stg[:, 0:2048], col[:, 23:24]))
                if taps:
                    S.wait("scalar", "mv", cv)
                    S.dma("scalar", wdbg[ec * 128:(ec + 1) * 128, :],
                          stg[:, 2048:4096], "shd")
                S.wait("tensor", "mv", cv)
                avh = {}
                for fc in range(NC):
                    S.guard(f"PT{fc % 2}")
                    cm = S.op("tensor", "mt", lambda e, fc=fc: e.transpose(
                        PT[fc % 2][:, 0:128],
                        stg[:, 2048 + fc * 128:2048 + fc * 128 + 128],
                        ident[:, :]))
                    S.wait("vector", "mt", cm)
                    cv = S.op("vector", "mv", lambda e, fc=fc: e.tensor_copy(
                        stg[:, fc * 128:(fc + 1) * 128], PT[fc % 2][:, 0:128]))
                    S.setg(f"PT{fc % 2}", "mv", cv)
                S.wait("tensor", "mv", cv)
                if ec == 0:
                    S.wait("tensor", "shd", vload)

                def avmm(e, ec=ec):
                    ins = None
                    for qh in range(NQH):
                        for fc in range(NC):
                            ins = e.matmul(
                                PV[qh][:, :], stg[:, fc * 128:(fc + 1) * 128],
                                pool[:, fc * T + qh * 512:fc * T + qh * 512 + 512],
                                start=(fc == 0), stop=(fc == NC - 1),
                                skip_group_check=True)
                    return ins
                S.guard("PV0"); S.guard("PV1")
                cm = S.op("tensor", "mt", avmm)
                eprev_mm = cm
                S.wait("vector", "mt", cm)
                for qh in range(NQH):
                    cv = S.op("vector", "mv", lambda e, ec=ec, qh=qh:
                              e.tensor_copy(
                                  att[:, ec * T + qh * 512:ec * T + qh * 512 + 512],
                                  PV[qh][:, :]))
                S.setg("PV0", "mv", cv); S.setg("PV1", "mv", cv)
            attn1_mv = S.cnt["mv"]

            o1state = {"dma": 0, "pv": 0, "st": 0}

            def wr_o1(ot, qh, ps, cm):
                S.wait("sync", "mv", o1state["pv"])
                cl = S.dma("sync", x1t[:, :],
                           x1d[ot * 128:(ot + 1) * 128,
                               qh * 512:(qh + 1) * 512], "sxo")
                S.wait("vector", "mt", cm)
                S.wait("vector", "sxo", cl)
                S.wait("vector", "sst", o1state["st"])

                def ev(e, ot=ot, qh=qh, ps=ps):
                    e.tensor_tensor(stg[:, 0:512],
                                    g[:, ot * T + qh * 512:ot * T + qh * 512 + 512],
                                    ps[:, :], ALU.mult)
                    return e.tensor_tensor(yst[:, :], stg[:, 0:512], x1t[:, :],
                                           ALU.add)
                cv = S.op("vector", "mv", ev)
                o1state["pv"] = cv
                S.wait("sync", "mv", cv)
                o1state["st"] = S.dma(
                    "sync", y[ot * 128:(ot + 1) * 128, qh * 512:(qh + 1) * 512],
                    yst[:, :], "sst")
                return ("mv", cv)

            S.wait("gpsimd", "mv", attn1_mv)
            emit_proj(wo1, wr_o1, attn1_mv, rhs_buf=att)

        def emit_taps():
            for tp in taps:
                src = {"x1d": x1d, "qd": qd, "k_full": k_full,
                       "v_full": v_full, "s1sum": s1sum, "s1loc": s1loc,
                       "hid0": hid0, "hid1": hid1, "vd1": vd1,
                       "wdbg": wdbg, "rdbg": rdbg, "hdbg": hdbg,
                       "xdbg": xdbg, "qdbg": qdbg, "g2d": g2acc,
                       "g2s01": g2s01, "pdbg": pdbg}.get(tp)
                for sname in ("shd", "sxa", "sxb", "mv", "ma", "mt"):
                    S.wait("gpsimd", sname, S.cnt.get(sname, 0))
                if src is not None:
                    S.dma("gpsimd", tap_t[tp].ap(), src.ap(), "sstg")
                elif tp == "cols_t":
                    S.dma("gpsimd", tap_t[tp].ap(), col[:, :], "sstg")
                elif tp == "rows_t":
                    S.dma("gpsimd", tap_t[tp].ap(), rows[:, :], "sstg")
                else:
                    buf = {"xn_t": xn, "g_t": g, "att_t": att}[tp]
                    for c in range(NC):
                        S.dma("gpsimd", tap_t[tp][c * 128:(c + 1) * 128, :],
                              buf[:, c * T:(c + 1) * T], "sstg")

        def emit_y_from(dram):
            prev_cv, prev_st = 0, 0
            for ot in range(NC):
                for qh in range(NQH):
                    S.wait("sync", "mv", prev_cv)
                    cl = S.dma("sync", x1t[:, :],
                               dram[ot * 128:(ot + 1) * 128,
                                    qh * 512:(qh + 1) * 512], "sy")
                    S.wait("vector", "sy", cl)
                    S.wait("vector", "sst", prev_st)
                    prev_cv = S.op("vector", "mv", lambda e:
                                   e.tensor_copy(yst[:, :], x1t[:, :]))
                    S.wait("sync", "mv", prev_cv)
                    prev_st = S.dma("sync",
                                    y[ot * 128:(ot + 1) * 128,
                                      qh * 512:(qh + 1) * 512], yst[:, :],
                                    "sst")

        if only_l0:
            S.wait("sync", "shd", x1_shd)
            emit_y_from(x1d)
        else:
            emit_layer1()
        emit_taps()
        S.wait("gpsimd", "sst", S.cnt.get("sst", 0))
        S.wait("gpsimd", "sstg", S.cnt.get("sstg", 0))
        S.wait("gpsimd", "shd", S.cnt.get("shd", 0))

        with nc.Block() as block:
            @block.sync
            def _(e):
                for f in S.ops["sync"]:
                    f(e)

            @block.gpsimd
            def _(e):
                for f in S.ops["gpsimd"]:
                    f(e)

            @block.vector
            def _(e):
                for f in S.ops["vector"]:
                    f(e)

            @block.scalar
            def _(e):
                for f in S.ops["scalar"]:
                    f(e)

            @block.tensor
            def _(e):
                for f in S.ops["tensor"]:
                    f(e)

    return nc


def _prep_inputs(inp):
    f = np.ascontiguousarray
    a = np.asarray

    def f32(v):
        return np.asarray(v, dtype=np.float32)

    shared = {
        "wq0": f(f32(inp["L0_attn_w_q"])), "wk0": f(f32(inp["L0_attn_w_k"])),
        "wv0": f(f32(inp["L0_attn_w_v"])), "wo0": f(f32(inp["L0_attn_w_o"])),
        "bq0": f(f32(inp["L0_attn_b_q"]).reshape(16, 128).T),
        "bk0": f(f32(inp["L0_attn_b_k"]).reshape(16, 128).T),
        "bv0": f(f32(inp["L0_attn_b_v"]).reshape(16, 128).T),
        "bo0": f(f32(inp["L0_attn_b_o"]).reshape(16, 128).T),
        "w1t0": f(f32(inp["L0_ffn_w1"]).T),
        "w2t0": f(f32(inp["L0_ffn_w2"]).T),
        "w3t0": f(f32(inp["L0_ffn_w3"]).T),
        "wq1": f(f32(inp["L1_attn_Wq"])), "wk1": f(f32(inp["L1_attn_Wk"])),
        "wv1": f(f32(inp["L1_attn_Wv"])), "wo1": f(f32(inp["L1_attn_Wo"])),
        "w1t1": f(f32(inp["L1_ffn_w1"]).T),
        "w2t1": f(f32(inp["L1_ffn_w2"]).T),
        "b11": f(f32(inp["L1_ffn_b1"]).reshape(64, 128).T),
        "lng1": f(f32(inp["L1_ffn_ln_g"]).reshape(64, 128).T),
        "lnb1": f(f32(inp["L1_ffn_ln_b"]).reshape(64, 128).T),
        "b21": f(f32(inp["L1_ffn_b2"]).reshape(16, 128).T),
        "kvec": f((np.arange(128)[:, None] + 128.0 * np.arange(16)[None, :])
                  .astype(np.float32)),
    }
    x = f32(inp["L0_attn_query"])
    in_maps = []
    for core in range(8):
        b, half = core // 2, core % 2
        m = dict(shared)
        m["xfm"] = f(x[b, half * 1024:(half + 1) * 1024, :].T)
        m["qiota"] = (half * 1024.0
                      + np.arange(1024, dtype=np.float32))[None, :]
        in_maps.append(m)
    return in_maps


def kernel(**inputs):
    in_maps = _prep_inputs(inputs)
    nc = build_program(taps=DEBUG_TAPS, only_l0=ONLY_L0)
    res = run_bass_kernel_spmd(nc, in_maps, core_ids=list(range(8)))
    out = np.empty((4, 2048, 2048), dtype=np.float32)
    for core in range(8):
        b, half = core // 2, core % 2
        out[b, half * 1024:(half + 1) * 1024, :] = res.results[core]["y"].T
    kernel.last_results = res
    return out


# version 04
# speedup vs baseline: 1.1606x; 1.0560x over previous
import numpy as np
from contextlib import ExitStack

import concourse.bass as bass
import concourse.mybir as mybir
from concourse.bass_utils import run_bass_kernel_spmd

F32 = mybir.dt.float32
BF16 = mybir.dt.bfloat16
I32 = mybir.dt.int32
ALU = mybir.AluOpType
AF = mybir.ActivationFunctionType
AX = mybir.AxisListType

D = 2048
T = 1024
NQH = 2
NH = 16
H0 = 5632
H1 = 8192
NC = D // 128
RS128 = 0.08838834764831845

ENGINES = ("sync", "gpsimd", "vector", "scalar", "tensor")

DEBUG_TAPS = ()
ONLY_L0 = False


class Sched:
    def __init__(self, nc, stk):
        self.nc = nc
        self.stk = stk
        self.ops = {e: [] for e in ENGINES}
        self.sems = {}
        self.cnt = {}
        self.guards = {}

    def sem(self, name):
        if name not in self.sems:
            self.sems[name] = self.stk.enter_context(self.nc.semaphore(name))
            self.cnt[name] = 0
        return self.sems[name]

    def on(self, eng, fn):
        self.ops[eng].append(fn)

    def wait(self, eng, sem, cnt):
        if cnt is None or cnt <= 0:
            return
        assert cnt <= self.cnt.get(sem, 0), f"wait beyond emitted: {sem} {cnt}"
        h = self.sem(sem)
        self.on(eng, lambda e, h=h, c=cnt: e.wait_ge(h, c))

    def bump(self, sem, n):
        self.sem(sem)
        self.cnt[sem] += n
        return self.cnt[sem]

    def dma(self, eng, out, in_, sem):
        h = self.sem(sem)
        c = self.bump(sem, 16)
        self.on(eng, lambda e, o=out, i=in_, h=h:
                e.dma_start(out=o, in_=i).then_inc(h, 16))
        return c

    def op(self, eng, sem, fn):
        h = self.sem(sem)
        c = self.bump(sem, 1)
        self.on(eng, lambda e, fn=fn, h=h: fn(e).then_inc(h, 1))
        return c

    def opq(self, eng, fn):
        self.on(eng, fn)

    def fence(self, eng, sems=("mv", "mt", "ma")):
        for s in sems:
            self.wait(eng, s, self.cnt.get(s, 0))

    def guard(self, tag):
        g = self.guards.get(tag)
        if g:
            self.wait("tensor", g[0], g[1])
            self.guards[tag] = None

    def setg(self, tag, sem, cnt):
        self.guards[tag] = (sem, cnt)

    def mm(self, tag, items, sem="mt"):
        self.guard(tag)
        h = self.sem(sem)
        c = self.bump(sem, 1)
        items = list(items)

        def fn(e, items=items, h=h):
            n = len(items)
            ins = None
            for i, (ps, lh, rh) in enumerate(items):
                ins = e.matmul(ps, lh, rh, start=(i == 0), stop=(i == n - 1),
                               skip_group_check=True)
            return ins.then_inc(h, 1)
        self.on("tensor", fn)
        return c


def build_program(taps=(), only_l0=False):
    nc = bass.Bass("TRN2", target_bir_lowering=False,
                   detect_race_conditions=False)

    def din(name, shape, dtype=F32):
        return nc.dram_tensor(name, shape, dtype, kind="ExternalInput")

    xfm = din("xfm", [D, T])
    wq0 = din("wq0", [D, D]); wk0 = din("wk0", [D, D])
    wv0 = din("wv0", [D, D]); wo0 = din("wo0", [D, D])
    bq0 = din("bq0", [128, NC]); bk0 = din("bk0", [128, NC])
    bv0 = din("bv0", [128, NC]); bo0 = din("bo0", [128, NC])
    w1t0 = din("w1t0", [D, H0]); w2t0 = din("w2t0", [D, H0])
    w3t0 = din("w3t0", [H0, D])
    wq1 = din("wq1", [D, D]); wk1 = din("wk1", [D, D])
    wv1 = din("wv1", [D, D]); wo1 = din("wo1", [D, D])
    w1t1 = din("w1t1", [D, H1]); w2t1 = din("w2t1", [H1, D])
    b11 = din("b11", [128, 64]); lng1 = din("lng1", [128, 64])
    lnb1 = din("lnb1", [128, 64]); b21 = din("b21", [128, NC])
    qiota = din("qiota", [1, T])
    kvec = din("kvec", [128, NH])

    y = nc.dram_tensor("y", [D, T], F32, kind="ExternalOutput")

    x1d = nc.dram_tensor("x1d", [D, T], BF16)
    qd = nc.dram_tensor("qd", [D, T], BF16)
    k_own = nc.dram_tensor("k_own", [D, T], BF16)
    v_own = nc.dram_tensor("v_own", [D, T], BF16)
    k_full = nc.dram_tensor("k_full", [2 * D, T], BF16)
    v_full = nc.dram_tensor("v_full", [2 * D, T], BF16)
    hid0 = nc.dram_tensor("hid0", [H0, T], BF16)
    hid1 = nc.dram_tensor("hid1", [H1, T], BF16)
    s1loc = nc.dram_tensor("s1loc", [D, D], F32)
    s1sum = nc.dram_tensor("s1sum", [D, D], F32)
    vd1 = nc.dram_tensor("vd1", [D, T], BF16)
    wdbg = nc.dram_tensor("wdbg", [D, D], BF16)
    rdbg = nc.dram_tensor("rdbg", [1, 4224], F32)
    hdbg = nc.dram_tensor("hdbg", [H1, T], BF16)
    g2acc = nc.dram_tensor("g2acc", [D, T], F32)
    xdbg = nc.dram_tensor("xdbg", [128, 8192], BF16)
    g2s01 = nc.dram_tensor("g2s01", [4096, 1024], F32)
    pdbg = nc.dram_tensor("pdbg", [128, 2048], BF16)
    qdbg = nc.dram_tensor("qdbg", [128, 16384], BF16)

    tshape = {"x1d": ([D, T], BF16), "qd": ([D, T], BF16),
              "k_full": ([2 * D, T], BF16), "v_full": ([2 * D, T], BF16),
              "s1loc": ([D, D], F32), "s1sum": ([D, D], F32),
              "hid0": ([H0, T], BF16), "hid1": ([H1, T], BF16),
              "vd1": ([D, T], BF16), "xn_t": ([D, T], BF16),
              "g_t": ([D, T], BF16), "att_t": ([D, T], BF16),
              "cols_t": ([128, 32], F32), "rows_t": ([1, 4224], F32),
              "wdbg": ([D, D], BF16), "rdbg": ([1, 4224], F32),
              "hdbg": ([H1, T], BF16), "xdbg": ([128, 8192], BF16),
              "g2d": ([D, T], F32), "g2s01": ([4096, 1024], F32),
              "pdbg": ([128, 2048], BF16),
              "qdbg": ([128, 16384], BF16)}
    tap_t = {tp: nc.dram_tensor("tap_" + tp, *tshape[tp], kind="ExternalOutput")
             for tp in taps}

    groups = [[0, 1], [2, 3], [4, 5], [6, 7]]

    with ExitStack() as stk:
        e_ = stk.enter_context
        S = Sched(nc, stk)

        xn = e_(nc.sbuf_tensor("xn", [128, NC * T], BF16))
        g = e_(nc.sbuf_tensor("g", [128, NC * T], BF16))
        att = e_(nc.sbuf_tensor("att", [128, NC * T], BF16))
        pool = e_(nc.sbuf_tensor("pool", [128, 24576], BF16))
        fbuf = [e_(nc.sbuf_tensor(f"fbuf{i}", [128, 2048], F32)) for i in range(2)]
        Bq = e_(nc.sbuf_tensor("Bq", [128, T], F32))
        stg = e_(nc.sbuf_tensor("stg", [128, 4096], BF16))
        stgi = e_(nc.sbuf_tensor("stgi", [128, 512], I32))
        rows = e_(nc.sbuf_tensor("rows", [1, 4224], F32))
        biases = e_(nc.sbuf_tensor("biases", [128, 5 * NC], F32))
        bias1 = e_(nc.sbuf_tensor("bias1", [128, 192], F32))
        qstage = e_(nc.sbuf_tensor("qstage", [128, T], BF16))
        x0t = e_(nc.sbuf_tensor("x0t", [128, 512], F32))
        x1t = e_(nc.sbuf_tensor("x1t", [128, 512], BF16))
        yst = e_(nc.sbuf_tensor("yst", [128, 512], F32))
        ident = e_(nc.sbuf_tensor("ident", [128, 128], BF16))
        ones_b = e_(nc.sbuf_tensor("ones_b", [128, 1], BF16))
        ones_f = e_(nc.sbuf_tensor("ones_f", [128, 1], F32))
        onesr_f = e_(nc.sbuf_tensor("onesr_f", [1, 128], F32))
        kv = e_(nc.sbuf_tensor("kv", [128, NH], F32))
        col = e_(nc.sbuf_tensor("col", [128, 32], F32))
        mx2 = e_(nc.sbuf_tensor("mx2", [128, 8], BF16))
        denb = e_(nc.sbuf_tensor("denb", [128, 1024], F32))
        colacc = e_(nc.sbuf_tensor("colacc", [128, 64], F32))
        gstg0 = e_(nc.sbuf_tensor("gstg0", [128, 512], F32))
        mxacc = e_(nc.sbuf_tensor("mxacc", [128, 64], F32))

        R0 = rows[0:1, 0:1024]
        R1 = rows[0:1, 1024:2048]
        R2 = rows[0:1, 2048:3072]
        R3 = rows[0:1, 3072:4096]
        RS = rows[0:1, 4096:4224]

        PA = [e_(nc.psum_tensor(f"PA{i}", [128, 512], F32)) for i in range(2)]
        PV = [e_(nc.psum_tensor(f"PV{i}", [128, 512], F32)) for i in range(2)]
        PS = [e_(nc.psum_tensor(f"PS{i}", [1, 512], F32)) for i in range(2)]
        PT = [e_(nc.psum_tensor(f"PT{i}", [128, 1024], BF16)) for i in range(2)]

        def bcast(dst_col, src_row, gtag="bc0", psum=None):
            n = src_row.shape[1]
            ps = PV[0] if psum is None else psum
            S.wait("tensor", "mv", S.cnt["mv"])
            S.guard(gtag)
            c = S.mm(gtag, [(ps[:, 0:n], onesr_f[0:1, :], src_row)])
            S.wait("vector", "mt", c)
            cv = S.op("vector", "mv", lambda e, d=dst_col, n=n, ps=ps:
                      e.tensor_copy(d, ps[:, 0:n]))
            S.setg(gtag, "mv", cv)
            return cv

        for dst, srcd in ((Bq[0:1, :], qiota[:, :]), (kv[:, :], kvec[:, :]),
                          (biases[:, 0:NC], bq0[:, :]),
                          (biases[:, NC:2 * NC], bk0[:, :]),
                          (biases[:, 2 * NC:3 * NC], bv0[:, :]),
                          (biases[:, 3 * NC:4 * NC], bo0[:, :]),
                          (biases[:, 4 * NC:5 * NC], b21[:, :]),
                          (bias1[:, 0:64], b11[:, :]),
                          (bias1[:, 64:128], lng1[:, :]),
                          (bias1[:, 128:192], lnb1[:, :])):
            c_ini = S.dma("sync", dst, srcd, "sini")
            S.wait("sync", "sini", c_ini)
        eps5 = col[:, 7:8]
        S.op("gpsimd", "mg", lambda e: e.memset(rows[:, :], 0.0))
        S.op("gpsimd", "mg", lambda e: e.memset(col[:, :], 0.0))
        S.op("gpsimd", "mg", lambda e: e.memset(ones_b[:, :], 1.0))
        S.op("gpsimd", "mg", lambda e: e.memset(ones_f[:, :], 1.0))
        S.op("gpsimd", "mg", lambda e: e.memset(onesr_f[:, :], 1.0))
        S.wait("gpsimd", "mg", S.cnt["mg"])
        S.op("gpsimd", "mg", lambda e: e.memset(eps5, 1e-5))
        c_mident = S.op("gpsimd", "mg", lambda e: e.memset(ident[:, :], 0.0))
        S.wait("gpsimd", "mg", c_mident)
        cg_ident = S.op("gpsimd", "mg", lambda e: e.affine_select(
            out=ident[:, :], in_=ident[:, :], compare_op=ALU.not_equal,
            fill=1.0, base=0, pattern=[[-1, 128]], channel_multiplier=1))
        S.wait("tensor", "mg", cg_ident)
        S.wait("vector", "mg", cg_ident)
        S.wait("scalar", "mg", cg_ident)
        S.wait("tensor", "sini", S.cnt["sini"])
        for qh in range(NQH):
            c = S.mm(f"PA{qh}", [(PA[qh][:, :], onesr_f[0:1, :],
                                  Bq[0:1, qh * 512:(qh + 1) * 512])])
            S.wait("vector", "mt", c)
            cv = S.op("vector", "mv", lambda e, qh=qh: e.tensor_copy(
                Bq[:, qh * 512:(qh + 1) * 512], PA[qh][:, :]))
            S.setg(f"PA{qh}", "mv", cv)

        def emit_layernorm(src_of_chunk, src_dtype):
            f32src = src_dtype == F32
            onesv = ones_f if f32src else ones_b
            if f32src:
                xt = [fbuf[0][:, 0:1024], fbuf[0][:, 1024:2048]]
                sqt = fbuf[1][:, 0:1024]
            else:
                xt = [pool[:, 0:1024], pool[:, 1024:2048]]
                sqt = pool[:, 2048:3072]
            for tg in ("PS0", "PS1", "PV0", "PV1"):
                S.guard(tg)
            S.fence("sync")
            last_mm = 0
            prev_mm = [0, 0]
            fence_v = S.cnt["mv"]
            for c in range(NC):
                b = c % 2
                sem = "sxa" if b == 0 else "sxb"
                S.wait("sync", "mt", prev_mm[b])
                if c == 0:
                    S.wait("sync", "mv", fence_v)
                cl = S.dma("sync", xt[b], src_of_chunk(c), sem)
                S.wait("vector", sem, cl)
                S.wait("vector", "mt", last_mm)
                cv = S.op("vector", "mv", lambda e, t=xt[b], s=sqt:
                          e.tensor_tensor(s, t, t, ALU.mult))
                S.wait("tensor", sem, cl)
                S.wait("tensor", "mv", cv)

                def mmfn(e, t=xt[b], s=sqt, first=(c == 0), stop=(c == NC - 1)):
                    ins = None
                    for qh in range(NQH):
                        sl = slice(qh * 512, (qh + 1) * 512)
                        e.matmul(PS[qh][:, :], onesv[:, 0:1], t[:, sl],
                                 start=first, stop=stop, skip_group_check=True)
                        ins = e.matmul(PV[qh][0:1, :], onesv[:, 0:1], s[:, sl],
                                       start=first, stop=stop,
                                       skip_group_check=True)
                    return ins
                last_mm = S.op("tensor", "mt", mmfn)
                prev_mm[b] = last_mm
            S.wait("vector", "mt", last_mm)

            def fin1(e):
                e.tensor_scalar(R0[:, 0:512], PS[0][:, :], 1.0 / D, None, ALU.mult)
                e.tensor_scalar(R0[:, 512:1024], PS[1][:, :], 1.0 / D, None, ALU.mult)
                e.tensor_scalar(R1[:, 0:512], PV[0][0:1, :], 1.0 / D, None, ALU.mult)
                return e.tensor_scalar(R1[:, 512:1024], PV[1][0:1, :], 1.0 / D,
                                       None, ALU.mult)
            cv = S.op("vector", "mv", fin1)
            for tg in ("PS0", "PS1", "PV0", "PV1"):
                S.setg(tg, "mv", cv)
            S.wait("vector", "mv", cv)
            cv = S.op("vector", "mv", lambda e: e.tensor_tensor(
                R2[:, :], R0[:, :], R0[:, :], ALU.mult))
            S.wait("vector", "mv", cv)
            cv = S.op("vector", "mv", lambda e: e.tensor_tensor(
                R1[:, :], R1[:, :], R2[:, :], ALU.subtract))
            S.wait("scalar", "mv", cv)
            S.wait("scalar", "mg", cg_ident)
            ca = S.op("scalar", "ma", lambda e: e.activation(
                R1[:, :], R1[:, :], AF.Sqrt, bias=eps5[0:1, :], scale=1.0))
            S.wait("vector", "ma", ca)
            cv = S.op("vector", "mv", lambda e: e.reciprocal(R1[:, :], R1[:, :]))
            prev = [0, 0]
            for qh in range(NQH):
                sl = slice(qh * 512, (qh + 1) * 512)
                S.wait("tensor", "mv", cv)
                c1 = S.mm("PV0", [(PV[0][:, :], onesr_f[0:1, :], R0[:, sl])])
                c2 = S.mm("PV1", [(PV[1][:, :], onesr_f[0:1, :], R1[:, sl])])
                for c in range(NC):
                    b = c % 2
                    sem = "sxa" if b == 0 else "sxb"
                    S.wait("sync", "mv", prev[b])
                    if c < 2 and qh == 0:
                        S.wait("sync", "mt", last_mm)
                    cl = S.dma("sync", xt[b], src_of_chunk(c), sem)
                    S.wait("vector", sem, cl)
                    S.wait("vector", "mt", c2)

                    def ap(e, t=xt[b], c=c, sl=sl):
                        e.tensor_tensor(t[:, sl], t[:, sl], PV[0][:, :],
                                        ALU.subtract)
                        return e.tensor_tensor(
                            xn[:, c * T + sl.start:c * T + sl.stop],
                            t[:, sl], PV[1][:, :], ALU.mult)
                    prev[b] = S.op("vector", "mv", ap)
                pc = max(prev)
                S.setg("PV0", "mv", pc)
                S.setg("PV1", "mv", pc)
            return S.cnt["mv"]

        wqa = pool[:, 0:8192]
        wqb = pool[:, 8192:16384]
        wfree = {0: None, 1: None}

        def emit_proj(wdram, out_writer, xn_ready, rhs_buf=None):
            loaded = {}
            rb = xn if rhs_buf is None else rhs_buf
            S.fence("gpsimd", ("mv", "mt", "ma"))

            def load_quarter(q):
                b = q % 2
                if wfree[b] is not None:
                    S.wait("gpsimd", wfree[b][0], wfree[b][1])
                    wfree[b] = None
                sem = "swa" if b == 0 else "swb"
                last = 0
                for c in range(NC):
                    last = S.dma("gpsimd", (wqa if b == 0 else wqb)[:, c * 512:(c + 1) * 512],
                                 wdram[c * 128:(c + 1) * 128, q * 512:(q + 1) * 512],
                                 sem)
                loaded[q] = (sem, last)

            def compute_quarter(q):
                b = q % 2
                wst = wqa if b == 0 else wqb
                S.wait("tensor", loaded[q][0], loaded[q][1])
                S.wait("tensor", "mv", xn_ready)
                for otl in range(4):
                    ot = q * 4 + otl
                    for qh in range(NQH):
                        ptag = f"PA{(ot * NQH + qh) % 2}"
                        ps = PA[(ot * NQH + qh) % 2]
                        items = []
                        for c in range(NC):
                            lh = wst[:, c * 512 + otl * 128:c * 512 + otl * 128 + 128]
                            rh = rb[:, c * T + qh * 512:c * T + qh * 512 + 512]
                            items.append((ps[:, :], lh, rh))
                        cm = S.mm(ptag, items)
                        sem, cnt = out_writer(ot, qh, ps, cm)
                        S.setg(ptag, sem, cnt)
                wfree[b] = ("mt", S.cnt["mt"])

            load_quarter(0)
            for q in range(1, 4):
                load_quarter(q)
                compute_quarter(q - 1)
            compute_quarter(3)

        qkv_state = {"dma": 0}

        def mk_qkv_writer(dst, bidx, maxcol):
            state = qkv_state

            def wr(ot, qh, ps, cm):
                S.wait("vector", "mt", cm)
                S.wait("vector", "shd", state["dma"])

                def ev(e, ot=ot, qh=qh, ps=ps):
                    return e.tensor_scalar_add(
                        qstage[:, qh * 512:(qh + 1) * 512], ps[:, :],
                        biases[:, bidx * NC + ot:bidx * NC + ot + 1])
                cv = S.op("vector", "mv", ev)
                if maxcol is not None:
                    idx = maxcol * 32 + ot * 2 + qh
                    cv = S.op("vector", "mv", lambda e, qh=qh, idx=idx:
                              e.reduce_max(mxacc[:, idx:idx + 1],
                                           qstage[:, qh * 512:(qh + 1) * 512],
                                           axis=AX.X, apply_absolute_value=True))
                if qh == NQH - 1:
                    S.wait("scalar", "mv", cv)
                    state["dma"] = S.dma(
                        "scalar", dst[ot * 128:(ot + 1) * 128, :], qstage[:, :],
                        "shd")
                return ("mv", cv)
            return wr

        def emit_wstats(wdram, nrows, rowlen, acc):
            S.fence("sync")
            prev = [0, 0]
            k = 0
            segs = []
            o = 0
            while o < rowlen:
                w = min(2048, rowlen - o)
                segs.append((o, w))
                o += w
            nacc = len(segs) * (nrows // 128)
            assert nacc <= 64
            for c in range(nrows // 128):
                for (o, w) in segs:
                    b = k % 2
                    S.wait("sync", "mv", prev[b])
                    sem = "sqa" if b == 0 else "sqb"
                    cl = S.dma("sync", fbuf[b][:, 0:w],
                               wdram[c * 128:(c + 1) * 128, o:o + w], sem)
                    S.wait("vector", sem, cl)
                    prev[b] = S.op("vector", "mv",
                                   lambda e, b=b, w=w, k=k: e.reduce_sum(
                                       colacc[:, k:k + 1], fbuf[b][:, 0:w],
                                       axis=AX.X, apply_absolute_value=True))
                    k += 1
            cv = S.cnt["mv"]
            S.wait("vector", "mv", cv)
            cv = S.op("vector", "mv", lambda e, n=nacc: e.reduce_sum(
                acc, colacc[:, 0:n], axis=AX.X))
            S.wait("vector", "mv", cv)

        def finalize_gamma(acc, nelem, tcol, ncol, grow, mode):
            S.wait("tensor", "mv", S.cnt["mv"])
            cm = S.mm("PS0", [(PS[0][0:1, 0:1], ones_f[:, 0:1], acc)])
            S.wait("vector", "mt", cm)
            op1 = ALU.add if mode == "l0" else ALU.max
            c2 = 1e-8 if mode == "l0" else 1e-5
            cv = S.op("vector", "mv", lambda e: e.tensor_scalar(
                grow[:, 1:2], PS[0][0:1, 0:1], 1.0 / nelem, c2, ALU.mult, op1))
            S.setg("PS0", "mv", cv)
            S.wait("vector", "mv", cv)

            def fin2(e):
                e.tensor_scalar_mul(grow[:, 2:3], grow[:, 1:2], 0.5)
                return e.tensor_scalar_mul(grow[:, 3:4], grow[:, 1:2], -0.5)
            cv = S.op("vector", "mv", fin2)
            S.wait("vector", "mv", cv)
            bcast(tcol, grow[:, 2:3], "bc0")
            bcast(ncol, grow[:, 3:4], "bc0")
            return S.cnt["mv"]

        def ternary(e, dst, src, w, tcol, ncol):
            t1 = stg[:, 0:2048]
            t2 = stg[:, 2048:4096]
            e.tensor_single_scalar(t1[:, 0:w], src, tcol, ALU.is_gt)
            e.tensor_single_scalar(t2[:, 0:w], src, ncol, ALU.is_lt)
            return e.tensor_tensor(dst, t1[:, 0:w], t2[:, 0:w], ALU.subtract)

        def src_x0(c):
            return xfm[c * 128:(c + 1) * 128, :]

        xn_mv = emit_layernorm(src_x0, F32)

        emit_proj(wq0, mk_qkv_writer(qd, 0, 0), xn_mv)
        emit_proj(wk0, mk_qkv_writer(k_own, 1, 1), xn_mv)
        emit_proj(wv0, mk_qkv_writer(v_own, 2, None), xn_mv)
        qkv_shd = S.cnt["shd"]

        S.wait("gpsimd", "shd", qkv_shd)
        S.op("gpsimd", "scc", lambda e: e.collective_compute(
            "AllGather", ALU.bypass, replica_groups=groups,
            ins=[k_own.ap().opt()], outs=[k_full.ap().opt()]))
        cc_ag = S.op("gpsimd", "scc", lambda e: e.collective_compute(
            "AllGather", ALU.bypass, replica_groups=groups,
            ins=[v_own.ap().opt()], outs=[v_full.ap().opt()]))

        S.wait("vector", "mv", S.cnt["mv"])
        cv = S.op("vector", "mv", lambda e: e.reduce_max(
            mx2[:, 0:1], mxacc[:, 0:32], axis=AX.X))
        cv = S.op("vector", "mv", lambda e: e.reduce_max(
            mx2[:, 1:2], mxacc[:, 32:64], axis=AX.X))
        S.wait("tensor", "mv", cv)
        S.wait("tensor", "mg", cg_ident)
        for i in range(2):
            S.guard(f"PT{i}")
            ctr = S.op("tensor", "mt", lambda e, i=i: e.transpose(
                PT[i][0:1, 0:128], mx2[:, i:i + 1], ident[:, :]))
            S.wait("vector", "mt", ctr)
            cvi = S.op("vector", "mv", lambda e, i=i: e.tensor_copy(
                stg[0:1, i * 128:(i + 1) * 128], PT[i][0:1, 0:128]))
            S.setg(f"PT{i}", "mv", cvi)
        S.wait("vector", "mv", S.cnt["mv"])
        cv = S.op("vector", "mv", lambda e: e.reduce_max(
            RS[:, 0:1], stg[0:1, 0:128], axis=AX.X))
        cv = S.op("vector", "mv", lambda e: e.reduce_max(
            RS[:, 1:2], stg[0:1, 128:256], axis=AX.X))
        S.wait("vector", "mv", cv)
        cv = S.op("vector", "mv", lambda e: e.tensor_tensor(
            RS[:, 2:3], RS[:, 0:1], RS[:, 1:2], ALU.mult))
        S.wait("vector", "mv", cv)
        cv = S.op("vector", "mv", lambda e: e.tensor_scalar_mul(
            RS[:, 2:3], RS[:, 2:3], -128.0 * RS128))
        S.wait("vector", "mv", cv)
        ebias = col[:, 2:3]
        bcast(ebias, RS[:, 2:3], "bc0")
        ebias_mv = S.cnt["mv"]

        thr = {}
        for i, (wd, nr, rl) in enumerate(((w1t0, D, H0), (w2t0, D, H0),
                                          (w3t0, H0, D))):
            acc = col[:, 3 + i:4 + i]
            emit_wstats(wd, nr, rl, acc)
            tcol = col[:, 8 + 2 * i:9 + 2 * i]
            ncol = col[:, 9 + 2 * i:10 + 2 * i]
            finalize_gamma(acc, nr * rl, tcol, ncol,
                           RS[:, 8 + 4 * i:12 + 4 * i], "l0")
            thr[i] = (tcol, ncol, S.cnt["mv"])

        wcolq = pool[:, 16384:24576]
        h1grp = pool[:, 0:4096]
        NG0 = H0 // 512
        hid_dma = 0
        wcol_free = 0
        for gi in range(NG0):
            for wi, (wd, (tcol, ncol, tmv)) in enumerate(
                    ((w1t0, thr[0]), (w2t0, thr[1]))):
                S.fence("sync", ("mv",))
                prev = [0, 0]
                for c in range(NC):
                    b = c % 2
                    S.wait("sync", "mv", prev[b])
                    sem = "sqa" if b == 0 else "sqb"
                    cl = S.dma("sync", fbuf[b][:, 0:512],
                               wd[c * 128:(c + 1) * 128,
                                  gi * 512:(gi + 1) * 512], sem)
                    S.wait("vector", sem, cl)
                    S.wait("vector", "mv", tmv)
                    S.wait("vector", "mt", wcol_free)

                    def qf(e, b=b, c=c, tcol=tcol, ncol=ncol):
                        return ternary(e, wcolq[:, c * 512:(c + 1) * 512],
                                       fbuf[b][:, 0:512], 512, tcol, ncol)
                    prev[b] = S.op("vector", "mv", qf)
                q_mv = S.cnt["mv"]
                S.wait("tensor", "mv", q_mv)
                for htl in range(4):
                    for qh in range(NQH):
                        ptag = f"PA{(htl * NQH + qh) % 2}"
                        ps = PA[(htl * NQH + qh) % 2]
                        items = []
                        for c in range(NC):
                            lh = wcolq[:, c * 512 + htl * 128:c * 512 + htl * 128 + 128]
                            rh = xn[:, c * T + qh * 512:c * T + qh * 512 + 512]
                            items.append((ps[:, :], lh, rh))
                        cm = S.mm(ptag, items)
                        sl = slice(htl * T + qh * 512, htl * T + qh * 512 + 512)
                        if wi == 0:
                            S.wait("scalar", "mt", cm)
                            S.wait("scalar", "shd", hid_dma)
                            ca = S.op("scalar", "ma", lambda e, sl=sl, ps=ps:
                                      e.activation(h1grp[:, sl], ps[:, :],
                                                   AF.Sigmoid))
                            S.setg(ptag, "ma", ca)
                        else:
                            S.wait("vector", "mt", cm)
                            S.wait("vector", "ma", S.cnt["ma"])
                            cv = S.op("vector", "mv", lambda e, sl=sl, ps=ps:
                                      e.tensor_tensor(h1grp[:, sl], h1grp[:, sl],
                                                      ps[:, :], ALU.mult))
                            S.setg(ptag, "mv", cv)
                            ht_g = gi * 4 + htl
                            S.wait("scalar", "mv", cv)
                            hid_dma = S.dma(
                                "scalar",
                                hid0[ht_g * 128:(ht_g + 1) * 128,
                                     qh * 512:(qh + 1) * 512],
                                h1grp[:, sl], "shd")
                wcol_free = S.cnt["mt"]

        qw3s = att[:, 0:16384]
        hstr = pool[:, 0:8192]
        sup_sizes = [8, 8, 8, 8, 8, 4]
        hc0 = 0
        hstr_free = 0
        qw3_free = 0
        for si, ssz in enumerate(sup_sizes):
            S.fence("sync", ("mv",))
            hload = 0
            prevq = [0, 0]
            for j in range(ssz):
                hc = hc0 + j
                S.wait("scalar", "shd", hid_dma if si == 0 and j == 0 else 0)
                S.wait("scalar", "mt", hstr_free if j == 0 else 0)
                hload = S.dma("scalar", hstr[:, j * 1024:(j + 1) * 1024],
                              hid0[hc * 128:(hc + 1) * 128, :], "shd")
                b = j % 2
                S.wait("sync", "mv", prevq[b])
                sem = "sqa" if b == 0 else "sqb"
                cl = S.dma("sync", fbuf[b][:, 0:2048],
                           w3t0[hc * 128:(hc + 1) * 128, :], sem)
                S.wait("vector", sem, cl)
                S.wait("vector", "mv", thr[2][2] if si == 0 and j == 0 else 0)
                S.wait("vector", "mt", qw3_free if j == 0 else 0)

                def qf3(e, b=b, j=j):
                    return ternary(e, qw3s[:, j * 2048:(j + 1) * 2048],
                                   fbuf[b][:, 0:2048], 2048, thr[2][0], thr[2][1])
                prevq[b] = S.op("vector", "mv", qf3)
            q_mv = S.cnt["mv"]
            S.wait("tensor", "mv", q_mv)
            S.wait("tensor", "shd", hload)
            for dt_ in range(NC):
                for qh in range(NQH):
                    ptag = f"PV{(dt_ * NQH + qh) % 2}"
                    ps = PV[(dt_ * NQH + qh) % 2]
                    items = []
                    for j in range(ssz):
                        lh = qw3s[:, j * 2048 + dt_ * 128:j * 2048 + dt_ * 128 + 128]
                        rh = hstr[:, j * 1024 + qh * 512:j * 1024 + qh * 512 + 512]
                        items.append((ps[:, :], lh, rh))
                    cm = S.mm(ptag, items)
                    S.wait("vector", "mt", cm)
                    sl = slice(dt_ * T + qh * 512, dt_ * T + qh * 512 + 512)
                    if si == 0:
                        cv = S.op("vector", "mv", lambda e, sl=sl, ps=ps:
                                  e.tensor_copy(g[:, sl], ps[:, :]))
                    else:
                        cv = S.op("vector", "mv", lambda e, sl=sl, ps=ps:
                                  e.tensor_tensor(g[:, sl], g[:, sl], ps[:, :],
                                                  ALU.add))
                    S.setg(ptag, "mv", cv)
            hstr_free = S.cnt["mt"]
            qw3_free = S.cnt["mt"]
            hc0 += ssz
        g_mv = S.cnt["mv"]
        S.wait("scalar", "mv", g_mv)
        for c in range(NC):
            ca = S.op("scalar", "ma", lambda e, c=c: e.activation(
                g[:, c * T:(c + 1) * T], g[:, c * T:(c + 1) * T], AF.Silu))
        g_ma = S.cnt["ma"]

        qh_t = pool[:, 0:1024]
        kh_t = pool[:, 1024:3072]
        vh_f = pool[:, 3072:5120]
        vh_t = pool[:, 5120:7168]
        et_t = pool[:, 7168:8192]
        den = R2
        S.wait("scalar", "scc", cc_ag)
        S.wait("scalar", "mv", g_mv)
        S.wait("scalar", "mt", S.cnt["mt"])
        head_mt = 0
        acc_prev = 0
        for h in range(NH):
            S.wait("scalar", "mt", head_mt)
            c_q = S.dma("scalar", qh_t, qd[h * 128:(h + 1) * 128, :], "shd")
            c_k = S.dma("scalar", kh_t[:, 0:1024],
                        k_full[h * 128:(h + 1) * 128, :], "shd")
            c_k = S.dma("scalar", kh_t[:, 1024:2048],
                        k_full[D + h * 128:D + h * 128 + 128, :], "shd")
            c_v = S.dma("scalar", vh_f[:, 0:1024],
                        v_full[h * 128:(h + 1) * 128, :], "shd")
            c_v = S.dma("scalar", vh_f[:, 1024:2048],
                        v_full[D + h * 128:D + h * 128 + 128, :], "shd")
            S.wait("tensor", "shd", c_v)
            for t in range(16):
                pt = PT[t % 2]
                S.guard(f"PT{t % 2}")
                cm = S.op("tensor", "mt", lambda e, t=t, pt=pt: e.transpose(
                    pt[:, 0:128], vh_f[:, t * 128:(t + 1) * 128], ident[:, :]))
                S.wait("vector", "mt", cm)
                cv = S.op("vector", "mv", lambda e, t=t, pt=pt: e.tensor_copy(
                    vh_t[:, t * 128:(t + 1) * 128], pt[:, 0:128]))
                S.setg(f"PT{t % 2}", "mv", cv)
            vt_mv = S.cnt["mv"]
            if h == 0:
                S.wait("tensor", "mv", ebias_mv)
                S.guard("PS0"); S.guard("PS1")
                S.guard("PV0"); S.guard("PV1")
            for t in range(16):
                S.guard("PA0"); S.guard("PA1")

                def smm(e, t=t):
                    e.matmul(PA[0][:, :], kh_t[:, t * 128:(t + 1) * 128],
                             qh_t[:, 0:512], start=True, stop=True,
                             skip_group_check=True)
                    return e.matmul(PA[1][:, :], kh_t[:, t * 128:(t + 1) * 128],
                                    qh_t[:, 512:1024], start=True, stop=True,
                                    skip_group_check=True)
                cm = S.op("tensor", "mt", smm)
                S.wait("scalar", "mt", cm)
                S.wait("scalar", "mt", acc_prev)
                ca = 0
                for qh in range(NQH):
                    ca = S.op("scalar", "ma", lambda e, qh=qh: e.activation(
                        et_t[:, qh * 512:(qh + 1) * 512], PA[qh][:, :], AF.Exp,
                        bias=ebias, scale=RS128))
                S.wait("vector", "ma", ca)

                def mask(e, t=t):
                    return e.scalar_tensor_tensor(
                        et_t[:, 0:1024], Bq[:, :], kv[:, t:t + 1],
                        et_t[:, 0:1024], op0=ALU.is_ge, op1=ALU.mult)
                cv = S.op("vector", "mv", mask)
                S.setg("PA0", "ma", ca)
                S.setg("PA1", "ma", ca)
                S.wait("tensor", "mv", cv)
                if t == 0:
                    S.wait("tensor", "mv", vt_mv)

                def acc(e, t=t):
                    e.matmul(PS[0][:, :], ones_b[:, 0:1], et_t[:, 0:512],
                             start=(t == 0), stop=(t == 15),
                             skip_group_check=True)
                    e.matmul(PS[1][:, :], ones_b[:, 0:1], et_t[:, 512:1024],
                             start=(t == 0), stop=(t == 15),
                             skip_group_check=True)
                    e.matmul(PV[0][:, :], vh_t[:, t * 128:(t + 1) * 128],
                             et_t[:, 0:512], start=(t == 0), stop=(t == 15),
                             skip_group_check=True)
                    return e.matmul(PV[1][:, :], vh_t[:, t * 128:(t + 1) * 128],
                                    et_t[:, 512:1024], start=(t == 0),
                                    stop=(t == 15), skip_group_check=True)
                acc_prev = S.op("tensor", "mt", acc)
            S.wait("vector", "mt", acc_prev)

            def dfin(e):
                e.tensor_copy(den[:, 0:512], PS[0][:, :])
                return e.tensor_copy(den[:, 512:1024], PS[1][:, :])
            cv = S.op("vector", "mv", dfin)
            S.setg("PS0", "mv", cv); S.setg("PS1", "mv", cv)
            S.wait("vector", "mv", cv)
            cv = S.op("vector", "mv", lambda e: e.reciprocal(den[:, :], den[:, :]))
            S.wait("tensor", "mv", cv)
            for qh in range(NQH):
                cmb = S.mm(f"PA{qh}", [(PA[qh][:, :], onesr_f[0:1, :],
                                        den[:, qh * 512:(qh + 1) * 512])])
                S.wait("vector", "mt", cmb)

                def avd(e, h=h, qh=qh):
                    e.tensor_copy(denb[:, qh * 512:(qh + 1) * 512], PA[qh][:, :])
                    return e.tensor_tensor(
                        att[:, h * T + qh * 512:h * T + qh * 512 + 512],
                        PV[qh][:, :], denb[:, qh * 512:(qh + 1) * 512], ALU.mult)
                cv = S.op("vector", "mv", avd)
                S.setg(f"PA{qh}", "mv", cv)
                S.setg(f"PV{qh}", "mv", cv)
            head_mt = S.cnt["mt"]
            if h == 0:
                pass
            S.guard("PS0"); S.guard("PS1")
        att_mv = S.cnt["mv"]

        ostate = {"dma": 0, "pv": 0}

        def wr_o0(ot, qh, ps, cm):
            S.wait("sync", "mv", ostate["pv"])
            cl = S.dma("sync", x0t[:, :],
                       xfm[ot * 128:(ot + 1) * 128, qh * 512:(qh + 1) * 512],
                       "sxo")
            S.wait("vector", "mt", cm)
            S.wait("vector", "sxo", cl)
            S.wait("vector", "shd", ostate["dma"])
            if ot == 0 and qh == 0:
                S.wait("vector", "ma", g_ma)

            def ev(e, ot=ot, qh=qh, ps=ps):
                e.scalar_tensor_tensor(
                    stg[:, 0:512], ps[:, :],
                    biases[:, 3 * NC + ot:3 * NC + ot + 1],
                    g[:, ot * T + qh * 512:ot * T + qh * 512 + 512],
                    op0=ALU.add, op1=ALU.mult)
                return e.tensor_tensor(x1t[:, :], stg[:, 0:512], x0t[:, :],
                                       ALU.add)
            cv = S.op("vector", "mv", ev)
            ostate["pv"] = cv
            S.wait("scalar", "mv", cv)
            ostate["dma"] = S.dma(
                "scalar", x1d[ot * 128:(ot + 1) * 128, qh * 512:(qh + 1) * 512],
                x1t[:, :], "shd")
            return ("mv", cv)

        S.wait("gpsimd", "mv", att_mv)
        emit_proj(wo0, wr_o0, att_mv, rhs_buf=att)
        x1_shd = S.cnt["shd"]

        def emit_layer1():
            S.wait("sync", "shd", x1_shd)

            def src_x1(c):
                return x1d[c * 128:(c + 1) * 128, :]
            xn1_mv = emit_layernorm(src_x1, BF16)

            vstate = {"dma": 0}

            def wr_v1(ot, qh, ps, cm):
                S.wait("vector", "mt", cm)
                S.wait("vector", "shd", vstate["dma"])
                cv = S.op("vector", "mv", lambda e, qh=qh, ps=ps: e.tensor_copy(
                    qstage[:, qh * 512:(qh + 1) * 512], ps[:, :]))
                if qh == NQH - 1:
                    S.wait("scalar", "mv", cv)
                    vstate["dma"] = S.dma(
                        "scalar", vd1[ot * 128:(ot + 1) * 128, :], qstage[:, :],
                        "shd")
                return ("mv", cv)
            emit_proj(wv1, wr_v1, xn1_mv)

            wstage = pool[:, 16384:24576]
            for wd, dstbuf in ((wq1, att), (wk1, pool)):
                for eq in range(4):
                    S.fence("gpsimd", ("mt",))
                    last = 0
                    for c in range(NC):
                        last = S.dma("gpsimd",
                                     wstage[:, c * 512:(c + 1) * 512],
                                     wd[c * 128:(c + 1) * 128,
                                        eq * 512:(eq + 1) * 512], "swa")
                    S.wait("tensor", "swa", last)
                    S.wait("tensor", "mv", xn1_mv)
                    for st_ in range(8):
                        ptag = f"PA{st_ % 2}"
                        ps = PA[st_ % 2]
                        items = []
                        for c in range(NC):
                            lh = xn[:, c * T + st_ * 128:c * T + st_ * 128 + 128]
                            rh = wstage[:, c * 512:(c + 1) * 512]
                            items.append((ps[:, :], lh, rh))
                        cm = S.mm(ptag, items)
                        S.wait("vector", "mt", cm)
                        cv = S.op("vector", "mv",
                                  lambda e, st_=st_, eq=eq, ps=ps, dstbuf=dstbuf:
                                  e.tensor_copy(
                                      dstbuf[:, st_ * 2048 + eq * 512:
                                             st_ * 2048 + eq * 512 + 512],
                                      ps[:, :]))
                        S.setg(ptag, "mv", cv)
            qk_mv = S.cnt["mv"]

            S.wait("tensor", "mv", qk_mv)
            s1state = {"dma": 0}
            for et in range(NC):
                for fq in range(4):
                    ptag = f"PA{(et * 4 + fq) % 2}"
                    ps = PA[(et * 4 + fq) % 2]
                    items = []
                    for st_ in range(8):
                        lh = att[:, st_ * 2048 + et * 128:st_ * 2048 + et * 128 + 128]
                        rh = pool[:, st_ * 2048 + fq * 512:st_ * 2048 + fq * 512 + 512]
                        items.append((ps[:, :], lh, rh))
                    cm = S.mm(ptag, items)
                    S.wait("vector", "mt", cm)
                    S.wait("vector", "shd", s1state["dma"])
                    cv = S.op("vector", "mv", lambda e, fq=fq, ps=ps:
                              e.tensor_copy(denb[0:128, fq * 256:fq * 256 + 512]
                                            if False else
                                            fbuf[fq % 2][:, 0:512], ps[:, :]))
                    S.setg(ptag, "mv", cv)
                    S.wait("scalar", "mv", cv)
                    s1state["dma"] = S.dma(
                        "scalar",
                        s1loc[et * 128:(et + 1) * 128, fq * 512:(fq + 1) * 512],
                        fbuf[fq % 2][:, 0:512], "shd")
            s1_shd = S.cnt["shd"]

            S.wait("gpsimd", "shd", s1_shd)
            cc_ar = S.op("gpsimd", "scc", lambda e: e.collective_compute(
                "AllReduce", ALU.add, replica_groups=groups,
                ins=[s1loc.ap().opt()], outs=[s1sum.ap().opt()]))

            acc4 = col[:, 3:4]
            acc5 = col[:, 4:5]
            t4, n4 = col[:, 14:15], col[:, 15:16]
            t5, n5 = col[:, 16:17], col[:, 17:18]
            emit_wstats(w1t1, D, H1, acc4)
            finalize_gamma(acc4, D * H1, t4, n4, RS[:, 24:28], "l1")
            g1row = RS[:, 25:26]
            thr4_mv = S.cnt["mv"]
            emit_wstats(w2t1, H1, D, acc5)
            finalize_gamma(acc5, H1 * D, t5, n5, RS[:, 28:32], "l1")
            g2row = RS[:, 29:30]
            thr5_mv = S.cnt["mv"]

            for tg in ("PS0", "PS1"):
                S.guard(tg)
            S.wait("vector", "mv", S.cnt["mv"])
            S.opq("vector", lambda e: e.memset(denb[:, :], 0.0))
            last_mm = 0
            cv = 0
            for c in range(NC):
                def absmax1(e, c=c):
                    e.tensor_scalar_mul(stg[:, 1024:2048],
                                        xn[:, c * T:(c + 1) * T], -1.0)
                    e.tensor_tensor(denb[:, :], denb[:, :],
                                    stg[:, 1024:2048], ALU.max)
                    return e.tensor_tensor(denb[:, :], denb[:, :],
                                           xn[:, c * T:(c + 1) * T], ALU.max)
                cv = S.op("vector", "mv", absmax1)
                cs = S.op("vector", "mv", lambda e, c=c: e.tensor_tensor(
                    stg[:, 0:1024], xn[:, c * T:(c + 1) * T],
                    xn[:, c * T:(c + 1) * T], ALU.mult))
                S.wait("tensor", "mv", cs)
                S.wait("tensor", "mt", last_mm) if False else None

                def mmq(e, first=(c == 0), stop=(c == NC - 1)):
                    e.matmul(PS[0][:, :], ones_b[:, 0:1], stg[:, 0:512],
                             start=first, stop=stop, skip_group_check=True)
                    return e.matmul(PS[1][:, :], ones_b[:, 0:1], stg[:, 512:1024],
                                    start=first, stop=stop, skip_group_check=True)
                last_mm = S.op("tensor", "mt", mmq)
                S.wait("vector", "mt", last_mm)
            S.wait("vector", "mv", cv)
            cv = S.op("vector", "mv", lambda e: e.tensor_copy(
                stg[:, 0:1024], denb[:, :]))
            S.wait("tensor", "mv", cv)
            for j in range(8):
                S.guard(f"PT{j % 2}")
                cm = S.op("tensor", "mt", lambda e, j=j: e.transpose(
                    PT[j % 2][:, 0:128], stg[:, j * 128:(j + 1) * 128],
                    ident[:, :]))
                S.wait("vector", "mt", cm)
                cv = S.op("vector", "mv", lambda e, j=j: e.reduce_max(
                    mxacc[:, j:j + 1], PT[j % 2][:, 0:128], axis=AX.X))
                S.setg(f"PT{j % 2}", "mv", cv)
            S.wait("sync", "mv", cv)
            gat = 0
            for j in range(8):
                gat = S.dma("sync", R3[:, j * 128:(j + 1) * 128].rearrange(
                    "a b -> b a"), mxacc[:, j:j + 1], "sg8") if False else gat
            for j in range(8):
                gat = S.dma("sync", R3[:, j * 128:(j + 1) * 128],
                            mxacc[:, j:j + 1], "sg8")
            S.wait("vector", "sg8", gat)
            S.wait("vector", "mt", last_mm)
            cv = S.op("vector", "mv", lambda e: e.tensor_scalar(
                R0[:, 0:512], PS[0][:, :], 1.0 / D, 1e-5, ALU.mult, ALU.add))
            cv = S.op("vector", "mv", lambda e: e.tensor_scalar(
                R0[:, 512:1024], PS[1][:, :], 1.0 / D, 1e-5, ALU.mult, ALU.add))
            S.setg("PS0", "mv", cv); S.setg("PS1", "mv", cv)
            S.wait("scalar", "mv", cv)
            ca = S.op("scalar", "ma", lambda e: e.activation(
                R0[:, :], R0[:, :], AF.Sqrt))
            S.wait("vector", "ma", ca)
            cv = S.op("vector", "mv", lambda e: e.reciprocal(R0[:, :], R0[:, :]))
            S.wait("vector", "mv", cv)
            cv = S.op("vector", "mv", lambda e: e.tensor_tensor(
                R3[:, :], R3[:, :], R0[:, :], ALU.mult))
            S.wait("vector", "mv", cv)
            cv = S.op("vector", "mv", lambda e: e.tensor_scalar_max(
                R3[:, :], R3[:, :], 1e-5))
            S.wait("vector", "mv", cv)
            cv = S.op("vector", "mv", lambda e: e.reciprocal(R1[:, :], R3[:, :]))
            S.wait("vector", "mv", cv)

            def mkalpha(e):
                e.tensor_tensor(R2[:, :], R0[:, :], R1[:, :], ALU.mult)
                return e.tensor_scalar_mul(R2[:, :], R2[:, :], 127.0)
            cv = S.op("vector", "mv", mkalpha)
            S.wait("vector", "mv", thr4_mv)
            S.wait("vector", "mv", cv)
            cv = S.op("vector", "mv", lambda e: e.tensor_single_scalar(
                R1[:, :], R3[:, :], g1row, ALU.mult))
            S.wait("vector", "mv", cv)
            cv = S.op("vector", "mv", lambda e: e.tensor_scalar_mul(
                R1[:, :], R1[:, :], 1.0 / 127.0))
            S.wait("vector", "mv", cv)
            bcast(denb[:, 0:512], R2[:, 0:512], "bc0")
            bcast(denb[:, 512:1024], R2[:, 512:1024], "bc0")
            ba_mv = S.cnt["mv"]
            S.wait("vector", "mv", ba_mv)
            for c in range(NC):
                def qa(e, c=c):
                    e.tensor_tensor(fbuf[0][:, 0:1024], xn[:, c * T:(c + 1) * T],
                                    denb[:, :], ALU.mult)
                    e.tensor_copy(stgi[:, 0:512], fbuf[0][:, 0:512])
                    e.tensor_copy(xn[:, c * T:c * T + 512], stgi[:, 0:512])
                    e.tensor_copy(stgi[:, 0:512], fbuf[0][:, 512:1024])
                    return e.tensor_copy(xn[:, c * T + 512:c * T + 1024],
                                         stgi[:, 0:512])
                cv = S.op("vector", "mv", qa)
            xq1_mv = S.cnt["mv"]

            for tg in ("PS0", "PS1", "PV0", "PV1"):
                S.guard(tg)
            S.wait("vector", "mv", xq1_mv)
            S.opq("vector", lambda e: e.memset(stg[:, 1024:2048], 0.0))
            bcast(denb[:, 0:512], R1[:, 0:512], "bc0")
            bcast(denb[:, 512:1024], R1[:, 512:1024], "bc0")
            bb_mv = S.cnt["mv"]
            NG1 = H1 // 512
            hdma = 0
            wcol_free = 0
            stat_mm = 0
            for gi in range(NG1):
                S.fence("sync", ("mv", "ma", "mt"))
                prev = [0, 0]
                ma_gate = S.cnt.get("ma", 0)
                for c in range(NC):
                    b = c % 2
                    sem = "sqa" if b == 0 else "sqb"
                    S.wait("sync", "mv", prev[b])
                    cl = S.dma("sync", fbuf[b][:, 0:512],
                               w1t1[c * 128:(c + 1) * 128,
                                    gi * 512:(gi + 1) * 512], sem)
                    S.wait("vector", sem, cl)
                    S.wait("vector", "mv", thr4_mv)
                    S.wait("vector", "ma", ma_gate if c == 0 else 0)
                    S.wait("vector", "mt", wcol_free)

                    def qf(e, b=b, c=c):
                        return ternary(e, wcolq[:, c * 512:(c + 1) * 512],
                                       fbuf[b][:, 0:512], 512, t4, n4)
                    prev[b] = S.op("vector", "mv", qf)
                q_mv = S.cnt["mv"]
                S.wait("tensor", "mv", q_mv)
                for htl in range(4):
                    ht = gi * 4 + htl
                    for qh in range(NQH):
                        ptag = f"PA{(htl * NQH + qh) % 2}"
                        ps = PA[(htl * NQH + qh) % 2]
                        items = []
                        for c in range(NC):
                            lh = wcolq[:, c * 512 + htl * 128:
                                       c * 512 + htl * 128 + 128]
                            rh = xn[:, c * T + qh * 512:c * T + qh * 512 + 512]
                            items.append((ps[:, :], lh, rh))
                        cm = S.mm(ptag, items)
                        S.wait("vector", "mt", cm)
                        S.wait("vector", "ma", S.cnt.get("ma", 0))
                        S.wait("vector", "mv", bb_mv)
                        sl5 = slice(qh * 512, qh * 512 + 512)
                        cv = S.op("vector", "mv", lambda e, ps=ps, sl5=sl5:
                                  e.tensor_tensor(fbuf[0][:, 0:512], ps[:, :],
                                                  denb[:, sl5], ALU.mult))
                        S.setg(ptag, "mv", cv)
                        S.wait("scalar", "mv", cv)
                        S.wait("scalar", "shd", hdma)
                        ca = S.op("scalar", "ma", lambda e, ht=ht: e.activation(
                            stg[:, 0:512], fbuf[0][:, 0:512], AF.Silu,
                            bias=bias1[:, ht // 2:ht // 2 + 1]
                            if False else bias1[:, ht:ht + 1] if ht < 64 else None))
                        ca = S.op("scalar", "ma", lambda e: e.activation(
                            stg[:, 512:1024], stg[:, 0:512], AF.Gelu))
                        S.wait("vector", "ma", ca)
                        S.wait("vector", "mt", stat_mm)

                        def stt_(e, sl5=sl5):
                            return e.tensor_tensor(fbuf[0][:, 0:512],
                                                   stg[:, 512:1024],
                                                   stg[:, 512:1024], ALU.mult)
                        cv = S.op("vector", "mv", stt_)
                        S.wait("tensor", "mv", cv)

                        def smm_(e, first=(ht == 0),
                                 stop=(ht == H1 // 128 - 1),
                                 qh=qh):
                            e.matmul(PS[qh][:, :], ones_b[:, 0:1],
                                     stg[:, 512:1024], start=first, stop=stop,
                                     skip_group_check=True)
                            return e.matmul(PV[qh][0:1, :], ones_f[:, 0:1],
                                            fbuf[0][:, 0:512], start=first,
                                            stop=stop, skip_group_check=True)
                        stat_mm = S.op("tensor", "mt", smm_)
                        S.wait("scalar", "mt", stat_mm)
                        hdma = S.dma("scalar",
                                     hid1[ht * 128:(ht + 1) * 128,
                                          qh * 512:(qh + 1) * 512],
                                     stg[:, 512:1024], "shd")
                wcol_free = S.cnt["mt"]
            S.wait("vector", "mt", stat_mm)
            if taps:
                S.wait("gpsimd", "shd", hdma)
                hdbg_c = S.dma("gpsimd", hdbg.ap(), hid1.ap(), "sstg")
                S.wait("scalar", "sstg", hdbg_c)

                def rawdump(e):
                    e.tensor_copy(R2[:, 0:512], PS[0][:, :])
                    e.tensor_copy(R2[:, 512:1024], PS[1][:, :])
                    e.tensor_copy(R3[:, 0:512], PV[0][0:1, :])
                    return e.tensor_copy(R3[:, 512:1024], PV[1][0:1, :])
                cvd = S.op("vector", "mv", rawdump)
                S.wait("scalar", "mv", cvd)
                cdd = S.dma("scalar", rdbg.ap(), rows[:, :], "shd")
                S.wait("vector", "shd", cdd)

            cv = S.op("vector", "mv", lambda e: e.tensor_scalar(
                R0[:, 0:512], PS[0][:, :], 1.0 / H1, None, ALU.mult))
            cv = S.op("vector", "mv", lambda e: e.tensor_scalar(
                R0[:, 512:1024], PS[1][:, :], 1.0 / H1, None, ALU.mult))
            cv = S.op("vector", "mv", lambda e: e.tensor_scalar(
                R1[:, 0:512], PV[0][0:1, :], 1.0 / H1, None, ALU.mult))
            cv = S.op("vector", "mv", lambda e: e.tensor_scalar(
                R1[:, 512:1024], PV[1][0:1, :], 1.0 / H1, None, ALU.mult))
            for tg in ("PS0", "PS1", "PV0", "PV1"):
                S.setg(tg, "mv", cv)
            S.wait("vector", "mv", cv)
            cv = S.op("vector", "mv", lambda e: e.tensor_tensor(
                R2[:, :], R0[:, :], R0[:, :], ALU.mult))
            S.wait("vector", "mv", cv)
            cv = S.op("vector", "mv", lambda e: e.tensor_tensor(
                R1[:, :], R1[:, :], R2[:, :], ALU.subtract))
            S.wait("scalar", "mv", cv)
            ca = S.op("scalar", "ma", lambda e: e.activation(
                R1[:, :], R1[:, :], AF.Sqrt, bias=eps5[0:1, :]))
            S.wait("vector", "ma", ca)
            cv = S.op("vector", "mv", lambda e: e.reciprocal(R1[:, :], R1[:, :]))
            mu_mv = S.cnt["mv"]

            S.wait("vector", "mv", mu_mv)
            bcast(denb[:, 0:512], R0[:, 0:512], "bc0")
            bcast(denb[:, 512:1024], R0[:, 512:1024], "bc0")
            bcast(fbuf[1][:, 0:512], R1[:, 0:512], "bc1", psum=PV[1])
            bcast(fbuf[1][:, 512:1024], R1[:, 512:1024], "bc1", psum=PV[1])
            bmu_mv = S.cnt["mv"]
            S.opq("vector", lambda e: e.memset(stg[:, 1024:2048], 0.0))
            for tg in ("PS0", "PS1"):
                S.guard(tg)
            h2_mm = 0
            h2dma = 0
            for hc in range(H1 // 128):
                cl = S.dma("scalar", stg[:, 0:1024],
                           hid1[hc * 128:(hc + 1) * 128, :], "shd")
                S.wait("vector", "shd", cl)
                S.wait("vector", "mv", bmu_mv)
                S.wait("vector", "mt", h2_mm)

                def lnap(e, hc=hc):
                    e.tensor_tensor(fbuf[0][:, 0:1024], stg[:, 0:1024],
                                    denb[:, :], ALU.subtract)
                    e.tensor_tensor(fbuf[0][:, 0:1024], fbuf[0][:, 0:1024],
                                    fbuf[1][:, 0:1024], ALU.mult)
                    return e.tensor_scalar(
                        stg[:, 0:1024], fbuf[0][:, 0:1024],
                        bias1[:, 64 + hc:64 + hc + 1],
                        bias1[:, 128 + hc:128 + hc + 1], ALU.mult, ALU.add)
                cv = S.op("vector", "mv", lnap)
                S.wait("vector", "mv", cv)

                def st2(e):
                    e.tensor_scalar_mul(stg[:, 2048:3072], stg[:, 0:1024],
                                        -1.0)
                    e.tensor_tensor(stg[:, 1024:2048], stg[:, 1024:2048],
                                    stg[:, 2048:3072], ALU.max)
                    e.tensor_tensor(stg[:, 1024:2048], stg[:, 1024:2048],
                                    stg[:, 0:1024], ALU.max)
                    return e.tensor_tensor(fbuf[0][:, 0:1024], stg[:, 0:1024],
                                           stg[:, 0:1024], ALU.mult)
                cv = S.op("vector", "mv", st2)
                S.wait("tensor", "mv", cv)

                def smm2(e, first=(hc == 0), stop=(hc == H1 // 128 - 1)):
                    e.matmul(PS[0][:, :], ones_f[:, 0:1], fbuf[0][:, 0:512],
                             start=first, stop=stop, skip_group_check=True)
                    return e.matmul(PS[1][:, :], ones_f[:, 0:1],
                                    fbuf[0][:, 512:1024], start=first,
                                    stop=stop, skip_group_check=True)
                h2_mm = S.op("tensor", "mt", smm2)
                S.wait("scalar", "mv", cv)
                S.wait("scalar", "mt", h2_mm)
                h2dma = S.dma("scalar", hid1[hc * 128:(hc + 1) * 128, :],
                              stg[:, 0:1024], "shd")
            S.wait("vector", "mt", h2_mm)
            S.wait("vector", "shd", h2dma)
            cv = S.op("vector", "mv", lambda e: e.tensor_copy(
                stg[:, 0:1024], stg[:, 1024:2048]))
            S.wait("tensor", "mv", cv)
            for j in range(8):
                S.guard(f"PT{j % 2}")
                cm = S.op("tensor", "mt", lambda e, j=j: e.transpose(
                    PT[j % 2][:, 0:128], stg[:, j * 128:(j + 1) * 128],
                    ident[:, :]))
                S.wait("vector", "mt", cm)
                cv = S.op("vector", "mv", lambda e, j=j: e.reduce_max(
                    mxacc[:, j:j + 1], PT[j % 2][:, 0:128], axis=AX.X))
                S.setg(f"PT{j % 2}", "mv", cv)
            S.wait("sync", "mv", cv)
            for j in range(8):
                gat = S.dma("sync", R3[:, j * 128:(j + 1) * 128],
                            mxacc[:, j:j + 1], "sg8")
            S.wait("vector", "sg8", gat)
            cv = S.op("vector", "mv", lambda e: e.tensor_scalar(
                R0[:, 0:512], PS[0][:, :], 1.0 / H1, 1e-5, ALU.mult, ALU.add))
            cv = S.op("vector", "mv", lambda e: e.tensor_scalar(
                R0[:, 512:1024], PS[1][:, :], 1.0 / H1, 1e-5, ALU.mult, ALU.add))
            S.setg("PS0", "mv", cv); S.setg("PS1", "mv", cv)
            S.wait("scalar", "mv", cv)
            ca = S.op("scalar", "ma", lambda e: e.activation(
                R0[:, :], R0[:, :], AF.Sqrt))
            S.wait("vector", "ma", ca)
            cv = S.op("vector", "mv", lambda e: e.reciprocal(R0[:, :], R0[:, :]))
            S.wait("vector", "mv", cv)
            cv = S.op("vector", "mv", lambda e: e.tensor_tensor(
                R3[:, :], R3[:, :], R0[:, :], ALU.mult))
            S.wait("vector", "mv", cv)
            cv = S.op("vector", "mv", lambda e: e.tensor_scalar_max(
                R3[:, :], R3[:, :], 1e-5))
            S.wait("vector", "mv", cv)
            cv = S.op("vector", "mv", lambda e: e.reciprocal(R1[:, :], R3[:, :]))
            S.wait("vector", "mv", cv)

            def mkalpha2(e):
                e.tensor_tensor(R2[:, :], R0[:, :], R1[:, :], ALU.mult)
                return e.tensor_scalar_mul(R2[:, :], R2[:, :], 127.0)
            cv = S.op("vector", "mv", mkalpha2)
            S.wait("vector", "mv", thr5_mv)
            S.wait("vector", "mv", cv)
            cv = S.op("vector", "mv", lambda e: e.tensor_single_scalar(
                R1[:, :], R3[:, :], g2row, ALU.mult))
            S.wait("vector", "mv", cv)
            cv = S.op("vector", "mv", lambda e: e.tensor_scalar_mul(
                R1[:, :], R1[:, :], 1.0 / 127.0))
            S.wait("vector", "mv", cv)
            bcast(denb[:, 0:512], R2[:, 0:512], "bc0")
            bcast(denb[:, 512:1024], R2[:, 512:1024], "bc0")
            ba2_mv = S.cnt["mv"]

            xq2b = pool[:, 0:8192]
            qw2s = att[:, 0:16384]
            sda_last = {0: 0, 1: 0}
            h3dma = 0
            xq2_used = 0
            qw2_used = 0
            prevq = [0, 0]
            prev_qa = 0
            S.fence("sync", ("mv", "ma", "mt"))
            for sup in range(8):
                for j in range(8):
                    hc = sup * 8 + j
                    S.wait("scalar", "mv", prev_qa)
                    cl = S.dma("scalar", stg[:, 0:1024],
                               hid1[hc * 128:(hc + 1) * 128, :], "shd")
                    S.wait("vector", "shd", cl)
                    S.wait("vector", "mv", ba2_mv)
                    S.wait("vector", "mt", xq2_used if j == 0 else 0)

                    def qa2(e, j=j):
                        e.tensor_tensor(x0t[:, :], stg[:, 0:512],
                                        denb[:, 0:512], ALU.mult)
                        e.tensor_copy(stgi[:, 0:512], x0t[:, :])
                        e.tensor_copy(xq2b[:, j * 1024:j * 1024 + 512],
                                      stgi[:, 0:512])
                        e.tensor_tensor(x0t[:, :], stg[:, 512:1024],
                                        denb[:, 512:1024], ALU.mult)
                        e.tensor_copy(stgi[:, 0:512], x0t[:, :])
                        return e.tensor_copy(xq2b[:, j * 1024 + 512:(j + 1) * 1024],
                                             stgi[:, 0:512])
                    prev_qa = cv = S.op("vector", "mv", qa2)
                    b = j % 2
                    sem = "sqa" if b == 0 else "sqb"
                    S.wait("sync", "mv", prevq[b])
                    cl = S.dma("sync", fbuf[b][:, 0:2048],
                               w2t1[hc * 128:(hc + 1) * 128, :], sem)
                    S.wait("vector", sem, cl)
                    S.wait("vector", "mv", thr5_mv)
                    S.wait("vector", "mt", qw2_used if j == 0 else 0)

                    def qf2a(e, b=b, j=j):
                        t1 = stg[:, 1024:2048]
                        t2 = stg[:, 2048:3072]
                        e.tensor_single_scalar(t1, fbuf[b][:, 0:1024], t5,
                                               ALU.is_gt)
                        e.tensor_single_scalar(t2, fbuf[b][:, 0:1024], n5,
                                               ALU.is_lt)
                        e.tensor_tensor(qw2s[:, j * 2048:j * 2048 + 1024], t1,
                                        t2, ALU.subtract)
                        e.tensor_single_scalar(t1, fbuf[b][:, 1024:2048], t5,
                                               ALU.is_gt)
                        e.tensor_single_scalar(t2, fbuf[b][:, 1024:2048], n5,
                                               ALU.is_lt)
                        return e.tensor_tensor(
                            qw2s[:, j * 2048 + 1024:(j + 1) * 2048], t1, t2,
                            ALU.subtract)
                    prevq[b] = S.op("vector", "mv", qf2a)
                q_mv = S.cnt["mv"]
                S.wait("tensor", "mv", q_mv)
                if sup == 0 and taps:
                    S.wait("sync", "mv", q_mv)
                    cpd = S.dma("sync", pdbg[:, 0:1024],
                                xq2b[:, 7 * 1024:8 * 1024], "spd")
                    S.wait("tensor", "spd", cpd)
                sda_before = S.cnt.get("sda", 0)
                if sup == 0:
                    S.sem("sda")
                    sda_last.clear(); sda_last.update({0: 0, 1: 0})
                for dt_ in range(NC):
                    for qh in range(NQH):
                        ptag = f"PA{(dt_ * NQH + qh) % 2}"
                        ps = PA[(dt_ * NQH + qh) % 2]
                        items = []
                        for j in range(8):
                            lh = qw2s[:, j * 2048 + dt_ * 128:
                                      j * 2048 + dt_ * 128 + 128]
                            rh = xq2b[:, j * 1024 + qh * 512:
                                      j * 1024 + qh * 512 + 512]
                            items.append((ps[:, :], lh, rh))
                        cm = S.mm(ptag, items)
                        gb = (dt_ * NQH + qh) % 2
                        gt = gstg0 if gb == 0 else yst
                        S.wait("vector", "mt", cm)
                        S.wait("vector", "sda", sda_last[gb])
                        cv = S.op("vector", "mv", lambda e, ps=ps, gt=gt:
                                  e.tensor_copy(gt[:, :], ps[:, :]))
                        S.setg(ptag, "mv", cv)
                        S.wait("gpsimd", "mv", cv)
                        if sup > 0:
                            S.wait("gpsimd", "sda", sda_before)

                        def acd(e, dt_=dt_, qh=qh, gt=gt, sup=sup):
                            return e.dma_start(
                                out=g2acc[dt_ * 128:(dt_ + 1) * 128,
                                          qh * 512:(qh + 1) * 512],
                                in_=gt[:, :],
                                accum_op=(ALU.bypass if sup == 0 else ALU.add))
                        h = S.sem("sda")
                        cda = S.bump("sda", 16)
                        S.on("gpsimd", lambda e, f=acd, h=h: f(e).then_inc(h, 16))
                        sda_last[gb] = cda
                xq2_used = S.cnt["mt"]
                qw2_used = S.cnt["mt"]
                if sup == 0 and taps:
                    S.wait("sync", "mt", S.cnt["mt"])
                    cpd2 = S.dma("sync", pdbg[:, 1024:2048],
                                 xq2b[:, 7 * 1024:8 * 1024], "spd")
                    S.wait("vector", "spd", cpd2)
                if sup in (0, 1) and taps:
                    S.wait("gpsimd", "sda", S.cnt.get("sda", 0))
                    S.dma("gpsimd", g2s01[sup * 2048:(sup + 1) * 2048, :],
                          g2acc.ap(), "sstg")
                if sup == 7 and taps:
                    S.wait("scalar", "mt", xq2_used)
                    S.dma("scalar", xdbg.ap(), xq2b, "shd")
                    S.dma("scalar", qdbg.ap(), qw2s, "shd")
            S.wait("vector", "mv", S.cnt["mv"])
            bcast(denb[:, 0:512], R1[:, 0:512], "bc0")
            bcast(denb[:, 512:1024], R1[:, 512:1024], "bc0")
            bb2_mv = S.cnt["mv"]
            S.wait("sync", "sda", S.cnt.get("sda", 0))
            prev_ld = 0
            for c in range(NC):
                for qh in range(NQH):
                    sl = slice(c * T + qh * 512, c * T + qh * 512 + 512)
                    S.wait("sync", "mv", prev_ld)
                    cl = S.dma("sync", yst[:, :],
                               g2acc[c * 128:(c + 1) * 128,
                                     qh * 512:(qh + 1) * 512], "sy")
                    S.wait("vector", "sy", cl)
                    S.wait("vector", "mv", bb2_mv)
                    cv = S.op("vector", "mv", lambda e, sl=sl, qh=qh:
                              e.tensor_tensor(fbuf[0][:, 0:512], yst[:, :],
                                              denb[:, qh * 512:qh * 512 + 512],
                                              ALU.mult))
                    prev_ld = cv
                    S.wait("scalar", "mv", cv)
                    ca = S.op("scalar", "ma", lambda e, sl=sl, c=c: e.activation(
                        g[:, sl], fbuf[0][:, 0:512], AF.Silu,
                        bias=biases[:, 4 * NC + c:4 * NC + c + 1]))
                    S.wait("vector", "ma", ca)
            g1_ma = S.cnt["ma"]

            S.wait("scalar", "scc", cc_ar)
            S.wait("scalar", "ma", g1_ma)
            S.fence("scalar", ("mv", "mt"))
            vload = 0
            for c in range(NC):
                vload = S.dma("scalar", pool[:, c * T:(c + 1) * T],
                              vd1[c * 128:(c + 1) * 128, :], "shd")
            att_done = 0
            eprev_mm = 0
            for ec in range(NC):
                b = ec % 2
                sem = "sqa" if b == 0 else "sqb"
                if ec == 0:
                    S.wait("sync", "scc", cc_ar)
                S.wait("sync", "mv", S.cnt["mv"] if ec < 2 else 0)
                S.wait("sync", "mt", eprev_mm)
                cl = S.dma("sync", fbuf[b][:, 0:2048],
                           s1sum[ec * 128:(ec + 1) * 128, :], sem)
                S.wait("vector", sem, cl)
                S.wait("vector", "ma", S.cnt.get("ma", 0))
                cv = S.op("vector", "mv", lambda e, b=b: e.reduce_max(
                    col[:, 20:21], fbuf[b][:, 0:2048], axis=AX.X))
                S.wait("vector", "mv", cv)
                cv = S.op("vector", "mv", lambda e: e.tensor_scalar_mul(
                    col[:, 21:22], col[:, 20:21], -0.125))
                S.wait("scalar", "mv", cv)
                S.wait("scalar", "mt", eprev_mm)
                ca = S.op("scalar", "ma", lambda e, b=b: e.activation(
                    stg[:, 0:2048], fbuf[b][:, 0:2048], AF.Exp,
                    bias=col[:, 21:22], scale=0.125,
                    accum_out=col[:, 22:23]))
                S.wait("vector", "ma", ca)
                cv = S.op("vector", "mv", lambda e: e.reciprocal(
                    col[:, 23:24], col[:, 22:23]))
                S.wait("vector", "mv", cv)
                S.wait("vector", "mt", eprev_mm)
                cv = S.op("vector", "mv", lambda e: e.tensor_scalar_mul(
                    stg[:, 2048:4096], stg[:, 0:2048], col[:, 23:24]))
                if taps:
                    S.wait("scalar", "mv", cv)
                    S.dma("scalar", wdbg[ec * 128:(ec + 1) * 128, :],
                          stg[:, 2048:4096], "shd")
                S.wait("tensor", "mv", cv)
                avh = {}
                for fc in range(NC):
                    S.guard(f"PT{fc % 2}")
                    cm = S.op("tensor", "mt", lambda e, fc=fc: e.transpose(
                        PT[fc % 2][:, 0:128],
                        stg[:, 2048 + fc * 128:2048 + fc * 128 + 128],
                        ident[:, :]))
                    S.wait("vector", "mt", cm)
                    cv = S.op("vector", "mv", lambda e, fc=fc: e.tensor_copy(
                        stg[:, fc * 128:(fc + 1) * 128], PT[fc % 2][:, 0:128]))
                    S.setg(f"PT{fc % 2}", "mv", cv)
                S.wait("tensor", "mv", cv)
                if ec == 0:
                    S.wait("tensor", "shd", vload)

                def avmm(e, ec=ec):
                    ins = None
                    for qh in range(NQH):
                        for fc in range(NC):
                            ins = e.matmul(
                                PV[qh][:, :], stg[:, fc * 128:(fc + 1) * 128],
                                pool[:, fc * T + qh * 512:fc * T + qh * 512 + 512],
                                start=(fc == 0), stop=(fc == NC - 1),
                                skip_group_check=True)
                    return ins
                S.guard("PV0"); S.guard("PV1")
                cm = S.op("tensor", "mt", avmm)
                eprev_mm = cm
                S.wait("vector", "mt", cm)
                for qh in range(NQH):
                    cv = S.op("vector", "mv", lambda e, ec=ec, qh=qh:
                              e.tensor_copy(
                                  att[:, ec * T + qh * 512:ec * T + qh * 512 + 512],
                                  PV[qh][:, :]))
                S.setg("PV0", "mv", cv); S.setg("PV1", "mv", cv)
            attn1_mv = S.cnt["mv"]

            o1state = {"dma": 0, "pv": 0, "st": 0}

            def wr_o1(ot, qh, ps, cm):
                S.wait("sync", "mv", o1state["pv"])
                cl = S.dma("sync", x1t[:, :],
                           x1d[ot * 128:(ot + 1) * 128,
                               qh * 512:(qh + 1) * 512], "sxo")
                S.wait("vector", "mt", cm)
                S.wait("vector", "sxo", cl)
                S.wait("vector", "sst", o1state["st"])

                def ev(e, ot=ot, qh=qh, ps=ps):
                    e.tensor_tensor(stg[:, 0:512],
                                    g[:, ot * T + qh * 512:ot * T + qh * 512 + 512],
                                    ps[:, :], ALU.mult)
                    return e.tensor_tensor(yst[:, :], stg[:, 0:512], x1t[:, :],
                                           ALU.add)
                cv = S.op("vector", "mv", ev)
                o1state["pv"] = cv
                S.wait("sync", "mv", cv)
                o1state["st"] = S.dma(
                    "sync", y[ot * 128:(ot + 1) * 128, qh * 512:(qh + 1) * 512],
                    yst[:, :], "sst")
                return ("mv", cv)

            S.wait("gpsimd", "mv", attn1_mv)
            emit_proj(wo1, wr_o1, attn1_mv, rhs_buf=att)

        def emit_taps():
            for tp in taps:
                src = {"x1d": x1d, "qd": qd, "k_full": k_full,
                       "v_full": v_full, "s1sum": s1sum, "s1loc": s1loc,
                       "hid0": hid0, "hid1": hid1, "vd1": vd1,
                       "wdbg": wdbg, "rdbg": rdbg, "hdbg": hdbg,
                       "xdbg": xdbg, "qdbg": qdbg, "g2d": g2acc,
                       "g2s01": g2s01, "pdbg": pdbg}.get(tp)
                for sname in ("shd", "sxa", "sxb", "mv", "ma", "mt"):
                    S.wait("gpsimd", sname, S.cnt.get(sname, 0))
                if src is not None:
                    S.dma("gpsimd", tap_t[tp].ap(), src.ap(), "sstg")
                elif tp == "cols_t":
                    S.dma("gpsimd", tap_t[tp].ap(), col[:, :], "sstg")
                elif tp == "rows_t":
                    S.dma("gpsimd", tap_t[tp].ap(), rows[:, :], "sstg")
                else:
                    buf = {"xn_t": xn, "g_t": g, "att_t": att}[tp]
                    for c in range(NC):
                        S.dma("gpsimd", tap_t[tp][c * 128:(c + 1) * 128, :],
                              buf[:, c * T:(c + 1) * T], "sstg")

        def emit_y_from(dram):
            prev_cv, prev_st = 0, 0
            for ot in range(NC):
                for qh in range(NQH):
                    S.wait("sync", "mv", prev_cv)
                    cl = S.dma("sync", x1t[:, :],
                               dram[ot * 128:(ot + 1) * 128,
                                    qh * 512:(qh + 1) * 512], "sy")
                    S.wait("vector", "sy", cl)
                    S.wait("vector", "sst", prev_st)
                    prev_cv = S.op("vector", "mv", lambda e:
                                   e.tensor_copy(yst[:, :], x1t[:, :]))
                    S.wait("sync", "mv", prev_cv)
                    prev_st = S.dma("sync",
                                    y[ot * 128:(ot + 1) * 128,
                                      qh * 512:(qh + 1) * 512], yst[:, :],
                                    "sst")

        if only_l0:
            S.wait("sync", "shd", x1_shd)
            emit_y_from(x1d)
        else:
            emit_layer1()
        emit_taps()
        S.wait("gpsimd", "sst", S.cnt.get("sst", 0))
        S.wait("gpsimd", "sstg", S.cnt.get("sstg", 0))
        S.wait("gpsimd", "shd", S.cnt.get("shd", 0))

        with nc.Block() as block:
            @block.sync
            def _(e):
                for f in S.ops["sync"]:
                    f(e)

            @block.gpsimd
            def _(e):
                for f in S.ops["gpsimd"]:
                    f(e)

            @block.vector
            def _(e):
                for f in S.ops["vector"]:
                    f(e)

            @block.scalar
            def _(e):
                for f in S.ops["scalar"]:
                    f(e)

            @block.tensor
            def _(e):
                for f in S.ops["tensor"]:
                    f(e)

    return nc


def _prep_inputs(inp):
    f = np.ascontiguousarray
    a = np.asarray

    def f32(v):
        return np.asarray(v, dtype=np.float32)

    shared = {
        "wq0": f(f32(inp["L0_attn_w_q"])), "wk0": f(f32(inp["L0_attn_w_k"])),
        "wv0": f(f32(inp["L0_attn_w_v"])), "wo0": f(f32(inp["L0_attn_w_o"])),
        "bq0": f(f32(inp["L0_attn_b_q"]).reshape(16, 128).T),
        "bk0": f(f32(inp["L0_attn_b_k"]).reshape(16, 128).T),
        "bv0": f(f32(inp["L0_attn_b_v"]).reshape(16, 128).T),
        "bo0": f(f32(inp["L0_attn_b_o"]).reshape(16, 128).T),
        "w1t0": f(f32(inp["L0_ffn_w1"]).T),
        "w2t0": f(f32(inp["L0_ffn_w2"]).T),
        "w3t0": f(f32(inp["L0_ffn_w3"]).T),
        "wq1": f(f32(inp["L1_attn_Wq"])), "wk1": f(f32(inp["L1_attn_Wk"])),
        "wv1": f(f32(inp["L1_attn_Wv"])), "wo1": f(f32(inp["L1_attn_Wo"])),
        "w1t1": f(f32(inp["L1_ffn_w1"]).T),
        "w2t1": f(f32(inp["L1_ffn_w2"]).T),
        "b11": f(f32(inp["L1_ffn_b1"]).reshape(64, 128).T),
        "lng1": f(f32(inp["L1_ffn_ln_g"]).reshape(64, 128).T),
        "lnb1": f(f32(inp["L1_ffn_ln_b"]).reshape(64, 128).T),
        "b21": f(f32(inp["L1_ffn_b2"]).reshape(16, 128).T),
        "kvec": f((np.arange(128)[:, None] + 128.0 * np.arange(16)[None, :])
                  .astype(np.float32)),
    }
    x = f32(inp["L0_attn_query"])
    in_maps = []
    for core in range(8):
        b, half = core // 2, core % 2
        m = dict(shared)
        m["xfm"] = f(x[b, half * 1024:(half + 1) * 1024, :].T)
        m["qiota"] = (half * 1024.0
                      + np.arange(1024, dtype=np.float32))[None, :]
        in_maps.append(m)
    return in_maps


def kernel(**inputs):
    in_maps = _prep_inputs(inputs)
    nc = build_program(taps=DEBUG_TAPS, only_l0=ONLY_L0)
    res = run_bass_kernel_spmd(nc, in_maps, core_ids=list(range(8)))
    out = np.empty((4, 2048, 2048), dtype=np.float32)
    for core in range(8):
        b, half = core // 2, core % 2
        out[b, half * 1024:(half + 1) * 1024, :] = res.results[core]["y"].T
    kernel.last_results = res
    return out
